# version 02
# baseline (speedup 1.0000x reference)
import contextlib
import numpy as np
import ml_dtypes
import concourse.bass as bass
import concourse.mybir as mybir
from concourse.bass_utils import run_bass_kernel_spmd

F32 = mybir.dt.float32
BF16 = mybir.dt.bfloat16
AF = mybir.ActivationFunctionType
ALU = mybir.AluOpType
AX = mybir.AxisListType
BF = ml_dtypes.bfloat16

NCORES = 8
B, T, D = 4, 4096, 1024
NTOK = B * T
TPC = NTOK // NCORES
DT = D // 128
CH = 512
NCH = TPC // CH


class P:

    ENGS = ("sync", "tensor", "vector", "scalar", "gpsimd")

    def __init__(self, name="prog"):
        self.nc = bass.Bass("TRN2", target_bir_lowering=False)
        self.stack = contextlib.ExitStack()
        self.phases = [[]]
        self._uid = 0

    def _nm(self, base):
        self._uid += 1
        return f"{base}_{self._uid}"

    def din(self, name, shape, dtype=F32):
        return self.nc.dram_tensor(name, list(shape), dtype, kind="ExternalInput").ap()

    def dout(self, name, shape, dtype=F32):
        return self.nc.dram_tensor(name, list(shape), dtype, kind="ExternalOutput").ap()

    def sb(self, shape, dtype=F32, name=None):
        return self.stack.enter_context(
            self.nc.sbuf_tensor(name or self._nm("sb"), list(shape), dtype))

    def ps(self, shape, dtype=F32, name=None):
        return self.stack.enter_context(
            self.nc.psum_tensor(name or self._nm("ps"), list(shape), dtype))

    def op(self, eng, fn, inc=1):
        self.phases[-1].append((eng, fn, inc))

    def bar(self):
        if self.phases[-1]:
            self.phases.append([])

    def dma(self, out, in_, eng="sync", **kw):
        self.op(eng, lambda E: E.dma_start(out=out, in_=in_, **kw), 16)

    def finish(self):
        self.bar()
        phases = [ph for ph in self.phases if ph]
        ptot = [sum(inc for (_, _, inc) in ph) for ph in phases]
        starts = [0]
        for t_ in ptot:
            starts.append(starts[-1] + t_)
        grand = starts[-1]
        sem = self.stack.enter_context(self.nc.semaphore("gsem"))
        used = set(e for ph in phases for (e, _, _) in ph)
        used.add("sync")
        with self.nc.Block() as block:
            def mk_emitter(engname):
                def emit(E):
                    for pi, ph in enumerate(phases):
                        mine = [o for o in ph if o[0] == engname]
                        if mine:
                            if starts[pi] > 0:
                                E.wait_ge(sem, starts[pi])
                            for (_, fn, inc) in mine:
                                fn(E).then_inc(sem, inc)
                    if engname == "sync":
                        E.wait_ge(sem, grand)
                return emit
            for en in self.ENGS:
                if en in used:
                    getattr(block, en)(mk_emitter(en))
        return self.nc



def ln_stats(p, srcs, n, onesb, onesr, s1, s2, rb, mb, sq_dsts, st, eps=1e-5, nfeat=None):
    kt = len(srcs)
    nf = nfeat or (kt * 128)
    mu = st[:, 0:n]; ex2 = st[:, n:2 * n]; var = st[:, 2 * n:3 * n]; tmp = st[:, 3 * n:4 * n]
    for j in range(kt):
        p.op("scalar", lambda E, o=sq_dsts[j], i=srcs[j]: E.activation(o, i, AF.Square))
    p.bar()
    for j in range(kt):
        p.op("tensor", lambda E, i=srcs[j], j_=j: E.matmul(
            s1, onesb[:, 0:1], i, start=(j_ == 0), stop=(j_ == kt - 1)))
    for j in range(kt):
        p.op("tensor", lambda E, i=sq_dsts[j], j_=j: E.matmul(
            s2, onesb[:, 0:1], i, start=(j_ == 0), stop=(j_ == kt - 1)))
    p.bar()
    p.op("vector", lambda E: E.tensor_scalar(mu, s1, 1.0 / nf, None, ALU.mult))
    p.op("scalar", lambda E: E.activation(ex2, s2, AF.Copy, scale=1.0 / nf))
    p.bar()
    p.op("scalar", lambda E: E.activation(tmp, mu, AF.Square))
    p.bar()
    p.op("vector", lambda E: E.tensor_tensor(var, ex2, tmp, ALU.subtract))
    p.bar()
    p.op("scalar", lambda E: E.activation(tmp, var, AF.Sqrt, bias=np.float32(eps)))
    p.bar()
    p.op("vector", lambda E: E.reciprocal(var, tmp))
    p.bar()
    p.op("vector", lambda E: E.tensor_tensor(tmp, mu, var, ALU.mult))
    p.bar()
    p.op("tensor", lambda E: E.matmul(rb, onesr[:], var, start=True, stop=True))
    p.op("tensor", lambda E: E.matmul(mb, onesr[:], tmp, start=True, stop=True))
    p.bar()


def ln_apply(p, srcs, rb, mb, dsts):
    for j in range(len(srcs)):
        p.op("vector", lambda E, i=srcs[j], o=dsts[j]: E.tensor_tensor(o, i, rb, ALU.mult))
    p.bar()
    for j in range(len(dsts)):
        p.op("vector", lambda E, o=dsts[j]: E.tensor_tensor(o, o, mb, ALU.subtract))
    p.bar()


def load_weight_bf16(p, dram_ap, kt, m, stage, wbf, pieces=None):
    for k in range(kt):
        p.dma(stage[:, 0:m], dram_ap[k * 128:(k + 1) * 128, :])
        p.bar()
        p.op("scalar", lambda E, k_=k: E.activation(
            wbf[:, k_ * m:(k_ + 1) * m], stage[:, 0:m], AF.Copy))
        p.bar()


def emit_mm(p, wbf, ktiles, m_tiles, rhs_fn, n, pmm, evac_fn, wcols=None):
    wcols = wcols or (m_tiles * 128)
    ng = len(pmm)
    for mg in range(0, m_tiles, ng):
        cnt = min(ng, m_tiles - mg)
        for mi in range(cnt):
            m = mg + mi
            for k in range(ktiles):
                p.op("tensor", lambda E, m_=m, mi_=mi, k_=k: E.matmul(
                    pmm[mi_][:],
                    wbf[:, k_ * wcols + m_ * 128: k_ * wcols + (m_ + 1) * 128],
                    rhs_fn(k_),
                    start=(k_ == 0), stop=(k_ == ktiles - 1)))
        p.bar()
        for mi in range(cnt):
            evac_fn(mg + mi, pmm[mi][:])
        p.bar()


def stream_mm(p, WS, wD, ktiles, m_total, m_per, rhs_fn, ev_global, pmm, n):
    mcols = m_per * 128
    for pc in range(0, m_total, m_per):
        p.dma(WS[:, 0:ktiles * mcols].rearrange("p (kt m) -> p kt m", m=mcols),
              wD.rearrange("(kt p) m -> p kt m", p=128)[:, :, pc * 128:(pc + m_per) * 128])
        p.bar()

        def ev(m, ps_, pc_=pc):
            ev_global(pc_ + m, ps_)
        emit_mm(p, WS, ktiles, m_per, rhs_fn, n, pmm, ev, wcols=mcols)



def build_kvr():
    p = P("kvr")
    x0T = p.din("x0T", (DT, 128, TPC + 1))
    tmkD = p.din("tmk", (128, DT)); tmvD = p.din("tmv", (128, DT)); tmrD = p.din("tmr", (128, DT))
    wkD = p.din("Wk", (D, D)); wvD = p.din("Wv", (D, D)); wrD = p.din("Wr", (D, D))
    wgD = p.din("Wg", (D, 8)); bgD = p.din("bg", (1, 8))
    kcO = p.dout("kc", (DT, 128, TPC)); kvO = p.dout("kv", (DT, 128, TPC))
    srO = p.dout("sr", (DT, 128, TPC)); w0O = p.dout("w0", (TPC, 8))
    xnbfO = p.dout("xnbf", (DT, 128, TPC), BF16)

    x0Tv = x0T.rearrange("kt p t -> p kt t")
    kcV = kcO.rearrange("kt p t -> p kt t")
    kvV = kvO.rearrange("kt p t -> p kt t")
    srV = srO.rearrange("kt p t -> p kt t")
    xnbfV = xnbfO.rearrange("kt p t -> p kt t")

    onesb = p.sb([128, 1], F32, "onesb")
    onesr = p.sb([1, 128], F32, "onesr")
    p.op("gpsimd", lambda E: E.memset(onesb[:], 1.0))
    p.op("gpsimd", lambda E: E.memset(onesr[:], 1.0))
    p.bar()
    wstage = p.sb([128, D], F32, "wstage")
    lnst = p.sb([1, 4 * CH], F32, "lnst")
    wkbf = p.sb([128, DT * D], BF16, "wkbf")
    wvbf = p.sb([128, DT * D], BF16, "wvbf")
    wrbf = p.sb([128, DT * D], BF16, "wrbf")
    wgf = p.sb([128, DT * 8], F32, "wgf")
    load_weight_bf16(p, wkD, DT, D, wstage, wkbf, pieces=2)
    load_weight_bf16(p, wvD, DT, D, wstage, wvbf, pieces=2)
    load_weight_bf16(p, wrD, DT, D, wstage, wrbf, pieces=2)
    for k_ in range(DT):
        p.dma(wgf[:, k_ * 8:(k_ + 1) * 8], wgD[k_ * 128:(k_ + 1) * 128, :])
    p.bar()
    tmbuf = p.sb([128, DT * 6], F32, "tmbuf")
    for i, dsrc in enumerate((tmkD, tmvD, tmrD)):
        p.dma(tmbuf[:, i * DT:(i + 1) * DT], dsrc[:])
    bgrow = p.sb([1, 8], F32, "bgrow")
    p.dma(bgrow[:], bgD[:])
    p.bar()
    for i in range(3):
        p.op("vector", lambda E, i_=i: E.tensor_scalar(
            tmbuf[:, (3 + i_) * DT:(4 + i_) * DT], tmbuf[:, i_ * DT:(i_ + 1) * DT],
            -1.0, 1.0, ALU.mult, ALU.add))
    p.bar()

    s1 = p.ps([1, CH], F32, "s1"); s2 = p.ps([1, CH], F32, "s2")
    rb = p.ps([128, CH], F32, "rb"); mb = p.ps([128, CH], F32, "mb")
    pmm = [p.ps([128, CH], F32, f"pmm{i}") for i in range(4)]

    bgs = p.sb([128, 8], F32, "bgs")
    p.op("tensor", lambda E: E.matmul(pmm[0][:, 0:8], onesr[:], bgrow[:], start=True, stop=True))
    p.bar()
    p.op("scalar", lambda E: E.activation(bgs[:], pmm[0][:, 0:8], AF.Copy))
    p.bar()

    xh = p.sb([128, DT * CH], F32, "xh")
    sqbuf = p.sb([128, DT * CH], F32, "sqbuf")
    xn = p.sb([128, DT * CH], F32, "xn")
    xprev = p.sb([128, DT], F32, "xprev")
    xprev2 = p.sb([128, DT], F32, "xprev2")
    mixbf = p.sb([128, DT * CH], BF16, "mixbf")
    xnbf = p.sb([128, DT * CH], BF16, "xnbf_t")
    mtmp = p.sb([128, DT * CH], F32, "mtmp")
    kcb = p.sb([128, DT * CH], F32, "kcb")
    vb = p.sb([128, DT * CH], F32, "vb")
    gbuf = p.sb([128, 8 * 3], F32, "gbuf")
    gred = p.sb([128, 4], F32, "gred")

    p.dma(xh[:, 0:DT], x0Tv[:, :, 0], allow_slow_non_contiguous=True)
    p.bar()
    hsrc = [xh[:, j:j + 1] for j in range(DT)]
    hsq = [sqbuf[:, j:j + 1] for j in range(DT)]
    ln_stats(p, hsrc, 1, onesb, onesr, s1[:, 0:1], s2[:, 0:1], rb[:, 0:1], mb[:, 0:1], hsq, lnst)
    ln_apply(p, hsrc, rb[:, 0:1], mb[:, 0:1], [xprev[:, j:j + 1] for j in range(DT)])

    for ch in range(NCH):
        c0 = ch * CH
        p.dma(xh[:].rearrange("p (kt t) -> p kt t", kt=DT),
              x0Tv[:, :, 1 + c0:1 + c0 + CH])
        p.bar()
        srcs = [xh[:, j * CH:(j + 1) * CH] for j in range(DT)]
        sqd = [sqbuf[:, j * CH:(j + 1) * CH] for j in range(DT)]
        xns = [xn[:, j * CH:(j + 1) * CH] for j in range(DT)]
        ln_stats(p, srcs, CH, onesb, onesr, s1[:], s2[:], rb[:], mb[:], sqd, lnst)
        ln_apply(p, srcs, rb[:], mb[:], xns)
        for j in range(DT):
            p.op("vector", lambda E, j_=j: E.tensor_copy(
                xprev2[:, j_:j_ + 1], xn[:, j_ * CH + CH - 1: j_ * CH + CH]))
        for j in range(DT):
            p.op("scalar", lambda E, j_=j: E.activation(
                xnbf[:, j_ * CH:(j_ + 1) * CH], xn[:, j_ * CH:(j_ + 1) * CH], AF.Copy))
        p.bar()
        p.dma(xnbfV[:, :, c0:c0 + CH], xnbf[:].rearrange("p (kt t) -> p kt t", kt=DT))
        p.bar()

        for proj in range(3):
            wbf = (wkbf, wvbf, wrbf)[proj]
            for j in range(DT):
                p.op("vector", lambda E, j_=j, t_=proj: E.tensor_scalar(
                    mtmp[:, j_ * CH: j_ * CH + 1], xprev[:, j_:j_ + 1],
                    tmbuf[:, (3 + t_) * DT + j_:(3 + t_) * DT + j_ + 1], None, ALU.mult))
                p.op("vector", lambda E, j_=j, t_=proj: E.tensor_scalar(
                    mtmp[:, j_ * CH + 1:(j_ + 1) * CH],
                    xn[:, j_ * CH: j_ * CH + CH - 1],
                    tmbuf[:, (3 + t_) * DT + j_:(3 + t_) * DT + j_ + 1], None, ALU.mult))
            p.bar()
            for j in range(DT):
                p.op("vector", lambda E, j_=j, t_=proj: E.scalar_tensor_tensor(
                    mixbf[:, j_ * CH:(j_ + 1) * CH], xn[:, j_ * CH:(j_ + 1) * CH],
                    tmbuf[:, t_ * DT + j_: t_ * DT + j_ + 1],
                    mtmp[:, j_ * CH:(j_ + 1) * CH], ALU.mult, ALU.add))
            p.bar()

            if proj == 0:
                def ev(m, ps_):
                    p.op("vector", lambda E, m_=m, q=ps_: E.tensor_scalar(
                        kcb[:, m_ * CH:(m_ + 1) * CH], q[:], 60.0, None, ALU.min))
            elif proj == 1:
                def ev(m, ps_):
                    p.op("scalar", lambda E, m_=m, q=ps_: E.activation(
                        vb[:, m_ * CH:(m_ + 1) * CH], q[:], AF.Copy))
            else:
                def ev(m, ps_):
                    p.op("scalar", lambda E, m_=m, q=ps_: E.activation(
                        mtmp[:, m_ * CH:(m_ + 1) * CH], q[:], AF.Sigmoid))
            emit_mm(p, wbf, DT, DT, lambda k: mixbf[:, k * CH:(k + 1) * CH], CH, pmm, ev)

            if proj == 0:
                for j in range(DT):
                    p.op("scalar", lambda E, j_=j: E.activation(
                        kcb[:, j_ * CH:(j_ + 1) * CH], kcb[:, j_ * CH:(j_ + 1) * CH], AF.Exp))
                p.bar()
                p.dma(kcV[:, :, c0:c0 + CH], kcb[:].rearrange("p (kt t) -> p kt t", kt=DT))
                p.bar()
            elif proj == 1:
                for j in range(DT):
                    p.op("vector", lambda E, j_=j: E.tensor_tensor(
                        vb[:, j_ * CH:(j_ + 1) * CH], vb[:, j_ * CH:(j_ + 1) * CH],
                        kcb[:, j_ * CH:(j_ + 1) * CH], ALU.mult))
                p.bar()
                p.dma(kvV[:, :, c0:c0 + CH], vb[:].rearrange("p (kt t) -> p kt t", kt=DT))
                p.bar()
            else:
                p.dma(srV[:, :, c0:c0 + CH], mtmp[:].rearrange("p (kt t) -> p kt t", kt=DT))
                p.bar()

        for j in range(DT):
            p.op("vector", lambda E, j_=j: E.tensor_copy(xprev[:, j_:j_ + 1], xprev2[:, j_:j_ + 1]))
        p.bar()

        for st_ in range(CH // 128):
            t0 = st_ * 128
            for kt in range(DT):
                p.op("tensor", lambda E, kt_=kt, t0_=t0: E.matmul(
                    mb[:, 0:8], xn[:, kt_ * CH + t0_: kt_ * CH + t0_ + 128],
                    wgf[:, kt_ * 8:(kt_ + 1) * 8],
                    start=(kt_ == 0), stop=(kt_ == DT - 1)))
            p.bar()
            G = gbuf[:, 0:8]; Gm = gbuf[:, 8:16]; W_ = gbuf[:, 16:24]
            p.op("scalar", lambda E: E.activation(G, mb[:, 0:8], AF.Copy))
            p.bar()
            p.op("vector", lambda E: E.tensor_tensor(G, G, bgs[:], ALU.add))
            p.bar()
            mx = gred[:, 0:1]; sm = gred[:, 1:2]; m2c = gred[:, 2:3]
            p.op("vector", lambda E: E.tensor_reduce(mx, G, AX.X, ALU.max))
            p.bar()
            p.op("vector", lambda E: E.tensor_scalar(mx, mx, -1.0, None, ALU.mult))
            p.bar()
            p.op("scalar", lambda E: E.activation(Gm, G, AF.Exp, bias=mx))
            p.bar()
            p.op("vector", lambda E: E.tensor_reduce(sm, Gm, AX.X, ALU.add))
            p.bar()
            p.op("vector", lambda E: E.reciprocal(sm, sm))
            p.bar()
            p.op("vector", lambda E: E.tensor_scalar(G, Gm, sm, None, ALU.mult))
            p.bar()
            p.op("vector", lambda E: E.tensor_reduce(mx, G, AX.X, ALU.max))
            p.bar()
            p.op("vector", lambda E: E.tensor_scalar(Gm, G, mx, None, ALU.is_equal))
            p.bar()
            p.op("vector", lambda E: E.scalar_tensor_tensor(
                Gm, Gm, -1e30, G, ALU.mult, ALU.add))
            p.bar()
            p.op("vector", lambda E: E.tensor_reduce(m2c, Gm, AX.X, ALU.max))
            p.bar()
            p.op("vector", lambda E: E.tensor_scalar(W_, G, m2c, None, ALU.is_ge))
            p.bar()
            p.op("vector", lambda E: E.tensor_tensor(W_, W_, G, ALU.mult))
            p.op("vector", lambda E: E.tensor_tensor(sm, mx, m2c, ALU.add))
            p.bar()
            p.op("vector", lambda E: E.reciprocal(sm, sm))
            p.bar()
            p.op("vector", lambda E: E.tensor_scalar(W_, W_, sm, None, ALU.mult))
            p.bar()
            p.dma(w0O[c0 + t0:c0 + t0 + 128, :], W_)
            p.bar()
    return p


def run_kvr(I, Hst):
    x0 = I["L0_attn_xq"]
    x0T = np.ascontiguousarray(x0.reshape(NTOK, D).T)
    p = build_kvr()
    nc = p.finish()
    in_maps = []
    for c in range(NCORES):
        c0 = c * TPC
        xs = np.empty((D, TPC + 1), np.float32)
        xs[:, 1:] = x0T[:, c0:c0 + TPC]
        if c0 % T == 0:
            xs[:, 0] = 0.0
        else:
            xs[:, 0] = x0T[:, c0 - 1]
        in_maps.append({
            "x0T": xs.reshape(DT, 128, TPC + 1),
            "tmk": np.ascontiguousarray(I["L0_attn_time_mix_k"].reshape(DT, 128).T),
            "tmv": np.ascontiguousarray(I["L0_attn_time_mix_v"].reshape(DT, 128).T),
            "tmr": np.ascontiguousarray(I["L0_attn_time_mix_r"].reshape(DT, 128).T),
            "Wk": np.ascontiguousarray(I["L0_attn_Wk"].T),
            "Wv": np.ascontiguousarray(I["L0_attn_Wv"].T),
            "Wr": np.ascontiguousarray(I["L0_attn_Wr"].T),
            "Wg": np.ascontiguousarray(I["L0_ffn_Wg"]),
            "bg": I["L0_ffn_bg"].reshape(1, 8),
        })
    res = run_bass_kernel_spmd(nc, in_maps, core_ids=list(range(NCORES)))
    p.stack.close()
    Hst["kcT"] = np.concatenate([r["kc"].reshape(D, TPC) for r in res.results], axis=1)
    Hst["kvT"] = np.concatenate([r["kv"].reshape(D, TPC) for r in res.results], axis=1)
    Hst["srT"] = np.concatenate([r["sr"].reshape(D, TPC) for r in res.results], axis=1)
    Hst["w0"] = np.concatenate([r["w0"] for r in res.results], axis=0)
    Hst["xn0bf"] = np.concatenate([r["xnbf"].reshape(D, TPC) for r in res.results], axis=1)



def build_scan0():
    p = P("scan0")
    kcD = p.din("kcS", (128, B, T)); kvD = p.din("kvS", (128, B, T))
    srD = p.din("srS", (128, B, T))
    tdD = p.din("td", (128, 1)); tfD = p.din("tf", (128, 1))
    outD = p.dout("rwkv", (128, B, T))

    sc = p.sb([128, 4], F32, "sc")
    dfull = p.sb([128, T], F32, "dfull")
    kvb = p.sb([128, T], F32, "kvb")
    kcbuf = p.sb([128, T], F32, "kcbuf")
    srb = p.sb([128, T], F32, "srb")
    Akv = p.sb([128, T + 1], F32, "Akv")
    Ak = p.sb([128, T + 1], F32, "Ak")
    wkvb = p.sb([128, T], F32, "wkvb")
    wkb = p.sb([128, T], F32, "wkb")

    p.dma(sc[:, 0:1], tdD[:])
    p.dma(sc[:, 1:2], tfD[:])
    p.op("gpsimd", lambda E: E.memset(Akv[:, 0:1], 0.0))
    p.op("gpsimd", lambda E: E.memset(Ak[:, 0:1], 0.0))
    p.op("gpsimd", lambda E: E.memset(dfull[:], 1.0))
    p.bar()
    p.op("scalar", lambda E: E.activation(sc[:, 2:3], sc[:, 0:1], AF.Exp))
    p.op("scalar", lambda E: E.activation(sc[:, 1:2], sc[:, 1:2], AF.Exp))
    p.bar()
    p.op("scalar", lambda E: E.activation(sc[:, 0:1], sc[:, 2:3], AF.Exp, scale=-1.0))
    p.bar()
    p.op("vector", lambda E: E.tensor_scalar(dfull[:], dfull[:], sc[:, 0:1], None, ALU.mult))
    p.bar()
    for b in range(B):
        p.dma(kvb[:], kvD[:, b, :])
        p.dma(kcbuf[:], kcD[:, b, :])
        p.dma(srb[:], srD[:, b, :])
        p.bar()
        p.op("vector", lambda E: E.tensor_tensor_scan(
            Akv[:, 1:T + 1], dfull[:], kvb[:], 0.0, ALU.mult, ALU.add))
        p.bar()
        p.op("vector", lambda E: E.tensor_tensor_scan(
            Ak[:, 1:T + 1], dfull[:], kcbuf[:], 0.0, ALU.mult, ALU.add))
        p.bar()
        p.op("vector", lambda E: E.scalar_tensor_tensor(
            wkvb[:], kvb[:], sc[:, 1:2], Akv[:, 0:T], ALU.mult, ALU.add))
        p.bar()
        p.op("vector", lambda E: E.scalar_tensor_tensor(
            wkb[:], kcbuf[:], sc[:, 1:2], Ak[:, 0:T], ALU.mult, ALU.add))
        p.bar()
        p.op("vector", lambda E: E.tensor_scalar(wkb[:], wkb[:], 1e-8, None, ALU.add))
        p.bar()
        p.op("vector", lambda E: E.reciprocal(wkb[:], wkb[:]))
        p.bar()
        p.op("vector", lambda E: E.tensor_tensor(wkvb[:], wkvb[:], wkb[:], ALU.mult))
        p.bar()
        p.op("vector", lambda E: E.tensor_tensor(wkvb[:], wkvb[:], srb[:], ALU.mult))
        p.bar()
        p.dma(outD[:, b, :], wkvb[:])
        p.bar()
    return p


def run_scan0(I, Hst):
    KC = Hst["kcT"].reshape(D, B, T)
    KV = Hst["kvT"].reshape(D, B, T)
    SR = Hst["srT"].reshape(D, B, T)
    p = build_scan0()
    nc = p.finish()
    in_maps = []
    for c in range(NCORES):
        cs = slice(c * 128, (c + 1) * 128)
        in_maps.append({
            "kcS": np.ascontiguousarray(KC[cs]),
            "kvS": np.ascontiguousarray(KV[cs]),
            "srS": np.ascontiguousarray(SR[cs]),
            "td": np.ascontiguousarray(I["L0_attn_time_decay"][cs]),
            "tf": np.ascontiguousarray(I["L0_attn_time_first"][cs]),
        })
    res = run_bass_kernel_spmd(nc, in_maps, core_ids=list(range(NCORES)))
    p.stack.close()
    Hst["rwkvT"] = np.concatenate([r["rwkv"] for r in res.results], axis=0)


MOE0_TAIL = True



def build_moe0():
    p = P("moe0")
    rwD = p.din("rwkvT", (DT, 128, TPC))
    x0D = p.din("x0T", (DT, 128, TPC))
    xn0D = p.din("xn0bf", (DT, 128, TPC), BF16)
    w0D = p.din("w0T", (8, TPC))
    woD = p.din("WoT", (D, D))
    weD = p.din("We", (8, D, D))
    beD = p.din("beH", (128, 8 * DT))
    wqD = p.din("WqT", (D, D)); wkD = p.din("WkT", (D, D)); wvD = p.din("WvT", (D, D))
    wg1D = p.din("wg1", (D, 8))
    x1O = p.dout("x1T", (DT, 128, TPC))
    xn1O = p.dout("xn1bf", (DT, 128, TPC), BF16)
    qO = p.dout("qbf", (DT, 128, TPC), BF16)
    kO = p.dout("kbf", (DT, 128, TPC), BF16)
    vO = p.dout("vbf", (DT, 128, TPC), BF16)
    mkO = p.dout("mask", (TPC, 8))
    gsO = p.dout("gs", (TPC, 1))

    rwV = rwD.rearrange("kt p t -> p kt t")
    x0V = x0D.rearrange("kt p t -> p kt t")
    xn0V = xn0D.rearrange("kt p t -> p kt t")
    x1V = x1O.rearrange("kt p t -> p kt t")
    xn1V = xn1O.rearrange("kt p t -> p kt t")
    qV = qO.rearrange("kt p t -> p kt t")
    kV = kO.rearrange("kt p t -> p kt t")
    vV = vO.rearrange("kt p t -> p kt t")

    onesb = p.sb([128, 1], F32, "onesb")
    onesr = p.sb([1, 128], F32, "onesr")
    p.op("gpsimd", lambda E: E.memset(onesb[:], 1.0))
    p.op("gpsimd", lambda E: E.memset(onesr[:], 1.0))
    p.bar()
    s1 = p.ps([1, CH], F32, "s1"); s2 = p.ps([1, CH], F32, "s2")
    rb = p.ps([128, CH], F32, "rb"); mb = p.ps([128, CH], F32, "mb")
    pmm = [p.ps([128, CH], F32, f"pmm{i}") for i in range(4)]
    lnst = p.sb([1, 4 * CH], F32, "lnst")
    A1 = p.sb([128, DT * TPC], F32, "A1")
    A2 = p.sb([128, DT * TPC], BF16, "A2")
    A3 = p.sb([128, DT * D], BF16, "A3")
    A4 = p.sb([128, DT * CH], F32, "A4")
    A5 = p.sb([128, DT * CH], F32, "A5")
    A6 = p.sb([128, DT * CH], BF16, "A6")
    wst = p.sb([128, D], F32, "wst")
    beH = p.sb([128, 8 * DT], F32, "beHs")
    wg1 = p.sb([128, DT * 8], F32, "wg1sb")
    wrow = p.sb([1, CH], F32, "wrow")
    gbuf = p.sb([128, 8 * 3], F32, "gbuf")
    gred = p.sb([128, 4], F32, "gred")

    p.dma(beH[:], beD[:])
    for k_ in range(DT):
        p.dma(wg1[:, k_ * 8:(k_ + 1) * 8], wg1D[k_ * 128:(k_ + 1) * 128, :])
    p.dma(A2[:].rearrange("p (kt t) -> p kt t", kt=DT), xn0V[:, :, :])
    p.bar()

    for e in range(8):
        for k_ in range(DT):
            p.dma(wst[:, 0:D], weD[e, k_ * 128:(k_ + 1) * 128, :])
            p.bar()
            p.op("scalar", lambda E, k2=k_: E.activation(
                A3[:, k2 * D:(k2 + 1) * D], wst[:, 0:D], AF.Copy))
            p.bar()
        for ch in range(NCH):
            c0 = ch * CH
            p.dma(wrow[:], w0D[e:e + 1, c0:c0 + CH])
            p.bar()
            p.op("tensor", lambda E: E.matmul(rb[:], onesr[:], wrow[:], start=True, stop=True))
            p.bar()

            def ev(m, ps_, e_=e, c0_=c0):
                p.op("vector", lambda E, m_=m, q=ps_: E.tensor_scalar(
                    A4[:, m_ * CH:(m_ + 1) * CH], q, beH[:, e_ * DT + m_:e_ * DT + m_ + 1],
                    None, ALU.add))
            emit_mm(p, A3, DT, DT,
                    lambda k, c0_=c0: A2[:, k * TPC + c0_: k * TPC + c0_ + CH],
                    CH, pmm, ev, wcols=D)
            if e == 0:
                for m in range(DT):
                    p.op("vector", lambda E, m_=m, c0_=c0: E.tensor_tensor(
                        A1[:, m_ * TPC + c0_: m_ * TPC + c0_ + CH],
                        A4[:, m_ * CH:(m_ + 1) * CH], rb[:], ALU.mult))
                p.bar()
            else:
                for m in range(DT):
                    p.op("vector", lambda E, m_=m: E.tensor_tensor(
                        A5[:, m_ * CH:(m_ + 1) * CH],
                        A4[:, m_ * CH:(m_ + 1) * CH], rb[:], ALU.mult))
                p.bar()
                for m in range(DT):
                    p.op("vector", lambda E, m_=m, c0_=c0: E.tensor_tensor(
                        A1[:, m_ * TPC + c0_: m_ * TPC + c0_ + CH],
                        A1[:, m_ * TPC + c0_: m_ * TPC + c0_ + CH],
                        A5[:, m_ * CH:(m_ + 1) * CH], ALU.add))
                p.bar()
    for m in range(DT):
        for ch in range(NCH):
            p.op("scalar", lambda E, m_=m, c_=ch: E.activation(
                A1[:, m_ * TPC + c_ * CH: m_ * TPC + (c_ + 1) * CH],
                A1[:, m_ * TPC + c_ * CH: m_ * TPC + (c_ + 1) * CH], AF.Silu))
    p.bar()

    woBF = A3
    load_weight_bf16(p, woD, DT, D, wst, woBF)
    for ch in range(NCH):
        c0 = ch * CH
        p.dma(A4[:].rearrange("p (kt t) -> p kt t", kt=DT), rwV[:, :, c0:c0 + CH])
        p.bar()
        for j in range(DT):
            p.op("scalar", lambda E, j_=j: E.activation(
                A6[:, j_ * CH:(j_ + 1) * CH], A4[:, j_ * CH:(j_ + 1) * CH], AF.Copy))
        p.bar()
        p.dma(A5[:].rearrange("p (kt t) -> p kt t", kt=DT), x0V[:, :, c0:c0 + CH])
        p.bar()

        def evs0(m, ps_, c0_=c0):
            p.op("vector", lambda E, m_=m, q=ps_: E.tensor_tensor(
                A4[:, m_ * CH:(m_ + 1) * CH], q,
                A1[:, m_ * TPC + c0_: m_ * TPC + c0_ + CH], ALU.mult))
        emit_mm(p, woBF, DT, DT, lambda k: A6[:, k * CH:(k + 1) * CH], CH, pmm, evs0, wcols=D)
        for m in range(DT):
            p.op("vector", lambda E, m_=m, c0_=c0: E.tensor_tensor(
                A1[:, m_ * TPC + c0_: m_ * TPC + c0_ + CH],
                A5[:, m_ * CH:(m_ + 1) * CH], A4[:, m_ * CH:(m_ + 1) * CH], ALU.add))
        p.bar()
        p.dma(x1V[:, :, c0:c0 + CH],
              A1[:].rearrange("p (kt t) -> p kt t", kt=DT)[:, :, c0:c0 + CH])
        p.bar()
        srcs = [A1[:, j * TPC + c0: j * TPC + c0 + CH] for j in range(DT)]
        sqd = [A5[:, j * CH:(j + 1) * CH] for j in range(DT)]
        xns = [A4[:, j * CH:(j + 1) * CH] for j in range(DT)]
        ln_stats(p, srcs, CH, onesb, onesr, s1[:], s2[:], rb[:], mb[:], sqd, lnst)
        ln_apply(p, srcs, rb[:], mb[:], xns)
        for j in range(DT):
            p.op("scalar", lambda E, j_=j, c0_=c0: E.activation(
                A2[:, j_ * TPC + c0_: j_ * TPC + c0_ + CH],
                A4[:, j_ * CH:(j_ + 1) * CH], AF.Copy))
        p.bar()
        p.dma(xn1V[:, :, c0:c0 + CH],
              A2[:].rearrange("p (kt t) -> p kt t", kt=DT)[:, :, c0:c0 + CH])
        p.bar()
        for st_ in range(CH // 128):
            t0 = st_ * 128
            for kt in range(DT):
                p.op("tensor", lambda E, kt_=kt, t0_=t0: E.matmul(
                    mb[:, 0:8], A4[:, kt_ * CH + t0_: kt_ * CH + t0_ + 128],
                    wg1[:, kt_ * 8:(kt_ + 1) * 8],
                    start=(kt_ == 0), stop=(kt_ == DT - 1)))
            p.bar()
            G = gbuf[:, 0:8]; Gm = gbuf[:, 8:16]; W_ = gbuf[:, 16:24]
            mx = gred[:, 0:1]; sm = gred[:, 1:2]
            p.op("scalar", lambda E: E.activation(G, mb[:, 0:8], AF.Copy))
            p.bar()
            p.op("vector", lambda E: E.tensor_reduce(mx, G, AX.X, ALU.max))
            p.bar()
            p.op("vector", lambda E: E.tensor_scalar(Gm, G, mx, None, ALU.is_ge))
            p.bar()
            p.op("vector", lambda E: E.tensor_scalar(mx, mx, -1.0, None, ALU.mult))
            p.bar()
            p.op("scalar", lambda E: E.activation(G, G, AF.Exp, bias=mx))
            p.bar()
            p.op("vector", lambda E: E.tensor_reduce(sm, G, AX.X, ALU.add))
            p.bar()
            p.op("vector", lambda E: E.reciprocal(sm, sm))
            p.bar()
            p.op("vector", lambda E: E.tensor_scalar(G, G, sm, None, ALU.mult))
            p.bar()
            p.op("vector", lambda E: E.tensor_tensor(W_, G, Gm, ALU.mult))
            p.bar()
            p.op("vector", lambda E: E.tensor_reduce(sm, W_, AX.X, ALU.add))
            p.bar()
            p.dma(mkO[c0 + t0:c0 + t0 + 128, :], Gm)
            p.dma(gsO[c0 + t0:c0 + t0 + 128, :], sm)
            p.bar()

    for proj, (wD_, outV) in enumerate(((wqD, qV), (wkD, kV), (wvD, vV))):
        load_weight_bf16(p, wD_, DT, D, wst, A3)
        for ch in range(NCH):
            c0 = ch * CH

            def evp(m, ps_):
                p.op("scalar", lambda E, m_=m, q=ps_: E.activation(
                    A6[:, m_ * CH:(m_ + 1) * CH], q, AF.Copy))
            emit_mm(p, A3, DT, DT,
                    lambda k, c0_=c0: A2[:, k * TPC + c0_: k * TPC + c0_ + CH],
                    CH, pmm, evp, wcols=D)
            p.dma(outV[:, :, c0:c0 + CH], A6[:].rearrange("p (kt t) -> p kt t", kt=DT))
            p.bar()
    return p


def run_moe0(I, Hst):
    p = build_moe0()
    nc = p.finish()
    x0T = np.ascontiguousarray(I["L0_attn_xq"].reshape(NTOK, D).T)
    rwT = Hst["rwkvT"].reshape(D, NTOK)
    w0T = np.ascontiguousarray(Hst["w0"].T)
    be = I["L0_ffn_be"]
    beH = np.ascontiguousarray(be.reshape(8, DT, 128).transpose(2, 0, 1).reshape(128, 8 * DT))
    in_maps = []
    for c in range(NCORES):
        sl = slice(c * TPC, (c + 1) * TPC)
        in_maps.append({
            "rwkvT": np.ascontiguousarray(rwT[:, sl]).reshape(DT, 128, TPC),
            "x0T": np.ascontiguousarray(x0T[:, sl]).reshape(DT, 128, TPC),
            "xn0bf": np.ascontiguousarray(Hst["xn0bf"][:, sl]).reshape(DT, 128, TPC),
            "w0T": np.ascontiguousarray(w0T[:, sl]),
            "WoT": np.ascontiguousarray(I["L0_attn_Wo"].T),
            "We": I["L0_ffn_We"],
            "beH": beH,
            "WqT": np.ascontiguousarray(I["L1_attn_Wq"].T),
            "WkT": np.ascontiguousarray(I["L1_attn_Wk"].T),
            "WvT": np.ascontiguousarray(I["L1_attn_Wv"].T),
            "wg1": I["L1_ffn_wg"],
        })
    res = run_bass_kernel_spmd(nc, in_maps, core_ids=list(range(NCORES)))
    p.stack.close()
    Hst["x1T"] = np.concatenate([r["x1T"].reshape(D, TPC) for r in res.results], axis=1)
    Hst["xn1bf"] = np.concatenate([r["xn1bf"].reshape(D, TPC) for r in res.results], axis=1)
    Hst["qbf"] = np.concatenate([r["qbf"].reshape(D, TPC) for r in res.results], axis=1)
    Hst["kbf"] = np.concatenate([r["kbf"].reshape(D, TPC) for r in res.results], axis=1)
    Hst["vbf"] = np.concatenate([r["vbf"].reshape(D, TPC) for r in res.results], axis=1)
    Hst["mask1"] = np.concatenate([r["mask"] for r in res.results], axis=0)
    Hst["gs"] = np.concatenate([r["gs"] for r in res.results], axis=0)[:, 0]



def build_attn():
    p = P("attn")
    NP = 8
    qD = p.din("qTs", (NP, 64, T), BF16)
    kD = p.din("kTs", (NP, 64, T), BF16)
    vD = p.din("vs", (NP, T, 64), BF16)
    mD = p.din("mrow", (1, NTOK))
    gD = p.din("gsrow", (1, NTOK))
    aoO = p.dout("ao", (NP, 64, T))
    wsO = p.dout("wswr", (1, NTOK))

    onesb = p.sb([128, 1], F32, "onesb")
    onesr = p.sb([1, 128], F32, "onesr")
    ones1 = p.sb([1, T], F32, "ones1")
    onesbBF = p.sb([128, 1], BF16, "onesbBF")
    p.op("gpsimd", lambda E: E.memset(onesb[:], 1.0))
    p.op("gpsimd", lambda E: E.memset(onesr[:], 1.0))
    p.op("gpsimd", lambda E: E.memset(ones1[:], 1.0))
    p.op("gpsimd", lambda E: E.memset(onesbBF[:], 1.0))
    p.bar()

    NPieces = 4
    PW = NTOK // NPieces
    cums = [p.sb([1, PW], F32, f"cum{i}") for i in range(NPieces)]
    mrb = p.sb([1, PW], F32, "mrb")
    wout = p.sb([1, PW], F32, "wout")
    for pc in range(NPieces):
        p.dma(mrb[:], mD[:, pc * PW:(pc + 1) * PW])
        p.bar()
        init = 0.0 if pc == 0 else cums[pc - 1][:, PW - 1:PW]
        p.op("vector", lambda E, i=init, pc_=pc: E.tensor_tensor_scan(
            cums[pc_][:], ones1[:, 0:PW], mrb[:], i, ALU.mult, ALU.add))
        p.bar()
        p.op("vector", lambda E, pc_=pc: E.tensor_scalar(
            wout[:], cums[pc_][:], 2048.5, None, ALU.is_lt))
        p.bar()
        p.op("vector", lambda E: E.tensor_tensor(wout[:], wout[:], mrb[:], ALU.mult))
        p.bar()
        p.dma(mrb[:], gD[:, pc * PW:(pc + 1) * PW])
        p.bar()
        p.op("vector", lambda E: E.tensor_tensor(wout[:], wout[:], mrb[:], ALU.mult))
        p.bar()
        p.dma(wsO[:, pc * PW:(pc + 1) * PW], wout[:])
        p.bar()

    qT = p.sb([64, T], BF16, "qT")
    kT = p.sb([64, T], BF16, "kT")
    vt = p.sb([128, 32 * 64], BF16, "vt")
    EsT = p.sb([128, 32 * 512], BF16, "EsT")
    numsb = p.sb([64, 512], F32, "numsb")
    aob = p.sb([64, T], F32, "aob")
    dsb = p.sb([1, 512], F32, "dsb")
    psc = [p.ps([128, 512], F32, f"psc{i}") for i in range(4)]
    pden = p.ps([1, 512], F32, "pden")
    pnum = p.ps([64, 512], F32, "pnum")
    pdb = p.ps([64, 512], F32, "pdb")
    NST = T // 128

    for i in range(8):
        p.dma(qT[:], qD[i])
        p.dma(kT[:], kD[i])
        p.dma(vt[:].rearrange("p (tt d) -> p tt d", tt=NST), vD[i].rearrange("(tt p) d -> p tt d", p=128))
        p.bar()
        for qg in range(T // 512):
            q0 = qg * 512
            for sg in range(0, NST, 4):
                for si in range(4):
                    st = sg + si
                    p.op("tensor", lambda E, st_=st, si_=si, q0_=q0: E.matmul(
                        psc[si_][:], kT[:, st_ * 128:(st_ + 1) * 128],
                        qT[:, q0_:q0_ + 512], start=True, stop=True))
                p.bar()
                for si in range(4):
                    st = sg + si
                    p.op("scalar", lambda E, st_=st, si_=si: E.activation(
                        EsT[:, st_ * 512:(st_ + 1) * 512], psc[si_][:],
                        AF.Exp, scale=np.float32(0.125)))
                p.bar()
            for st in range(NST):
                p.op("tensor", lambda E, st_=st: E.matmul(
                    pden[:], onesbBF[:, 0:1], EsT[:, st_ * 512:(st_ + 1) * 512],
                    start=(st_ == 0), stop=(st_ == NST - 1)))
            for st in range(NST):
                p.op("tensor", lambda E, st_=st: E.matmul(
                    pnum[:], vt[:, st_ * 64:(st_ + 1) * 64],
                    EsT[:, st_ * 512:(st_ + 1) * 512],
                    start=(st_ == 0), stop=(st_ == NST - 1)))
            p.bar()
            p.op("vector", lambda E: E.tensor_scalar(dsb[:], pden[:], 1e-8, None, ALU.add))
            p.op("scalar", lambda E: E.activation(numsb[:], pnum[:], AF.Copy))
            p.bar()
            p.op("vector", lambda E: E.reciprocal(dsb[:], dsb[:]))
            p.bar()
            p.op("tensor", lambda E: E.matmul(pdb[:], onesr[:, 0:64], dsb[:], start=True, stop=True))
            p.bar()
            p.op("vector", lambda E, q0_=q0: E.tensor_tensor(
                aob[:, q0_:q0_ + 512], numsb[:], pdb[:], ALU.mult))
            p.bar()
        p.dma(aoO[i], aob[:])
        p.bar()
    return p


def run_attn(I, Hst):
    p = build_attn()
    nc = p.finish()
    Qr = Hst["qbf"].reshape(16, 64, B, T)
    Kr = Hst["kbf"].reshape(16, 64, B, T)
    Vr = Hst["vbf"].reshape(16, 64, B, T)
    m1T = np.ascontiguousarray(Hst["mask1"].T)
    gsr = np.ascontiguousarray(Hst["gs"].reshape(1, NTOK))
    in_maps = []
    for c in range(NCORES):
        qs = []; ks = []; vs = []
        for j in range(8):
            pr = c * 8 + j
            b = pr // 16; h = pr % 16
            qs.append(np.ascontiguousarray(Qr[h, :, b, :]))
            ks.append(np.ascontiguousarray(Kr[h, :, b, :]))
            vs.append(np.ascontiguousarray(Vr[h, :, b, :].T))
        in_maps.append({
            "qTs": np.stack(qs), "kTs": np.stack(ks), "vs": np.stack(vs),
            "mrow": np.ascontiguousarray(m1T[c:c + 1, :]),
            "gsrow": gsr,
        })
    res = run_bass_kernel_spmd(nc, in_maps, core_ids=list(range(NCORES)))
    p.stack.close()
    AO = np.empty((B, 16, 64, T), np.float32)
    for c in range(NCORES):
        for j in range(8):
            pr = c * 8 + j
            AO[pr // 16, pr % 16] = res.results[c]["ao"][j]
    Hst["aoT"] = np.ascontiguousarray(AO.transpose(1, 2, 0, 3).reshape(D, NTOK))
    Hst["wsw"] = np.concatenate([r["wswr"] for r in res.results], axis=0)



PREP_SPECS = None


def build_prep(specs):
    p = P("prep")
    CW = 4096
    stg = [p.sb([128, CW], F32, f"pst{i}") for i in range(2)]
    cvt = [p.sb([128, CW], BF16, f"pcv{i}") for i in range(2)]
    for name, cols in specs:
        din = p.din(name, (128, cols))
        dout = p.dout(name + "_bf", (128, cols), BF16)
        for c0 in range(0, cols, CW):
            w = min(CW, cols - c0)
            i = (c0 // CW) % 2
            p.dma(stg[i][:, 0:w], din[:, c0:c0 + w])
            p.bar()
            p.op("scalar", lambda E, i_=i, w_=w: E.activation(
                cvt[i_][:, 0:w_], stg[i_][:, 0:w_], AF.Copy))
            p.bar()
            p.dma(dout[:, c0:c0 + w], cvt[i][:, 0:w])
            p.bar()
    return p


def run_prep(I, Hst):
    raw = {
        "Wk0": I["L0_attn_Wk"].T, "Wv0": I["L0_attn_Wv"].T, "Wr0": I["L0_attn_Wr"].T,
        "Wo0": I["L0_attn_Wo"].T, "We0": I["L0_ffn_We"],
        "Wq1": I["L1_attn_Wq"].T, "Wk1": I["L1_attn_Wk"].T, "Wv1": I["L1_attn_Wv"].T,
        "Wo1": I["L1_attn_Wo"].T,
        "w1s": I["L1_ffn_w1"], "w2s": I["L1_ffn_w2"],
        "Win1": I["L2_attn_m1_W_in"].T, "Win2": I["L2_attn_m2_W_in"].T,
        "Wxp1": I["L2_attn_m1_W_xp"].T, "Wxp2": I["L2_attn_m2_W_xp"].T,
        "Wdt1": I["L2_attn_m1_W_dt"].T, "Wdt2": I["L2_attn_m2_W_dt"].T,
        "Wout1": I["L2_attn_m1_W_out"].T, "Wout2": I["L2_attn_m2_W_out"].T,
        "tw1": I["L2_ffn_w1_w"].T, "tw2": I["L2_ffn_w2_w"].T,
        "th1": I["L2_ffn_h1_w"].T, "th2": I["L2_ffn_h2_w"].T,
        "tw3": I["L2_ffn_w3_w"].T,
        "uw": I["L3_attn_u_w"].T, "vw": I["L3_attn_v_w"].T, "ow": I["L3_attn_o_w"].T,
        "fw1": I["L3_ffn_w1"], "fw2": I["L3_ffn_w2"],
    }
    shapes = {}
    specs = []
    for k, v in raw.items():
        v = np.ascontiguousarray(v, np.float32)
        shapes[k] = v.shape
        n = v.size
        assert n % (8 * 128) == 0, (k, v.shape)
        specs.append((k, n // (8 * 128)))
    p = build_prep(specs)
    nc = p.finish()
    in_maps = []
    for c in range(NCORES):
        m = {}
        for k, v in raw.items():
            vv = np.ascontiguousarray(v, np.float32).reshape(8, -1)
            m[k] = vv[c].reshape(128, -1)
        in_maps.append(m)
    res = run_bass_kernel_spmd(nc, in_maps, core_ids=list(range(NCORES)))
    p.stack.close()
    WB = {}
    for k in raw:
        parts = [res.results[c][k + "_bf"].reshape(-1) for c in range(NCORES)]
        WB[k] = np.concatenate(parts).reshape(shapes[k])
    Hst["WB"] = WB



def build_ffn1a():
    p = P("ffn1a")
    xnD = p.din("xn1bf", (DT, 128, TPC), BF16)
    w1D = p.din("w1bf", (8, D, 4096), BF16)
    w2D = p.din("w2bf", (8, 4096, D), BF16)
    wsD = p.din("wswT", (8, TPC))
    g1O = p.dout("g1", (DT, 128, TPC))
    xnV = xnD.rearrange("kt p t -> p kt t")
    g1V = g1O.rearrange("kt p t -> p kt t")

    onesr = p.sb([1, 128], F32, "onesr")
    p.op("gpsimd", lambda E: E.memset(onesr[:], 1.0))
    p.bar()
    acc = p.sb([128, DT * TPC], F32, "acc")
    wA = p.sb([128, DT * 2048], BF16, "wA")
    wB = p.sb([128, 16 * D], BF16, "wB")
    hbuf = p.sb([128, 32 * CH], BF16, "hbuf")
    xnc = p.sb([128, DT * CH], BF16, "xnc")
    ytmp = p.sb([128, DT * CH], F32, "ytmp")
    wtmp = p.sb([128, DT * CH], F32, "wtmp")
    wrow = p.sb([1, CH], F32, "wrow")
    pm = [p.ps([128, CH], F32, f"pm{i}") for i in range(8)]

    for ch in range(NCH):
        c0 = ch * CH
        p.dma(xnc[:].rearrange("p (kt t) -> p kt t", kt=DT), xnV[:, :, c0:c0 + CH])
        p.bar()
        for e in range(8):
            for mh in range(2):
                p.dma(wA[:].rearrange("p (kt m) -> p kt m", m=2048),
                      w1D[e].rearrange("(kt p) m -> p kt m", p=128)[:, :, mh * 2048:(mh + 1) * 2048])
                p.bar()
                for mg in range(2):
                    for mi in range(8):
                        m = mg * 8 + mi
                        for k in range(DT):
                            p.op("tensor", lambda E, m_=m, mi_=mi, k_=k: E.matmul(
                                pm[mi_][:], wA[:, k_ * 2048 + m_ * 128: k_ * 2048 + (m_ + 1) * 128],
                                xnc[:, k_ * CH:(k_ + 1) * CH],
                                start=(k_ == 0), stop=(k_ == DT - 1)))
                    p.bar()
                    for mi in range(8):
                        m = mh * 16 + mg * 8 + mi
                        p.op("scalar", lambda E, m_=m, mi_=mi: E.activation(
                            hbuf[:, m_ * CH:(m_ + 1) * CH], pm[mi_][:], AF.Relu))
                    p.bar()
            for kh in range(2):
                p.dma(wB[:].rearrange("p (kt m) -> p kt m", m=D),
                      w2D[e].rearrange("(khh kt p) m -> khh p kt m", khh=2, p=128)[kh])
                p.bar()
                for m in range(DT):
                    for k in range(16):
                        kg = kh * 16 + k
                        p.op("tensor", lambda E, m_=m, k_=k, kg_=kg, kh_=kh: E.matmul(
                            pm[m_][:], wB[:, k_ * D + m_ * 128: k_ * D + (m_ + 1) * 128],
                            hbuf[:, kg_ * CH:(kg_ + 1) * CH],
                            start=(kh_ == 0 and k_ == 0), stop=(kh_ == 1 and k_ == 15)))
                p.bar()
            p.dma(wrow[:], wsD[e:e + 1, c0:c0 + CH])
            p.bar()
            pb = p.ps([128, CH], F32, "pbr") if False else None
            for m in range(DT):
                p.op("scalar", lambda E, m_=m: E.activation(
                    ytmp[:, m_ * CH:(m_ + 1) * CH], pm[m_][:], AF.Copy))
            p.bar()
            p.op("tensor", lambda E: E.matmul(pm[0][:], onesr[:], wrow[:], start=True, stop=True))
            p.bar()
            if e == 0:
                for m in range(DT):
                    p.op("vector", lambda E, m_=m, c0_=c0: E.tensor_tensor(
                        acc[:, m_ * TPC + c0_: m_ * TPC + c0_ + CH],
                        ytmp[:, m_ * CH:(m_ + 1) * CH], pm[0][:], ALU.mult))
                p.bar()
            else:
                for m in range(DT):
                    p.op("vector", lambda E, m_=m: E.tensor_tensor(
                        wtmp[:, m_ * CH:(m_ + 1) * CH],
                        ytmp[:, m_ * CH:(m_ + 1) * CH], pm[0][:], ALU.mult))
                p.bar()
                for m in range(DT):
                    p.op("vector", lambda E, m_=m, c0_=c0: E.tensor_tensor(
                        acc[:, m_ * TPC + c0_: m_ * TPC + c0_ + CH],
                        acc[:, m_ * TPC + c0_: m_ * TPC + c0_ + CH],
                        wtmp[:, m_ * CH:(m_ + 1) * CH], ALU.add))
                p.bar()
    for m in range(DT):
        for ch in range(NCH):
            p.op("scalar", lambda E, m_=m, c_=ch: E.activation(
                acc[:, m_ * TPC + c_ * CH: m_ * TPC + (c_ + 1) * CH],
                acc[:, m_ * TPC + c_ * CH: m_ * TPC + (c_ + 1) * CH], AF.Silu))
    p.bar()
    for ch in range(NCH):
        c0 = ch * CH
        p.dma(g1V[:, :, c0:c0 + CH],
              acc[:].rearrange("p (kt t) -> p kt t", kt=DT)[:, :, c0:c0 + CH])
    p.bar()
    return p


def run_ffn1a(I, Hst):
    p = build_ffn1a()
    nc = p.finish()
    wswT = Hst["wsw"]
    in_maps = []
    for c in range(NCORES):
        sl = slice(c * TPC, (c + 1) * TPC)
        in_maps.append({
            "xn1bf": np.ascontiguousarray(Hst["xn1bf"][:, sl]).reshape(DT, 128, TPC),
            "w1bf": Hst["WB"]["w1s"],
            "w2bf": Hst["WB"]["w2s"],
            "wswT": np.ascontiguousarray(wswT[:, sl]),
        })
    res = run_bass_kernel_spmd(nc, in_maps, core_ids=list(range(NCORES)))
    p.stack.close()
    Hst["g1T"] = np.concatenate([r["g1"].reshape(D, TPC) for r in res.results], axis=1)



def build_ffn1b():
    p = P("ffn1b")
    aoD = p.din("aoT", (DT, 128, TPC))
    woD = p.din("Wo1bf", (D, D), BF16)
    boD = p.din("boH", (128, DT))
    x1D = p.din("x1T", (DT, 128, TPC))
    g1D = p.din("g1T", (DT, 128, TPC))
    lnwD = p.din("ln1wH", (128, DT)); lnbD = p.din("ln1bH", (128, DT))
    win1D = p.din("Win1bf", (D, 4096), BF16)
    win2D = p.din("Win2bf", (D, 4096), BF16)
    x2O = p.dout("x2T", (DT, 128, TPC))
    xn2O = p.dout("xn2bf", (DT, 128, TPC), BF16)
    xlnO = p.dout("xlnT", (DT, 128, TPC))
    xpO = [p.dout(nm, (16, 128, TPC)) for nm in ("xp1", "z1", "xp2", "z2")]
    aoV = aoD.rearrange("kt p t -> p kt t")
    x1V = x1D.rearrange("kt p t -> p kt t")
    g1V = g1D.rearrange("kt p t -> p kt t")
    x2V = x2O.rearrange("kt p t -> p kt t")
    xn2V = xn2O.rearrange("kt p t -> p kt t")
    xlnV = xlnO.rearrange("kt p t -> p kt t")
    xpV = [o.rearrange("kt p t -> p kt t") for o in xpO]

    onesb = p.sb([128, 1], F32, "onesb")
    onesr = p.sb([1, 128], F32, "onesr")
    p.op("gpsimd", lambda E: E.memset(onesb[:], 1.0))
    p.op("gpsimd", lambda E: E.memset(onesr[:], 1.0))
    p.bar()
    s1p = p.ps([1, CH], F32, "s1"); s2p = p.ps([1, CH], F32, "s2")
    rb = p.ps([128, CH], F32, "rb"); mb = p.ps([128, CH], F32, "mb")
    pmm = [p.ps([128, CH], F32, f"pmm{i}") for i in range(4)]
    lnst = p.sb([1, 4 * CH], F32, "lnst")
    wo = p.sb([128, DT * D], BF16, "wo")
    boH = p.sb([128, DT], F32, "boHs")
    lnw = p.sb([128, DT], F32, "lnw"); lnb = p.sb([128, DT], F32, "lnbs")
    aoc = p.sb([128, DT * CH], F32, "aoc")
    aocb = p.sb([128, DT * CH], BF16, "aocb")
    x1c = p.sb([128, DT * CH], F32, "x1c")
    sqc = p.sb([128, DT * CH], F32, "sqc")
    xnc = p.sb([128, DT * CH], F32, "xnc")
    xlc = p.sb([128, DT * CH], F32, "xlc")
    xlbf = p.sb([128, DT * TPC], BF16, "xlbf")
    xzc = p.sb([128, 8 * CH], F32, "xzc")
    p.dma(boH[:], boD[:]); p.dma(lnw[:], lnwD[:]); p.dma(lnb[:], lnbD[:])
    for k_ in range(DT):
        p.dma(wo[:, k_ * D:(k_ + 1) * D], woD[k_ * 128:(k_ + 1) * 128, :])
    p.bar()

    for ch in range(NCH):
        c0 = ch * CH
        p.dma(aoc[:].rearrange("p (kt t) -> p kt t", kt=DT), aoV[:, :, c0:c0 + CH])
        p.dma(x1c[:].rearrange("p (kt t) -> p kt t", kt=DT), x1V[:, :, c0:c0 + CH])
        p.bar()
        for j in range(DT):
            p.op("scalar", lambda E, j_=j: E.activation(
                aocb[:, j_ * CH:(j_ + 1) * CH], aoc[:, j_ * CH:(j_ + 1) * CH], AF.Copy))
        p.bar()

        def evs1(m, ps_):
            p.op("vector", lambda E, m_=m, q=ps_: E.tensor_scalar(
                sqc[:, m_ * CH:(m_ + 1) * CH], q, boH[:, m_:m_ + 1], None, ALU.add))
        emit_mm(p, wo, DT, DT, lambda k: aocb[:, k * CH:(k + 1) * CH], CH, pmm, evs1, wcols=D)
        p.dma(xnc[:].rearrange("p (kt t) -> p kt t", kt=DT), g1V[:, :, c0:c0 + CH])
        p.bar()
        for j in range(DT):
            p.op("vector", lambda E, j_=j: E.tensor_tensor(
                sqc[:, j_ * CH:(j_ + 1) * CH], sqc[:, j_ * CH:(j_ + 1) * CH],
                xnc[:, j_ * CH:(j_ + 1) * CH], ALU.mult))
        p.bar()
        for j in range(DT):
            p.op("vector", lambda E, j_=j: E.tensor_tensor(
                x1c[:, j_ * CH:(j_ + 1) * CH], x1c[:, j_ * CH:(j_ + 1) * CH],
                sqc[:, j_ * CH:(j_ + 1) * CH], ALU.add))
        p.bar()
        p.dma(x2V[:, :, c0:c0 + CH], x1c[:].rearrange("p (kt t) -> p kt t", kt=DT))
        p.bar()
        srcs = [x1c[:, j * CH:(j + 1) * CH] for j in range(DT)]
        sqd = [sqc[:, j * CH:(j + 1) * CH] for j in range(DT)]
        xns = [xnc[:, j * CH:(j + 1) * CH] for j in range(DT)]
        ln_stats(p, srcs, CH, onesb, onesr, s1p[:], s2p[:], rb[:], mb[:], sqd, lnst)
        ln_apply(p, srcs, rb[:], mb[:], xns)
        for j in range(DT):
            p.op("scalar", lambda E, j_=j: E.activation(
                aocb[:, j_ * CH:(j_ + 1) * CH], xnc[:, j_ * CH:(j_ + 1) * CH], AF.Copy))
        p.bar()
        p.dma(xn2V[:, :, c0:c0 + CH], aocb[:].rearrange("p (kt t) -> p kt t", kt=DT))
        p.bar()
        srcs2 = [xnc[:, j * CH:(j + 1) * CH] for j in range(DT)]
        xls = [xlc[:, j * CH:(j + 1) * CH] for j in range(DT)]
        ln_stats(p, srcs2, CH, onesb, onesr, s1p[:], s2p[:], rb[:], mb[:], sqd, lnst)
        ln_apply(p, srcs2, rb[:], mb[:], xls)
        for j in range(DT):
            p.op("vector", lambda E, j_=j: E.tensor_scalar(
                xlc[:, j_ * CH:(j_ + 1) * CH], xlc[:, j_ * CH:(j_ + 1) * CH],
                lnw[:, j_:j_ + 1], lnb[:, j_:j_ + 1], ALU.mult, ALU.add))
        p.bar()
        p.dma(xlnV[:, :, c0:c0 + CH], xlc[:].rearrange("p (kt t) -> p kt t", kt=DT))
        for j in range(DT):
            p.op("scalar", lambda E, j_=j, c0_=c0: E.activation(
                xlbf[:, j_ * TPC + c0_: j_ * TPC + c0_ + CH],
                xlc[:, j_ * CH:(j_ + 1) * CH], AF.Copy))
        p.bar()

    wih = p.sb([128, DT * 2048], BF16, "wih")
    for mi_, winD in enumerate((win1D, win2D)):
        for mh in range(2):
            p.dma(wih[:].rearrange("p (kt m) -> p kt m", m=2048),
                  winD.rearrange("(kt p) m -> p kt m", p=128)[:, :, mh * 2048:(mh + 1) * 2048])
            p.bar()
            outV = xpV[mi_ * 2] if mh == 0 else xpV[mi_ * 2 + 1]
            for ch in range(NCH):
                c0 = ch * CH
                for mg8 in range(2):

                    def evp(m, ps_):
                        p.op("scalar", lambda E, m_=m, q=ps_: E.activation(
                            xzc[:, m_ * CH:(m_ + 1) * CH], q, AF.Copy))
                    emit_mm(p, wih[:, mg8 * 1024:], DT, 8,
                            lambda k, c0_=c0: xlbf[:, k * TPC + c0_: k * TPC + c0_ + CH],
                            CH, pmm, evp, wcols=2048)
                    p.dma(outV[:, mg8 * 8:(mg8 + 1) * 8, c0:c0 + CH],
                          xzc[:].rearrange("p (kt t) -> p kt t", kt=8))
                    p.bar()
    return p


def run_ffn1b(I, Hst):
    p = build_ffn1b()
    nc = p.finish()
    WB = Hst["WB"]
    boH = np.ascontiguousarray(I["L1_attn_bo"].reshape(DT, 128).T)
    lnw = np.ascontiguousarray(I["L2_attn_ln1_w"].reshape(DT, 128).T)
    lnb = np.ascontiguousarray(I["L2_attn_ln1_b"].reshape(DT, 128).T)
    in_maps = []
    for c in range(NCORES):
        sl = slice(c * TPC, (c + 1) * TPC)
        in_maps.append({
            "aoT": np.ascontiguousarray(Hst["aoT"][:, sl]).reshape(DT, 128, TPC),
            "Wo1bf": WB["Wo1"],
            "boH": boH,
            "x1T": np.ascontiguousarray(Hst["x1T"][:, sl]).reshape(DT, 128, TPC),
            "g1T": np.ascontiguousarray(Hst["g1T"][:, sl]).reshape(DT, 128, TPC),
            "ln1wH": lnw, "ln1bH": lnb,
            "Win1bf": WB["Win1"], "Win2bf": WB["Win2"],
        })
    res = run_bass_kernel_spmd(nc, in_maps, core_ids=list(range(NCORES)))
    p.stack.close()
    for nm, key in (("x2T", "x2T"), ("xlnT", "xlnT")):
        Hst[key] = np.concatenate([r[nm].reshape(D, TPC) for r in res.results], axis=1)
    Hst["xn2bf"] = np.concatenate([r["xn2bf"].reshape(D, TPC) for r in res.results], axis=1)
    for nm in ("xp1", "z1", "xp2", "z2"):
        Hst[nm] = np.concatenate([r[nm].reshape(2048, TPC) for r in res.results], axis=1)



def build_conv():
    p = P("conv")
    xps = [p.din("xp1S", (8, 128, T)), p.din("xp2S", (8, 128, T))]
    cws = [p.din("cw1", (128, 8 * 4)), p.din("cw2", (128, 8 * 4))]
    cbs = [p.din("cb1", (128, 8)), p.din("cb2", (128, 8))]
    outs = [p.dout("xc1S", (8, 128, T)), p.dout("xc2S", (8, 128, T))]
    xpb = p.sb([128, T + 3], F32, "xpb")
    accA = p.sb([128, T], F32, "accA")
    accB = p.sb([128, T], F32, "accB")
    xcb = p.sb([128, T], F32, "xcb")
    cw = p.sb([128, 8 * 4 * 2], F32, "cw")
    cb = p.sb([128, 8 * 2], F32, "cb")
    p.dma(cw[:, 0:32], cws[0][:]); p.dma(cw[:, 32:64], cws[1][:])
    p.dma(cb[:, 0:8], cbs[0][:]); p.dma(cb[:, 8:16], cbs[1][:])
    p.op("gpsimd", lambda E: E.memset(xpb[:, 0:3], 0.0))
    p.bar()
    for mi in range(2):
        for j in range(8):
            p.dma(xpb[:, 3:T + 3], xps[mi][j])
            p.bar()
            cwo = mi * 32 + j * 4
            p.op("vector", lambda E, c=cwo: E.tensor_scalar(
                accA[:], xpb[:, 0:T], cw[:, c:c + 1], None, ALU.mult))
            p.bar()
            p.op("vector", lambda E, c=cwo: E.scalar_tensor_tensor(
                accB[:], xpb[:, 1:T + 1], cw[:, c + 1:c + 2], accA[:], ALU.mult, ALU.add))
            p.bar()
            p.op("vector", lambda E, c=cwo: E.scalar_tensor_tensor(
                accA[:], xpb[:, 2:T + 2], cw[:, c + 2:c + 3], accB[:], ALU.mult, ALU.add))
            p.bar()
            p.op("vector", lambda E, c=cwo: E.scalar_tensor_tensor(
                accB[:], xpb[:, 3:T + 3], cw[:, c + 3:c + 4], accA[:], ALU.mult, ALU.add))
            p.bar()
            p.op("scalar", lambda E, mi_=mi, j_=j: E.activation(
                xcb[:], accB[:], AF.Silu, bias=cb[:, mi_ * 8 + j_: mi_ * 8 + j_ + 1]))
            p.bar()
            p.dma(outs[mi][j], xcb[:])
            p.bar()
    return p


def run_conv(I, Hst):
    p = build_conv()
    nc = p.finish()
    XP1 = Hst["xp1"].reshape(2048, B, T)
    XP2f = np.ascontiguousarray(np.flip(Hst["xp2"].reshape(2048, B, T), axis=2))
    def cwl(pre):
        c = I[f"L2_attn_{pre}_conv_w"]
        return c.reshape(2, 8, 128, 4)
    def cbl(pre):
        c = I[f"L2_attn_{pre}_conv_b"]
        return c.reshape(2, 8, 128)
    cw1 = cwl("m1"); cw2 = cwl("m2"); cb1 = cbl("m1"); cb2 = cbl("m2")
    in_maps = []
    for c in range(NCORES):
        b = c // 2; eh = c % 2
        sl = slice(eh * 1024, (eh + 1) * 1024)
        in_maps.append({
            "xp1S": np.ascontiguousarray(XP1[sl, b]).reshape(8, 128, T),
            "xp2S": np.ascontiguousarray(XP2f[sl, b]).reshape(8, 128, T),
            "cw1": np.ascontiguousarray(cw1[eh].transpose(1, 0, 2).reshape(128, 32)),
            "cw2": np.ascontiguousarray(cw2[eh].transpose(1, 0, 2).reshape(128, 32)),
            "cb1": np.ascontiguousarray(cb1[eh].transpose(1, 0)),
            "cb2": np.ascontiguousarray(cb2[eh].transpose(1, 0)),
        })
    res = run_bass_kernel_spmd(nc, in_maps, core_ids=list(range(NCORES)))
    p.stack.close()
    XC1 = np.empty((2048, B, T), np.float32)
    XC2f = np.empty((2048, B, T), np.float32)
    for c in range(NCORES):
        b = c // 2; eh = c % 2
        sl = slice(eh * 1024, (eh + 1) * 1024)
        XC1[sl, b] = res.results[c]["xc1S"].reshape(1024, T)
        XC2f[sl, b] = res.results[c]["xc2S"].reshape(1024, T)
    Hst["XC1"] = XC1
    Hst["XC2f"] = XC2f



def build_xdbl():
    p = P("xdbl")
    TH = T // 2
    xcD = [p.din("xc1S", (16, 128, TH)), p.din("xc2S", (16, 128, TH))]
    wxD = [p.din("Wxp1bf", (2048, 192), BF16), p.din("Wxp2bf", (2048, 192), BF16)]
    wdD = [p.din("Wdt1bf", (64, 2048), BF16), p.din("Wdt2bf", (64, 2048), BF16)]
    dtbD = [p.din("dtb1", (128, 16)), p.din("dtb2", (128, 16))]
    dtO = [p.dout("dt1S", (16, 128, TH)), p.dout("dt2S", (16, 128, TH))]
    dtxO = [p.dout("dtx1S", (16, 128, TH), BF16), p.dout("dtx2S", (16, 128, TH), BF16)]
    bmO = [p.dout("Bm1S", (64, TH), BF16), p.dout("Bm2S", (64, TH), BF16)]
    cmO = [p.dout("Cm1S", (64, TH)), p.dout("Cm2S", (64, TH))]

    wxp = p.sb([128, 16 * 192], BF16, "wxp")
    wdt = p.sb([64, 2048], BF16, "wdt")
    dtb = p.sb([128, 16], F32, "dtb")
    xcc = p.sb([128, 16 * CH], F32, "xcc")
    xcbf = p.sb([128, 16 * CH], BF16, "xcbf")
    dtc = p.sb([128, 16 * CH], F32, "dtc")
    dtxc = p.sb([128, 16 * CH], BF16, "dtxc")
    dlo = p.sb([64, CH], BF16, "dlo")
    bmc = p.sb([64, CH], BF16, "bmc")
    cmc = p.sb([64, CH], F32, "cmc")
    t1 = p.sb([128, CH], F32, "t1")
    t2 = p.sb([128, CH], F32, "t2")
    psA = p.ps([128, CH], F32, "psA")
    psB = p.ps([128, CH], F32, "psB")
    pm = [p.ps([128, CH], F32, f"pm{i}") for i in range(4)]

    for mi in range(2):
        p.dma(wxp[:].rearrange("p (kt m) -> p kt m", m=192),
              wxD[mi].rearrange("(kt p) m -> p kt m", p=128))
        p.dma(wdt[:], wdD[mi][:])
        p.dma(dtb[:], dtbD[mi][:])
        p.bar()
        for ch in range(4):
            c0 = ch * CH
            p.dma(xcc[:].rearrange("p (kt t) -> p kt t", kt=16), xcD[mi].rearrange("kt p t -> p kt t")[:, :, c0:c0 + CH])
            p.bar()
            for k in range(16):
                p.op("scalar", lambda E, k_=k: E.activation(
                    xcbf[:, k_ * CH:(k_ + 1) * CH], xcc[:, k_ * CH:(k_ + 1) * CH], AF.Copy))
            p.bar()
            for k in range(16):
                p.op("tensor", lambda E, k_=k: E.matmul(
                    psA[:], wxp[:, k_ * 192: k_ * 192 + 128],
                    xcbf[:, k_ * CH:(k_ + 1) * CH],
                    start=(k_ == 0), stop=(k_ == 15)))
            for k in range(16):
                p.op("tensor", lambda E, k_=k: E.matmul(
                    psB[0:64, :], wxp[:, k_ * 192 + 128: k_ * 192 + 192],
                    xcbf[:, k_ * CH:(k_ + 1) * CH],
                    start=(k_ == 0), stop=(k_ == 15)))
            p.bar()
            p.op("scalar", lambda E: E.activation(dlo[:], psA[0:64, :], AF.Copy))
            p.op("scalar", lambda E: E.activation(bmc[:], psA[64:128, :], AF.Copy))
            p.op("scalar", lambda E: E.activation(cmc[:], psB[0:64, :], AF.Copy))
            p.bar()
            p.dma(bmO[mi][:, c0:c0 + CH], bmc[:])
            p.dma(cmO[mi][:, c0:c0 + CH], cmc[:])
            for mg in range(4):
                for mi2 in range(4):
                    m = mg * 4 + mi2
                    p.op("tensor", lambda E, m_=m, mi2_=mi2: E.matmul(
                        pm[mi2_][:], wdt[:, m_ * 128:(m_ + 1) * 128], dlo[:],
                        start=True, stop=True))
                p.bar()
                for mi2 in range(4):
                    m = mg * 4 + mi2
                    p.op("vector", lambda E, m_=m, mi2_=mi2: E.tensor_scalar(
                        t1[:], pm[mi2_][:], dtb[:, m_:m_ + 1], None, ALU.add))
                    p.bar()
                    p.op("scalar", lambda E: E.activation(t2[:], t1[:], AF.Exp))
                    p.bar()
                    p.op("scalar", lambda E, m_=m: E.activation(
                        dtc[:, m_ * CH:(m_ + 1) * CH], t2[:], AF.Ln, bias=np.float32(1.0)))
                    p.bar()
            for k in range(16):
                p.op("vector", lambda E, k_=k: E.tensor_tensor(
                    dtxc[:, k_ * CH:(k_ + 1) * CH], dtc[:, k_ * CH:(k_ + 1) * CH],
                    xcc[:, k_ * CH:(k_ + 1) * CH], ALU.mult))
            p.bar()
            p.dma(dtO[mi].rearrange("kt p t -> p kt t")[:, :, c0:c0 + CH], dtc[:].rearrange("p (kt t) -> p kt t", kt=16))
            p.dma(dtxO[mi].rearrange("kt p t -> p kt t")[:, :, c0:c0 + CH], dtxc[:].rearrange("p (kt t) -> p kt t", kt=16))
            p.bar()
    return p


def run_xdbl(I, Hst):
    p = build_xdbl()
    nc = p.finish()
    TH = T // 2
    WB = Hst["WB"]
    def dtb_l(pre):
        return I[f"L2_attn_{pre}_dt_b"].reshape(16, 128).T
    in_maps = []
    for c in range(NCORES):
        b = c // 2; th = c % 2
        ts = slice(th * TH, (th + 1) * TH)
        in_maps.append({
            "xc1S": np.ascontiguousarray(Hst["XC1"][:, b, ts]).reshape(16, 128, TH),
            "xc2S": np.ascontiguousarray(Hst["XC2f"][:, b, ts]).reshape(16, 128, TH),
            "Wxp1bf": WB["Wxp1"], "Wxp2bf": WB["Wxp2"],
            "Wdt1bf": WB["Wdt1"], "Wdt2bf": WB["Wdt2"],
            "dtb1": np.ascontiguousarray(dtb_l("m1")),
            "dtb2": np.ascontiguousarray(dtb_l("m2")),
        })
    res = run_bass_kernel_spmd(nc, in_maps, core_ids=list(range(NCORES)))
    p.stack.close()
    DTs = {}
    for nm, dt_ in (("dt1S", np.float32), ("dt2S", np.float32),
                    ("dtx1S", BF), ("dtx2S", BF),
                    ("Bm1S", BF), ("Bm2S", BF), ("Cm1S", np.float32), ("Cm2S", np.float32)):
        if nm.startswith("dt"):
            A = np.empty((2048, B, T), dt_)
        else:
            A = np.empty((64, B, T), dt_)
        for c in range(NCORES):
            b = c // 2; th = c % 2
            ts = slice(th * TH, (th + 1) * TH)
            r = res.results[c][nm]
            if nm.startswith("dt"):
                A[:, b, ts] = r.reshape(2048, TH)
            else:
                A[:, b, ts] = r
        DTs[nm] = A
    Hst.update(DTs)



def build_scan2():
    p = P("scan2")
    TH = T // 2
    dtD = [p.din("dt1S", (8, 128, T)), p.din("dt2S", (8, 128, T))]
    dtxD = [p.din("dtx1S", (8, 128, T), BF16), p.din("dtx2S", (8, 128, T), BF16)]
    bmD = [p.din("Bm1S", (64, T), BF16), p.din("Bm2S", (64, T), BF16)]
    cmD = [p.din("Cm1S", (64, T)), p.din("Cm2S", (64, T))]
    xcD = [p.din("xc1S", (8, 128, T)), p.din("xc2S", (8, 128, T))]
    zD = [p.din("z1S", (8, 128, T)), p.din("z2S", (8, 128, T))]
    alD = [p.din("Al1", (128, 512)), p.din("Al2", (128, 512))]
    dD = [p.din("D1", (128, 8)), p.din("D2", (128, 8))]
    yO = [p.dout("y1S", (8, 128, T)), p.dout("y2S", (8, 128, T))]

    AH = p.sb([128, 1024], F32, "AH")
    dvec = p.sb([128, 16], F32, "dvec")
    NJ = 4
    dtb_ = p.sb([128, NJ * TH], BF16, "dtb_")
    dtxb = p.sb([128, NJ * TH], BF16, "dtxb")
    yac = p.sb([128, NJ * TH], F32, "yac")
    Bb2 = [p.sb([128, TH], BF16, f"Bb{i}") for i in range(2)]
    Cb2 = [p.sb([128, TH], F32, f"Cb{i}") for i in range(2)]
    al2 = [p.sb([128, TH], F32, f"al{i}") for i in range(2)]
    uu2 = [p.sb([128, TH], F32, f"uu{i}") for i in range(2)]
    hh2 = [p.sb([128, TH], F32, f"hh{i}") for i in range(2)]
    tm2 = [p.sb([128, TH], F32, f"tm{i}") for i in range(2)]
    hl = p.sb([128, 512], F32, "hl")
    dts = p.sb([128, NJ * TH], F32, "dts")

    for mi in range(2):
        p.dma(AH[:, mi * 512:(mi + 1) * 512], alD[mi][:])
        p.dma(dvec[:, mi * 8:(mi + 1) * 8], dD[mi][:])
    p.bar()
    for mi in range(2):
        p.op("scalar", lambda E, mi_=mi: E.activation(
            AH[:, mi_ * 512:(mi_ + 1) * 512], AH[:, mi_ * 512:(mi_ + 1) * 512], AF.Exp))
    p.bar()
    p.op("vector", lambda E: E.tensor_scalar(AH[:], AH[:], -1.0, None, ALU.mult))
    p.bar()

    for mi in range(2):
        for jg in range(2):
            j0 = jg * NJ
            for half in range(2):
                h0 = half * TH
                for j in range(NJ):
                    p.dma(dts[:, j * TH:(j + 1) * TH], dtD[mi][j0 + j][:, h0:h0 + TH])
                    p.dma(dtxb[:, j * TH:(j + 1) * TH], dtxD[mi][j0 + j][:, h0:h0 + TH])
                p.bar()
                for j in range(NJ):
                    p.op("scalar", lambda E, j_=j: E.activation(
                        dtb_[:, j_ * TH:(j_ + 1) * TH], dts[:, j_ * TH:(j_ + 1) * TH], AF.Copy))
                p.bar()
                p.dma(Bb2[0][:], bass.AP(bmD[mi].tensor, 0 * T + h0, [[0, 128], [1, TH]]))
                p.dma(Cb2[0][:], bass.AP(cmD[mi].tensor, 0 * T + h0, [[0, 128], [1, TH]]))
                p.bar()
                NIT = 64 * NJ
                seq = [(j, n) for n in range(64) for j in range(NJ)]
                for it in range(NIT + 3):
                    if it < NIT:
                        j, n = seq[it]
                        if j == min(2, NJ - 1) and n + 1 < 64:
                            p.dma(Bb2[(n + 1) % 2][:], bass.AP(
                                bmD[mi].tensor, (n + 1) * T + h0, [[0, 128], [1, TH]]))
                            p.dma(Cb2[(n + 1) % 2][:], bass.AP(
                                cmD[mi].tensor, (n + 1) * T + h0, [[0, 128], [1, TH]]))
                        p.op("scalar", lambda E, j_=j, n_=n, mi_=mi, i_=it, j0_=j0: E.activation(
                            al2[i_ % 2][:], dtb_[:, j_ * TH:(j_ + 1) * TH], AF.Exp,
                            scale=AH[:, mi_ * 512 + (j_ + j0_) * 64 + n_:
                                     mi_ * 512 + (j_ + j0_) * 64 + n_ + 1]))
                        p.op("gpsimd", lambda E, j_=j, n_=n, i_=it: E.tensor_tensor(
                            uu2[i_ % 2][:], dtxb[:, j_ * TH:(j_ + 1) * TH],
                            Bb2[n_ % 2][:], ALU.mult))
                    itb = it - 1
                    if 0 <= itb < NIT:
                        j, n = seq[itb]
                        hidx = (j + j0) * 64 + n
                        init = 0.0 if half == 0 else hl[:, hidx:hidx + 1]
                        p.op("vector", lambda E, i_=itb, ii=init: E.tensor_tensor_scan(
                            hh2[i_ % 2][:], al2[i_ % 2][:], uu2[i_ % 2][:], ii,
                            ALU.mult, ALU.add))
                    itc = it - 2
                    if 0 <= itc < NIT:
                        j, n = seq[itc]
                        hidx = (j + j0) * 64 + n
                        if half == 0:
                            p.op("vector", lambda E, i_=itc, h_=hidx: E.tensor_copy(
                                hl[:, h_:h_ + 1], hh2[i_ % 2][:, TH - 1:TH]))
                        p.op("gpsimd", lambda E, i_=itc, n_=n: E.tensor_tensor(
                            tm2[i_ % 2][:], hh2[i_ % 2][:], Cb2[n_ % 2][:], ALU.mult))
                    itd = it - 3
                    if 0 <= itd < NIT:
                        j, n = seq[itd]
                        if n == 0:
                            p.op("vector", lambda E, i_=itd, j_=j: E.tensor_copy(
                                yac[:, j_ * TH:(j_ + 1) * TH], tm2[i_ % 2][:]))
                        else:
                            p.op("vector", lambda E, i_=itd, j_=j: E.tensor_tensor(
                                yac[:, j_ * TH:(j_ + 1) * TH], yac[:, j_ * TH:(j_ + 1) * TH],
                                tm2[i_ % 2][:], ALU.add))
                    p.bar()
                for j in range(NJ):
                    p.dma(al2[0][:], xcD[mi][j0 + j][:, h0:h0 + TH])
                    p.dma(al2[1][:], zD[mi][j0 + j][:, h0:h0 + TH])
                    p.bar()
                    p.op("vector", lambda E, j_=j, mi_=mi, j0_=j0: E.scalar_tensor_tensor(
                        al2[0][:], al2[0][:], dvec[:, mi_ * 8 + j0_ + j_: mi_ * 8 + j0_ + j_ + 1],
                        yac[:, j_ * TH:(j_ + 1) * TH], ALU.mult, ALU.add))
                    p.bar()
                    p.op("scalar", lambda E: E.activation(al2[1][:], al2[1][:], AF.Silu))
                    p.bar()
                    p.op("vector", lambda E: E.tensor_tensor(al2[0][:], al2[0][:], al2[1][:], ALU.mult))
                    p.bar()
                    p.dma(yO[mi][j0 + j][:, h0:h0 + TH], al2[0][:])
                    p.bar()
    return p


def run_scan2(I, Hst):
    p = build_scan2()
    nc = p.finish()
    Z1 = Hst["z1"].reshape(2048, B, T)
    Z2f = np.ascontiguousarray(np.flip(Hst["z2"].reshape(2048, B, T), axis=2))
    def al_l(pre, eh):
        a = I[f"L2_attn_{pre}_A_log"][eh * 1024:(eh + 1) * 1024]
        return np.ascontiguousarray(a.reshape(8, 128, 64).transpose(1, 0, 2).reshape(128, 512))
    def d_l(pre, eh):
        d = I[f"L2_attn_{pre}_D"][eh * 1024:(eh + 1) * 1024]
        return np.ascontiguousarray(d.reshape(8, 128).T)
    in_maps = []
    for c in range(NCORES):
        b = c // 2; eh = c % 2
        sl = slice(eh * 1024, (eh + 1) * 1024)
        in_maps.append({
            "dt1S": np.ascontiguousarray(Hst["dt1S"][sl, b]).reshape(8, 128, T),
            "dt2S": np.ascontiguousarray(Hst["dt2S"][sl, b]).reshape(8, 128, T),
            "dtx1S": np.ascontiguousarray(Hst["dtx1S"][sl, b]).reshape(8, 128, T),
            "dtx2S": np.ascontiguousarray(Hst["dtx2S"][sl, b]).reshape(8, 128, T),
            "Bm1S": np.ascontiguousarray(Hst["Bm1S"][:, b]),
            "Bm2S": np.ascontiguousarray(Hst["Bm2S"][:, b]),
            "Cm1S": np.ascontiguousarray(Hst["Cm1S"][:, b]),
            "Cm2S": np.ascontiguousarray(Hst["Cm2S"][:, b]),
            "xc1S": np.ascontiguousarray(Hst["XC1"][sl, b]).reshape(8, 128, T),
            "xc2S": np.ascontiguousarray(Hst["XC2f"][sl, b]).reshape(8, 128, T),
            "z1S": np.ascontiguousarray(Z1[sl, b]).reshape(8, 128, T),
            "z2S": np.ascontiguousarray(Z2f[sl, b]).reshape(8, 128, T),
            "Al1": al_l("m1", eh), "Al2": al_l("m2", eh),
            "D1": d_l("m1", eh), "D2": d_l("m2", eh),
        })
    res = run_bass_kernel_spmd(nc, in_maps, core_ids=list(range(NCORES)))
    p.stack.close()
    Y1 = np.empty((2048, B, T), np.float32)
    Y2f = np.empty((2048, B, T), np.float32)
    for c in range(NCORES):
        b = c // 2; eh = c % 2
        sl = slice(eh * 1024, (eh + 1) * 1024)
        Y1[sl, b] = res.results[c]["y1S"].reshape(1024, T)
        Y2f[sl, b] = res.results[c]["y2S"].reshape(1024, T)
    Hst["Y1"] = Y1
    Hst["Y2"] = np.ascontiguousarray(np.flip(Y2f, axis=2))



def build_out2():
    p = P("out2")
    y1D = p.din("y1S", (16, 128, TPC)); y2D = p.din("y2S", (16, 128, TPC))
    wo1D = p.din("Wout1bf", (2048, D), BF16); wo2D = p.din("Wout2bf", (2048, D), BF16)
    ln2wD = p.din("ln2wH", (128, DT)); ln2bD = p.din("ln2bH", (128, DT))
    xlnD = p.din("xlnT", (DT, 128, TPC))
    x2D = p.din("x2T", (DT, 128, TPC))
    xn2D = p.din("xn2bf", (DT, 128, TPC), BF16)
    tw1D = p.din("tw1bf", (D, 4096), BF16); tw2D = p.din("tw2bf", (D, 4096), BF16)
    th1D = p.din("th1bf", (4096, 4096), BF16); th2D = p.din("th2bf", (4096, 4096), BF16)
    tw3D = p.din("tw3bf", (4096, D), BF16)
    s1D = p.din("ts1H", (128, 32)); s2D = p.din("ts2H", (128, 32))
    h1D = p.din("th1H", (128, 32)); h2D = p.din("th2H", (128, 32))
    s3D = p.din("ts3H", (128, DT))
    x3O = p.dout("x3T", (DT, 128, TPC))
    xn3O = p.dout("xn3T", (DT, 128, TPC))
    xn3bO = p.dout("xn3bf", (DT, 128, TPC), BF16)
    y1V = y1D.rearrange("kt p t -> p kt t"); y2V = y2D.rearrange("kt p t -> p kt t")
    xlnV = xlnD.rearrange("kt p t -> p kt t"); x2V = x2D.rearrange("kt p t -> p kt t")
    xn2V = xn2D.rearrange("kt p t -> p kt t")
    x3V = x3O.rearrange("kt p t -> p kt t"); xn3V = xn3O.rearrange("kt p t -> p kt t")
    xn3bV = xn3bO.rearrange("kt p t -> p kt t")

    onesb = p.sb([128, 1], F32, "onesb"); onesr = p.sb([1, 128], F32, "onesr")
    p.op("gpsimd", lambda E: E.memset(onesb[:], 1.0))
    p.op("gpsimd", lambda E: E.memset(onesr[:], 1.0))
    p.bar()
    s1p = p.ps([1, CH], F32, "s1"); s2p = p.ps([1, CH], F32, "s2")
    rb = p.ps([128, CH], F32, "rb"); mb = p.ps([128, CH], F32, "mb")
    pmm = [p.ps([128, CH], F32, f"pmm{i}") for i in range(4)]
    lnst = p.sb([1, 4 * CH], F32, "lnst")
    F1 = p.sb([128, 16 * CH], F32, "F1")
    ybf = p.sb([128, 16 * CH], BF16, "ybf")
    macc = p.sb([128, DT * CH], F32, "macc")
    s2c = p.sb([128, DT * CH], F32, "s2c")
    x2c = p.sb([128, DT * CH], F32, "x2c")
    xn2c = p.sb([128, DT * CH], BF16, "xn2c")
    HA = p.sb([128, 32 * CH], BF16, "HA")
    HB = p.sb([128, 32 * CH], BF16, "HB")
    WS = p.sb([128, 8192], BF16, "WS")
    scl = p.sb([128, 32 * 4 + DT], F32, "scl")
    ln2w = p.sb([128, DT], F32, "ln2w"); ln2b = p.sb([128, DT], F32, "ln2b")
    p.dma(scl[:, 0:32], s1D[:]); p.dma(scl[:, 32:64], s2D[:])
    p.dma(scl[:, 64:96], h1D[:]); p.dma(scl[:, 96:128], h2D[:])
    p.dma(scl[:, 128:128 + DT], s3D[:])
    p.dma(ln2w[:], ln2wD[:]); p.dma(ln2b[:], ln2bD[:])
    p.bar()

    for ch in range(NCH):
        c0 = ch * CH
        for (yV, woD, first) in ((y1V, wo1D, True), (y2V, wo2D, False)):
            p.dma(F1[:].rearrange("p (kt t) -> p kt t", kt=16), yV[:, :, c0:c0 + CH])
            p.bar()
            for k in range(16):
                p.op("scalar", lambda E, k_=k: E.activation(
                    ybf[:, k_ * CH:(k_ + 1) * CH], F1[:, k_ * CH:(k_ + 1) * CH], AF.Copy))
            p.bar()
            if first:
                def evm(m, ps_):
                    p.op("scalar", lambda E, m_=m, q=ps_: E.activation(
                        macc[:, m_ * CH:(m_ + 1) * CH], q, AF.Copy))
            else:
                def evm(m, ps_):
                    p.op("vector", lambda E, m_=m, q=ps_: E.tensor_tensor(
                        macc[:, m_ * CH:(m_ + 1) * CH], macc[:, m_ * CH:(m_ + 1) * CH],
                        q, ALU.add))
            stream_mm(p, WS, woD, 16, DT, 4,
                      lambda k: ybf[:, k * CH:(k + 1) * CH], evm, pmm, CH)
        srcs = [macc[:, j * CH:(j + 1) * CH] for j in range(DT)]
        sqd = [F1[:, j * CH:(j + 1) * CH] for j in range(DT)]
        dsts = [s2c[:, j * CH:(j + 1) * CH] for j in range(DT)]
        ln_stats(p, srcs, CH, onesb, onesr, s1p[:], s2p[:], rb[:], mb[:], sqd, lnst)
        ln_apply(p, srcs, rb[:], mb[:], dsts)
        for j in range(DT):
            p.op("vector", lambda E, j_=j: E.tensor_scalar(
                s2c[:, j_ * CH:(j_ + 1) * CH], s2c[:, j_ * CH:(j_ + 1) * CH],
                ln2w[:, j_:j_ + 1], ln2b[:, j_:j_ + 1], ALU.mult, ALU.add))
        p.bar()
        p.dma(x2c[:].rearrange("p (kt t) -> p kt t", kt=DT), xlnV[:, :, c0:c0 + CH])
        p.bar()
        for j in range(DT):
            p.op("vector", lambda E, j_=j: E.tensor_tensor(
                s2c[:, j_ * CH:(j_ + 1) * CH], s2c[:, j_ * CH:(j_ + 1) * CH],
                x2c[:, j_ * CH:(j_ + 1) * CH], ALU.add))
        p.bar()
        p.dma(xn2c[:].rearrange("p (kt t) -> p kt t", kt=DT), xn2V[:, :, c0:c0 + CH])
        p.bar()
        for li, (twD, sof, act) in enumerate(((tw1D, 0, AF.Silu), (tw2D, 32, AF.Copy))):
            dst = HA if li == 0 else HB

            def evt(mm_, ps_, sof_=sof, act_=act, dst_=dst):
                p.op("scalar", lambda E, q=ps_, mm2=mm_: E.activation(
                    dst_[:, mm2 * CH:(mm2 + 1) * CH], q, act_,
                    scale=scl[:, sof_ + mm2:sof_ + mm2 + 1]))
            stream_mm(p, WS, twD, DT, 32, 4,
                      lambda k: xn2c[:, k * CH:(k + 1) * CH], evt, pmm, CH)
        for m in range(32):
            p.op("vector", lambda E, m_=m: E.tensor_tensor(
                HA[:, m_ * CH:(m_ + 1) * CH], HA[:, m_ * CH:(m_ + 1) * CH],
                HB[:, m_ * CH:(m_ + 1) * CH], ALU.mult))
        p.bar()
        for li, (thD, sof) in enumerate(((th1D, 64), (th2D, 96))):
            srcbuf = HA if li == 0 else HB
            dstbuf = HB if li == 0 else HA

            def evg(mm_, ps_, sof_=sof, dst_=dstbuf):
                p.op("scalar", lambda E, q=ps_, mm2=mm_: E.activation(
                    dst_[:, mm2 * CH:(mm2 + 1) * CH], q, AF.Gelu,
                    scale=scl[:, sof_ + mm2:sof_ + mm2 + 1]))
            stream_mm(p, WS, thD, 32, 32, 2,
                      lambda k, s_=srcbuf: s_[:, k * CH:(k + 1) * CH], evg, pmm, CH)

        def evw3(mm_, ps_):
            p.op("scalar", lambda E, q=ps_, m_=mm_: E.activation(
                macc[:, m_ * CH:(m_ + 1) * CH], q, AF.Silu,
                scale=scl[:, 128 + m_:128 + m_ + 1]))
        stream_mm(p, WS, tw3D, 32, DT, 2,
                  lambda k: HA[:, k * CH:(k + 1) * CH], evw3, pmm, CH)
        p.dma(x2c[:].rearrange("p (kt t) -> p kt t", kt=DT), x2V[:, :, c0:c0 + CH])
        p.bar()
        for j in range(DT):
            p.op("vector", lambda E, j_=j: E.tensor_tensor(
                macc[:, j_ * CH:(j_ + 1) * CH], macc[:, j_ * CH:(j_ + 1) * CH],
                s2c[:, j_ * CH:(j_ + 1) * CH], ALU.mult))
        p.bar()
        for j in range(DT):
            p.op("vector", lambda E, j_=j: E.tensor_tensor(
                x2c[:, j_ * CH:(j_ + 1) * CH], x2c[:, j_ * CH:(j_ + 1) * CH],
                macc[:, j_ * CH:(j_ + 1) * CH], ALU.add))
        p.bar()
        p.dma(x3V[:, :, c0:c0 + CH], x2c[:].rearrange("p (kt t) -> p kt t", kt=DT))
        p.bar()
        srcs = [x2c[:, j * CH:(j + 1) * CH] for j in range(DT)]
        sqd = [F1[:, j * CH:(j + 1) * CH] for j in range(DT)]
        dsts = [s2c[:, j * CH:(j + 1) * CH] for j in range(DT)]
        ln_stats(p, srcs, CH, onesb, onesr, s1p[:], s2p[:], rb[:], mb[:], sqd, lnst)
        ln_apply(p, srcs, rb[:], mb[:], dsts)
        for j in range(DT):
            p.op("scalar", lambda E, j_=j: E.activation(
                xn2c[:, j_ * CH:(j_ + 1) * CH], s2c[:, j_ * CH:(j_ + 1) * CH], AF.Copy))
        p.bar()
        p.dma(xn3V[:, :, c0:c0 + CH], s2c[:].rearrange("p (kt t) -> p kt t", kt=DT))
        p.dma(xn3bV[:, :, c0:c0 + CH], xn2c[:].rearrange("p (kt t) -> p kt t", kt=DT))
        p.bar()
    return p


def run_out2(I, Hst):
    p = build_out2()
    nc = p.finish()
    WB = Hst["WB"]
    Y1 = Hst["Y1"].reshape(2048, NTOK); Y2 = Hst["Y2"].reshape(2048, NTOK)
    def scaleH(name, nt):
        return np.ascontiguousarray(I[name].reshape(nt, 128).T)
    in_maps = []
    for c in range(NCORES):
        sl = slice(c * TPC, (c + 1) * TPC)
        in_maps.append({
            "y1S": np.ascontiguousarray(Y1[:, sl]).reshape(16, 128, TPC),
            "y2S": np.ascontiguousarray(Y2[:, sl]).reshape(16, 128, TPC),
            "Wout1bf": WB["Wout1"], "Wout2bf": WB["Wout2"],
            "ln2wH": scaleH("L2_attn_ln2_w", DT), "ln2bH": scaleH("L2_attn_ln2_b", DT),
            "xlnT": np.ascontiguousarray(Hst["xlnT"][:, sl]).reshape(DT, 128, TPC),
            "x2T": np.ascontiguousarray(Hst["x2T"][:, sl]).reshape(DT, 128, TPC),
            "xn2bf": np.ascontiguousarray(Hst["xn2bf"][:, sl]).reshape(DT, 128, TPC),
            "tw1bf": WB["tw1"], "tw2bf": WB["tw2"],
            "th1bf": WB["th1"], "th2bf": WB["th2"], "tw3bf": WB["tw3"],
            "ts1H": scaleH("L2_ffn_w1_s", 32), "ts2H": scaleH("L2_ffn_w2_s", 32),
            "th1H": scaleH("L2_ffn_h1_s", 32), "th2H": scaleH("L2_ffn_h2_s", 32),
            "ts3H": scaleH("L2_ffn_w3_s", DT),
        })
    res = run_bass_kernel_spmd(nc, in_maps, core_ids=list(range(NCORES)))
    p.stack.close()
    Hst["x3T"] = np.concatenate([r["x3T"].reshape(D, TPC) for r in res.results], axis=1)
    Hst["xn3T"] = np.concatenate([r["xn3T"].reshape(D, TPC) for r in res.results], axis=1)
    Hst["xn3bf"] = np.concatenate([r["xn3bf"].reshape(D, TPC) for r in res.results], axis=1)



def build_l3a():
    p = P("l3a")
    xnD = p.din("xn3T", (DT, 128, TPC))
    uwD = p.din("uwbf", (D, 2048), BF16); vwD = p.din("vwbf", (D, 2048), BF16)
    ubD = p.din("ubH", (128, 16)); vbD = p.din("vbH", (128, 16))
    ceD = p.din("cent", (8, D)); wrD = p.din("wrt", (D, 8))
    uO = p.dout("u3S", (16, 128, TPC)); vO = p.dout("v3S", (16, 128, TPC))
    gO = p.dout("g3g", (TPC, 8))
    xnV = xnD.rearrange("kt p t -> p kt t")
    uV = uO.rearrange("kt p t -> p kt t"); vV = vO.rearrange("kt p t -> p kt t")

    onesb = p.sb([128, 1], F32, "onesb"); onesr = p.sb([1, 128], F32, "onesr")
    iden = p.sb([128, 128], F32, "iden")
    p.op("gpsimd", lambda E: E.memset(onesb[:], 1.0))
    p.op("gpsimd", lambda E: E.memset(onesr[:], 1.0))
    p.op("gpsimd", lambda E: E.memset(iden[:], 0.0))
    p.bar()
    p.op("gpsimd", lambda E: E.affine_select(
        out=iden[:], in_=iden[:], compare_op=ALU.not_equal, fill=1.0,
        base=0, pattern=[[-1, 128]], channel_multiplier=1))
    p.bar()
    s1p = p.ps([1, CH], F32, "s1"); s2p = p.ps([1, CH], F32, "s2")
    rb = p.ps([128, CH], F32, "rb"); mb = p.ps([128, CH], F32, "mb")
    pmm = [p.ps([128, CH], F32, f"pmm{i}") for i in range(4)]
    xnc = p.sb([128, DT * CH], F32, "xnc")
    sqc = p.sb([128, DT * CH], F32, "sqc")
    xhbf = p.sb([128, DT * CH], BF16, "xhbf")
    xnormc = p.sb([128, DT * CH], F32, "xnormc")
    uvb = p.sb([128, 16 * CH], F32, "uvb")
    WS = p.sb([128, DT * 2048], BF16, "WS")
    cent = p.sb([8, D], F32, "cent_s")
    cnT = p.sb([128, DT * 8], F32, "cnT")
    wrt = p.sb([128, DT * 8], F32, "wrt_s")
    bias = p.sb([128, 32], F32, "bias")
    gbuf = p.sb([128, 8 * 3], F32, "gbuf")
    gred = p.sb([128, 4], F32, "gred")
    stt_ = p.sb([1, 4 * CH], F32, "stt")
    crow = p.sb([8, 4], F32, "crow")

    p.dma(cent[:], ceD[:])
    for k_ in range(DT):
        p.dma(wrt[:, k_ * 8:(k_ + 1) * 8], wrD[k_ * 128:(k_ + 1) * 128, :])
    p.dma(bias[:, 0:16], ubD[:]); p.dma(bias[:, 16:32], vbD[:])
    p.bar()
    sqe = p.sb([8, D], F32, "sqe")
    p.op("scalar", lambda E: E.activation(sqe[:], cent[:], AF.Square))
    p.bar()
    p.op("vector", lambda E: E.tensor_reduce(crow[:, 0:1], sqe[:], AX.X, ALU.add))
    p.bar()
    p.op("scalar", lambda E: E.activation(crow[:, 1:2], crow[:, 0:1], AF.Sqrt))
    p.bar()
    p.op("vector", lambda E: E.tensor_scalar(crow[:, 1:2], crow[:, 1:2], 1e-12, None, ALU.max))
    p.bar()
    p.op("vector", lambda E: E.reciprocal(crow[:, 2:3], crow[:, 1:2]))
    p.bar()
    p.op("vector", lambda E: E.tensor_scalar(sqe[:], cent[:], crow[:, 2:3], None, ALU.mult))
    p.bar()
    for k_ in range(DT):
        p.op("tensor", lambda E, k2=k_: E.transpose(
            pmm[0][:, 0:8], sqe[:, k2 * 128:(k2 + 1) * 128], iden[0:8, 0:8]))
        p.bar()
        p.op("scalar", lambda E, k2=k_: E.activation(
            cnT[:, k2 * 8:(k2 + 1) * 8], pmm[0][:, 0:8], AF.Copy))
        p.bar()

    for ch in range(NCH):
        c0 = ch * CH
        p.dma(xnc[:].rearrange("p (kt t) -> p kt t", kt=DT), xnV[:, :, c0:c0 + CH])
        p.bar()
        for j in range(DT):
            p.op("scalar", lambda E, j_=j: E.activation(
                sqc[:, j_ * CH:(j_ + 1) * CH], xnc[:, j_ * CH:(j_ + 1) * CH], AF.Square))
        p.bar()
        for j in range(DT):
            p.op("tensor", lambda E, j_=j: E.matmul(
                s1p[:], onesb[:, 0:1], sqc[:, j_ * CH:(j_ + 1) * CH],
                start=(j_ == 0), stop=(j_ == DT - 1)))
        p.bar()
        nr = stt_[:, 0:CH]; ar = stt_[:, CH:2 * CH]; br = stt_[:, 2 * CH:3 * CH]
        p.op("scalar", lambda E: E.activation(nr, s1p[:], AF.Sqrt))
        p.bar()
        p.op("vector", lambda E: E.tensor_scalar(ar, nr, 1.0 / 32.0, 1e-8, ALU.mult, ALU.add))
        p.op("vector", lambda E: E.tensor_scalar(br, nr, 1e-12, None, ALU.max))
        p.bar()
        p.op("vector", lambda E: E.reciprocal(ar, ar))
        p.op("vector", lambda E: E.reciprocal(br, br))
        p.bar()
        p.op("tensor", lambda E: E.matmul(rb[:], onesr[:], ar, start=True, stop=True))
        p.op("tensor", lambda E: E.matmul(mb[:], onesr[:], br, start=True, stop=True))
        p.bar()
        for j in range(DT):
            p.op("vector", lambda E, j_=j: E.tensor_tensor(
                xhbf[:, j_ * CH:(j_ + 1) * CH], xnc[:, j_ * CH:(j_ + 1) * CH], rb[:], ALU.mult))
            p.op("vector", lambda E, j_=j: E.tensor_tensor(
                xnormc[:, j_ * CH:(j_ + 1) * CH], xnc[:, j_ * CH:(j_ + 1) * CH], mb[:], ALU.mult))
        p.bar()
        for st_ in range(CH // 128):
            t0 = st_ * 128
            for kt in range(DT):
                p.op("tensor", lambda E, kt_=kt, t0_=t0: E.matmul(
                    mb[:, 0:8], xnormc[:, kt_ * CH + t0_: kt_ * CH + t0_ + 128],
                    cnT[:, kt_ * 8:(kt_ + 1) * 8],
                    start=(kt_ == 0), stop=False))
            for kt in range(DT):
                p.op("tensor", lambda E, kt_=kt, t0_=t0: E.matmul(
                    mb[:, 0:8], xnc[:, kt_ * CH + t0_: kt_ * CH + t0_ + 128],
                    wrt[:, kt_ * 8:(kt_ + 1) * 8],
                    start=False, stop=(kt_ == DT - 1)))
            p.bar()
            G = gbuf[:, 0:8]; Gm = gbuf[:, 8:16]; W_ = gbuf[:, 16:24]
            mx = gred[:, 0:1]; sm = gred[:, 1:2]; m2c = gred[:, 2:3]
            p.op("scalar", lambda E: E.activation(G, mb[:, 0:8], AF.Copy))
            p.bar()
            p.op("vector", lambda E: E.tensor_reduce(mx, G, AX.X, ALU.max))
            p.bar()
            p.op("vector", lambda E: E.tensor_scalar(Gm, G, mx, None, ALU.is_equal))
            p.bar()
            p.op("vector", lambda E: E.scalar_tensor_tensor(Gm, Gm, -1e30, G, ALU.mult, ALU.add))
            p.bar()
            p.op("vector", lambda E: E.tensor_reduce(m2c, Gm, AX.X, ALU.max))
            p.bar()
            p.op("vector", lambda E: E.tensor_scalar(Gm, G, m2c, None, ALU.is_ge))
            p.bar()
            p.op("vector", lambda E: E.tensor_scalar(mx, mx, -1.0, None, ALU.mult))
            p.bar()
            p.op("scalar", lambda E: E.activation(G, G, AF.Exp, bias=mx))
            p.bar()
            p.op("vector", lambda E: E.tensor_tensor(G, G, Gm, ALU.mult))
            p.bar()
            p.op("vector", lambda E: E.tensor_reduce(sm, G, AX.X, ALU.add))
            p.bar()
            p.op("vector", lambda E: E.reciprocal(sm, sm))
            p.bar()
            p.op("vector", lambda E: E.tensor_scalar(G, G, sm, None, ALU.mult))
            p.bar()
            p.dma(gO[c0 + t0:c0 + t0 + 128, :], G)
            p.bar()
        for pi, (wD_, outV, bof) in enumerate(((uwD, uV, 0), (vwD, vV, 16))):
            p.dma(WS[:].rearrange("p (kt m) -> p kt m", m=2048),
                  wD_.rearrange("(kt p) m -> p kt m", p=128))
            p.bar()

            def evuv(m, ps_, bof_=bof):
                p.op("scalar", lambda E, m_=m, q=ps_: E.activation(
                    uvb[:, m_ * CH:(m_ + 1) * CH], q, AF.Silu,
                    bias=bias[:, bof_ + m_:bof_ + m_ + 1]))
            emit_mm(p, WS, DT, 16, lambda k: xhbf[:, k * CH:(k + 1) * CH], CH, pmm,
                    evuv, wcols=2048)
            p.dma(outV[:, :, c0:c0 + CH], uvb[:].rearrange("p (kt t) -> p kt t", kt=16))
            p.bar()
    return p


def run_l3a(I, Hst):
    p = build_l3a()
    nc = p.finish()
    WB = Hst["WB"]
    ubH = np.ascontiguousarray(I["L3_attn_u_b"].reshape(16, 128).T)
    vbH = np.ascontiguousarray(I["L3_attn_v_b"].reshape(16, 128).T)
    in_maps = []
    for c in range(NCORES):
        sl = slice(c * TPC, (c + 1) * TPC)
        in_maps.append({
            "xn3T": np.ascontiguousarray(Hst["xn3T"][:, sl]).reshape(DT, 128, TPC),
            "uwbf": WB["uw"], "vwbf": WB["vw"],
            "ubH": ubH, "vbH": vbH,
            "cent": I["L3_ffn_centroids"],
            "wrt": np.ascontiguousarray(I["L3_ffn_w_route"].T),
        })
    res = run_bass_kernel_spmd(nc, in_maps, core_ids=list(range(NCORES)))
    p.stack.close()
    Hst["u3S"] = np.concatenate([r["u3S"].reshape(2048, TPC) for r in res.results], axis=1)
    Hst["v3S"] = np.concatenate([r["v3S"].reshape(2048, TPC) for r in res.results], axis=1)
    Hst["g3g"] = np.concatenate([r["g3g"] for r in res.results], axis=0)



def build_l3b():
    p = P("l3b")
    xnD = p.din("xn3bf", (DT, 128, TPC), BF16)
    w1D = p.din("fw1bf", (8, D, 512), BF16)
    w2D = p.din("fw2bf", (8, 512, D), BF16)
    b1D = p.din("b1H", (128, 8 * 4)); b2D = p.din("b2H", (128, 8 * DT))
    gtD = p.din("gtT", (8, TPC))
    g3O = p.dout("g3S", (DT, 128, TPC))
    xnV = xnD.rearrange("kt p t -> p kt t")
    g3V = g3O.rearrange("kt p t -> p kt t")
    onesr = p.sb([1, 128], F32, "onesr")
    p.op("gpsimd", lambda E: E.memset(onesr[:], 1.0))
    p.bar()
    pmm = [p.ps([128, CH], F32, f"pmm{i}") for i in range(4)]
    pw = p.ps([128, CH], F32, "pw")
    xnc = p.sb([128, DT * CH], BF16, "xnc")
    w1 = p.sb([128, DT * 512], BF16, "w1")
    w2 = p.sb([128, 4 * D], BF16, "w2")
    hbf = p.sb([128, 4 * CH], BF16, "hbf")
    b1 = p.sb([128, 8 * 4], F32, "b1s"); b2 = p.sb([128, 8 * DT], F32, "b2s")
    acc = p.sb([128, DT * TPC], F32, "acc")
    ytmp = p.sb([128, DT * CH], F32, "ytmp")
    wtmp = p.sb([128, DT * CH], F32, "wtmp")
    wrow = p.sb([1, CH], F32, "wrow")
    p.dma(b1[:], b1D[:]); p.dma(b2[:], b2D[:])
    p.bar()
    for ch in range(NCH):
        c0 = ch * CH
        p.dma(xnc[:].rearrange("p (kt t) -> p kt t", kt=DT), xnV[:, :, c0:c0 + CH])
        p.bar()
        for e in range(8):
            p.dma(w1[:].rearrange("p (kt m) -> p kt m", m=512),
                  w1D[e].rearrange("(kt p) m -> p kt m", p=128))
            p.dma(w2[:].rearrange("p (kt m) -> p kt m", m=D),
                  w2D[e].rearrange("(kt p) m -> p kt m", p=128))
            p.bar()

            def evh(m, ps_, e_=e):
                p.op("scalar", lambda E, m_=m, q=ps_: E.activation(
                    hbf[:, m_ * CH:(m_ + 1) * CH], q, AF.Gelu,
                    bias=b1[:, e_ * 4 + m_: e_ * 4 + m_ + 1]))
            emit_mm(p, w1, DT, 4, lambda k: xnc[:, k * CH:(k + 1) * CH], CH, pmm, evh,
                    wcols=512)

            def evy(m, ps_, e_=e):
                p.op("vector", lambda E, m_=m, q=ps_, e2=e_: E.tensor_scalar(
                    ytmp[:, m_ * CH:(m_ + 1) * CH], q, b2[:, e2 * DT + m_: e2 * DT + m_ + 1],
                    None, ALU.add))
            emit_mm(p, w2, 4, DT, lambda k: hbf[:, k * CH:(k + 1) * CH], CH, pmm, evy,
                    wcols=D)
            p.dma(wrow[:], gtD[e:e + 1, c0:c0 + CH])
            p.bar()
            p.op("tensor", lambda E: E.matmul(pw[:], onesr[:], wrow[:], start=True, stop=True))
            p.bar()
            if e == 0:
                for m in range(DT):
                    p.op("vector", lambda E, m_=m, c0_=c0: E.tensor_tensor(
                        acc[:, m_ * TPC + c0_: m_ * TPC + c0_ + CH],
                        ytmp[:, m_ * CH:(m_ + 1) * CH], pw[:], ALU.mult))
                p.bar()
            else:
                for m in range(DT):
                    p.op("vector", lambda E, m_=m: E.tensor_tensor(
                        wtmp[:, m_ * CH:(m_ + 1) * CH],
                        ytmp[:, m_ * CH:(m_ + 1) * CH], pw[:], ALU.mult))
                p.bar()
                for m in range(DT):
                    p.op("vector", lambda E, m_=m, c0_=c0: E.tensor_tensor(
                        acc[:, m_ * TPC + c0_: m_ * TPC + c0_ + CH],
                        acc[:, m_ * TPC + c0_: m_ * TPC + c0_ + CH],
                        wtmp[:, m_ * CH:(m_ + 1) * CH], ALU.add))
                p.bar()
    for m in range(DT):
        for ch in range(NCH):
            p.op("scalar", lambda E, m_=m, c_=ch: E.activation(
                acc[:, m_ * TPC + c_ * CH: m_ * TPC + (c_ + 1) * CH],
                acc[:, m_ * TPC + c_ * CH: m_ * TPC + (c_ + 1) * CH], AF.Silu))
    p.bar()
    for ch in range(NCH):
        c0 = ch * CH
        p.dma(g3V[:, :, c0:c0 + CH],
              acc[:].rearrange("p (kt t) -> p kt t", kt=DT)[:, :, c0:c0 + CH])
    p.bar()
    return p


def run_l3b(I, Hst):
    p = build_l3b()
    nc = p.finish()
    WB = Hst["WB"]
    b1H = np.ascontiguousarray(I["L3_ffn_b1"].reshape(8, 4, 128).transpose(2, 0, 1).reshape(128, 32))
    b2H = np.ascontiguousarray(I["L3_ffn_b2"].reshape(8, DT, 128).transpose(2, 0, 1).reshape(128, 8 * DT))
    gtT = np.ascontiguousarray(Hst["g3g"].T)
    in_maps = []
    for c in range(NCORES):
        sl = slice(c * TPC, (c + 1) * TPC)
        in_maps.append({
            "xn3bf": np.ascontiguousarray(Hst["xn3bf"][:, sl]).reshape(DT, 128, TPC),
            "fw1bf": WB["fw1"], "fw2bf": WB["fw2"],
            "b1H": b1H, "b2H": b2H,
            "gtT": np.ascontiguousarray(gtT[:, sl]),
        })
    res = run_bass_kernel_spmd(nc, in_maps, core_ids=list(range(NCORES)))
    p.stack.close()
    Hst["g3S"] = np.concatenate([r["g3S"].reshape(D, TPC) for r in res.results], axis=1)



def build_mix():
    p = P("mix")
    TS = T // NCORES
    uD = p.din("uS", (16, 128, B, TS)); vD = p.din("vS", (16, 128, B, TS))
    x3D = p.din("x3S", (DT, 128, B, TS)); xn3D = p.din("xn3S", (DT, 128, B, TS))
    g3D = p.din("g3Ss", (DT, 128, B, TS))
    mhD = p.din("MH", (128, 16 * 16))
    owD = p.din("owbf", (2048, D), BF16)
    obD = p.din("obH", (128, DT))
    x4O = p.dout("x4S", (DT, 128, B, TS))
    uV = uD.rearrange("kt p b t -> p kt b t"); vV = vD.rearrange("kt p b t -> p kt b t")
    x3V = x3D.rearrange("kt p b t -> p kt b t")
    xn3V = xn3D.rearrange("kt p b t -> p kt b t")
    g3V = g3D.rearrange("kt p b t -> p kt b t")
    x4V = x4O.rearrange("kt p b t -> p kt b t")

    pmm = [p.ps([128, TS], F32, f"pmm{i}") for i in range(4)]
    MH = p.sb([128, 256], F32, "MHs")
    ow = p.sb([128, 16 * D], BF16, "ows")
    obH = p.sb([128, DT], F32, "obHs")
    vt = p.sb([128, 4 * TS], F32, "vt")
    ut = p.sb([128, 4 * TS], F32, "ut")
    yb = p.sb([128, 4 * TS], F32, "yb")
    obuf = p.sb([128, 16 * 4 * TS], BF16, "obuf")
    xc1 = p.sb([128, 4 * TS], F32, "xc1")
    xc2 = p.sb([128, 4 * TS], F32, "xc2")
    xc3 = p.sb([128, 4 * TS], F32, "xc3")
    p.dma(MH[:], mhD[:]); p.dma(obH[:], obD[:])
    p.dma(ow[:].rearrange("p (kt m) -> p kt m", m=D),
          owD.rearrange("(kt p) m -> p kt m", p=128))
    p.bar()
    for jj in range(16):
        p.dma(vt[:].rearrange("p (b t) -> p b t", b=B), vV[:, jj])
        p.dma(ut[:].rearrange("p (b t) -> p b t", b=B), uV[:, jj])
        p.bar()
        for i in range(B):
            p.op("vector", lambda E, jj_=jj, i_=i: E.tensor_scalar(
                yb[:, i_ * TS:(i_ + 1) * TS], vt[:, 0:TS],
                MH[:, jj_ * 16 + i_ * 4: jj_ * 16 + i_ * 4 + 1], None, ALU.mult))
        p.bar()
        for k in range(1, B):
            for i in range(B):
                p.op("vector", lambda E, jj_=jj, i_=i, k_=k: E.scalar_tensor_tensor(
                    yb[:, i_ * TS:(i_ + 1) * TS], vt[:, k_ * TS:(k_ + 1) * TS],
                    MH[:, jj_ * 16 + i_ * 4 + k_: jj_ * 16 + i_ * 4 + k_ + 1],
                    yb[:, i_ * TS:(i_ + 1) * TS], ALU.mult, ALU.add))
            p.bar()
        for i in range(B):
            p.op("vector", lambda E, jj_=jj, i_=i: E.tensor_tensor(
                obuf[:, jj_ * 4 * TS + i_ * TS: jj_ * 4 * TS + (i_ + 1) * TS],
                ut[:, i_ * TS:(i_ + 1) * TS], yb[:, i_ * TS:(i_ + 1) * TS], ALU.mult))
        p.bar()
    for m in range(DT):
        for nb in range(4):
            for k in range(16):
                p.op("tensor", lambda E, m_=m, nb_=nb, k_=k: E.matmul(
                    pmm[nb_][:], ow[:, k_ * D + m_ * 128: k_ * D + (m_ + 1) * 128],
                    obuf[:, k_ * 4 * TS + nb_ * TS: k_ * 4 * TS + (nb_ + 1) * TS],
                    start=(k_ == 0), stop=(k_ == 15)))
        p.bar()
        p.dma(xc1[:].rearrange("p (b t) -> p b t", b=B), xn3V[:, m])
        p.dma(xc2[:].rearrange("p (b t) -> p b t", b=B), x3V[:, m])
        p.dma(xc3[:].rearrange("p (b t) -> p b t", b=B), g3V[:, m])
        p.bar()
        for nb in range(4):
            p.op("vector", lambda E, m_=m, nb_=nb: E.tensor_scalar(
                yb[:, nb_ * TS:(nb_ + 1) * TS], pmm[nb_][:], obH[:, m_:m_ + 1],
                None, ALU.add))
        p.bar()
        p.op("vector", lambda E: E.tensor_tensor(yb[:], yb[:], xc1[:], ALU.add))
        p.bar()
        p.op("vector", lambda E: E.tensor_tensor(yb[:], yb[:], xc3[:], ALU.mult))
        p.bar()
        p.op("vector", lambda E: E.tensor_tensor(yb[:], yb[:], xc2[:], ALU.add))
        p.bar()
        p.dma(x4V[:, m], yb[:].rearrange("p (b t) -> p b t", b=B))
        p.bar()
    return p


def run_mix(I, Hst):
    p = build_mix()
    nc = p.finish()
    TS = T // NCORES
    WB = Hst["WB"]
    U = Hst["u3S"].reshape(2048, B, T); V = Hst["v3S"].reshape(2048, B, T)
    X3 = Hst["x3T"].reshape(D, B, T); XN3 = Hst["xn3T"].reshape(D, B, T)
    G3 = Hst["g3S"].reshape(D, B, T)
    pos, zero, neg = I["L3_attn_pos"], I["L3_attn_zero"], I["L3_attn_neg"]
    M = np.zeros((8, 4, 4), np.float32)
    for i in range(4):
        for k_ in range(4):
            dlt = i - k_
            if dlt == 0:
                M[:, i, k_] = zero[:, 0]
            elif dlt > 0:
                M[:, i, k_] = pos[:, dlt - 1]
            else:
                M[:, i, k_] = neg[:, -dlt - 1]
    MH = np.repeat(M, 256, axis=0).reshape(16, 128, 16)
    MH = np.ascontiguousarray(MH.transpose(1, 0, 2).reshape(128, 256))
    obH = np.ascontiguousarray(I["L3_attn_o_b"].reshape(DT, 128).T)
    in_maps = []
    for c in range(NCORES):
        ts = slice(c * TS, (c + 1) * TS)
        in_maps.append({
            "uS": np.ascontiguousarray(U[:, :, ts]).reshape(16, 128, B, TS),
            "vS": np.ascontiguousarray(V[:, :, ts]).reshape(16, 128, B, TS),
            "x3S": np.ascontiguousarray(X3[:, :, ts]).reshape(DT, 128, B, TS),
            "xn3S": np.ascontiguousarray(XN3[:, :, ts]).reshape(DT, 128, B, TS),
            "g3Ss": np.ascontiguousarray(G3[:, :, ts]).reshape(DT, 128, B, TS),
            "MH": MH, "owbf": WB["ow"], "obH": obH,
        })
    res = run_bass_kernel_spmd(nc, in_maps, core_ids=list(range(NCORES)))
    p.stack.close()
    X4 = np.empty((D, B, T), np.float32)
    for c in range(NCORES):
        ts = slice(c * TS, (c + 1) * TS)
        X4[:, :, ts] = res.results[c]["x4S"].reshape(D, B, TS)
    return np.ascontiguousarray(X4.transpose(1, 2, 0))


def kernel(**I):
    I = {k: np.asarray(v, np.float32) for k, v in I.items()}
    Hst = {}
    run_prep(I, Hst)
    run_kvr(I, Hst)
    run_scan0(I, Hst)
    run_moe0(I, Hst)
    run_attn(I, Hst)
    run_ffn1a(I, Hst)
    run_ffn1b(I, Hst)
    run_conv(I, Hst)
    run_xdbl(I, Hst)
    run_scan2(I, Hst)
    run_out2(I, Hst)
    run_l3a(I, Hst)
    run_l3b(I, Hst)
    return run_mix(I, Hst)


# version 03 (final)
# speedup vs baseline: 1.0769x; 1.0769x over previous
import contextlib
import numpy as np
import ml_dtypes
import concourse.bass as bass
import concourse.mybir as mybir
from concourse.bass_utils import run_bass_kernel_spmd

F32 = mybir.dt.float32
BF16 = mybir.dt.bfloat16
AF = mybir.ActivationFunctionType
ALU = mybir.AluOpType
AX = mybir.AxisListType
BF = ml_dtypes.bfloat16

NCORES = 8
B, T, D = 4, 4096, 1024
NTOK = B * T
TPC = NTOK // NCORES
DT = D // 128
CH = 512
NCH = TPC // CH


class P:

    ENGS = ("sync", "tensor", "vector", "scalar", "gpsimd")

    def __init__(self, name="prog"):
        self.nc = bass.Bass("TRN2", target_bir_lowering=False)
        self.stack = contextlib.ExitStack()
        self.phases = [[]]
        self._uid = 0

    def _nm(self, base):
        self._uid += 1
        return f"{base}_{self._uid}"

    def din(self, name, shape, dtype=F32):
        return self.nc.dram_tensor(name, list(shape), dtype, kind="ExternalInput").ap()

    def dout(self, name, shape, dtype=F32):
        return self.nc.dram_tensor(name, list(shape), dtype, kind="ExternalOutput").ap()

    def sb(self, shape, dtype=F32, name=None):
        return self.stack.enter_context(
            self.nc.sbuf_tensor(name or self._nm("sb"), list(shape), dtype))

    def ps(self, shape, dtype=F32, name=None):
        return self.stack.enter_context(
            self.nc.psum_tensor(name or self._nm("ps"), list(shape), dtype))

    def op(self, eng, fn, inc=1):
        self.phases[-1].append((eng, fn, inc))

    def bar(self):
        if self.phases[-1]:
            self.phases.append([])

    def dma(self, out, in_, eng="sync", **kw):
        self.op(eng, lambda E: E.dma_start(out=out, in_=in_, **kw), 16)

    def finish(self):
        self.bar()
        phases = [ph for ph in self.phases if ph]
        ptot = [sum(inc for (_, _, inc) in ph) for ph in phases]
        starts = [0]
        for t_ in ptot:
            starts.append(starts[-1] + t_)
        grand = starts[-1]
        sem = self.stack.enter_context(self.nc.semaphore("gsem"))
        used = set(e for ph in phases for (e, _, _) in ph)
        used.add("sync")
        with self.nc.Block() as block:
            def mk_emitter(engname):
                def emit(E):
                    for pi, ph in enumerate(phases):
                        mine = [o for o in ph if o[0] == engname]
                        if mine:
                            if starts[pi] > 0:
                                E.wait_ge(sem, starts[pi])
                            for (_, fn, inc) in mine:
                                fn(E).then_inc(sem, inc)
                    if engname == "sync":
                        E.wait_ge(sem, grand)
                return emit
            for en in self.ENGS:
                if en in used:
                    getattr(block, en)(mk_emitter(en))
        return self.nc



def ln_stats(p, srcs, n, onesb, onesr, s1, s2, rb, mb, sq_dsts, st, eps=1e-5, nfeat=None):
    kt = len(srcs)
    nf = nfeat or (kt * 128)
    mu = st[:, 0:n]; ex2 = st[:, n:2 * n]; var = st[:, 2 * n:3 * n]; tmp = st[:, 3 * n:4 * n]
    for j in range(kt):
        p.op("scalar", lambda E, o=sq_dsts[j], i=srcs[j]: E.activation(o, i, AF.Square))
    p.bar()
    for j in range(kt):
        p.op("tensor", lambda E, i=srcs[j], j_=j: E.matmul(
            s1, onesb[:, 0:1], i, start=(j_ == 0), stop=(j_ == kt - 1)))
    for j in range(kt):
        p.op("tensor", lambda E, i=sq_dsts[j], j_=j: E.matmul(
            s2, onesb[:, 0:1], i, start=(j_ == 0), stop=(j_ == kt - 1)))
    p.bar()
    p.op("vector", lambda E: E.tensor_scalar(mu, s1, 1.0 / nf, None, ALU.mult))
    p.op("scalar", lambda E: E.activation(ex2, s2, AF.Copy, scale=1.0 / nf))
    p.bar()
    p.op("scalar", lambda E: E.activation(tmp, mu, AF.Square))
    p.bar()
    p.op("vector", lambda E: E.tensor_tensor(var, ex2, tmp, ALU.subtract))
    p.bar()
    p.op("scalar", lambda E: E.activation(tmp, var, AF.Sqrt, bias=np.float32(eps)))
    p.bar()
    p.op("vector", lambda E: E.reciprocal(var, tmp))
    p.bar()
    p.op("vector", lambda E: E.tensor_tensor(tmp, mu, var, ALU.mult))
    p.bar()
    p.op("tensor", lambda E: E.matmul(rb, onesr[:], var, start=True, stop=True))
    p.op("tensor", lambda E: E.matmul(mb, onesr[:], tmp, start=True, stop=True))
    p.bar()


def ln_apply(p, srcs, rb, mb, dsts):
    for j in range(len(srcs)):
        p.op("vector", lambda E, i=srcs[j], o=dsts[j]: E.tensor_tensor(o, i, rb, ALU.mult))
    p.bar()
    for j in range(len(dsts)):
        p.op("vector", lambda E, o=dsts[j]: E.tensor_tensor(o, o, mb, ALU.subtract))
    p.bar()


def load_weight_bf16(p, dram_ap, kt, m, stage, wbf, pieces=None):
    for k in range(kt):
        p.dma(stage[:, 0:m], dram_ap[k * 128:(k + 1) * 128, :])
        p.bar()
        p.op("scalar", lambda E, k_=k: E.activation(
            wbf[:, k_ * m:(k_ + 1) * m], stage[:, 0:m], AF.Copy))
        p.bar()


def emit_mm(p, wbf, ktiles, m_tiles, rhs_fn, n, pmm, evac_fn, wcols=None):
    wcols = wcols or (m_tiles * 128)
    ng = len(pmm)
    for mg in range(0, m_tiles, ng):
        cnt = min(ng, m_tiles - mg)
        for mi in range(cnt):
            m = mg + mi
            for k in range(ktiles):
                p.op("tensor", lambda E, m_=m, mi_=mi, k_=k: E.matmul(
                    pmm[mi_][:],
                    wbf[:, k_ * wcols + m_ * 128: k_ * wcols + (m_ + 1) * 128],
                    rhs_fn(k_),
                    start=(k_ == 0), stop=(k_ == ktiles - 1)))
        p.bar()
        for mi in range(cnt):
            evac_fn(mg + mi, pmm[mi][:])
        p.bar()


def stream_mm(p, WS, wD, ktiles, m_total, m_per, rhs_fn, ev_global, pmm, n):
    mcols = m_per * 128
    for pc in range(0, m_total, m_per):
        p.dma(WS[:, 0:ktiles * mcols].rearrange("p (kt m) -> p kt m", m=mcols),
              wD.rearrange("(kt p) m -> p kt m", p=128)[:, :, pc * 128:(pc + m_per) * 128])
        p.bar()

        def ev(m, ps_, pc_=pc):
            ev_global(pc_ + m, ps_)
        emit_mm(p, WS, ktiles, m_per, rhs_fn, n, pmm, ev, wcols=mcols)



def build_kvr():
    p = P("kvr")
    x0T = p.din("x0T", (DT, 128, TPC + 1))
    tmkD = p.din("tmk", (128, DT)); tmvD = p.din("tmv", (128, DT)); tmrD = p.din("tmr", (128, DT))
    wkD = p.din("Wk", (D, D)); wvD = p.din("Wv", (D, D)); wrD = p.din("Wr", (D, D))
    wgD = p.din("Wg", (D, 8)); bgD = p.din("bg", (1, 8))
    kcO = p.dout("kc", (DT, 128, TPC)); kvO = p.dout("kv", (DT, 128, TPC))
    srO = p.dout("sr", (DT, 128, TPC)); w0O = p.dout("w0", (TPC, 8))
    xnbfO = p.dout("xnbf", (DT, 128, TPC), BF16)

    x0Tv = x0T.rearrange("kt p t -> p kt t")
    kcV = kcO.rearrange("kt p t -> p kt t")
    kvV = kvO.rearrange("kt p t -> p kt t")
    srV = srO.rearrange("kt p t -> p kt t")
    xnbfV = xnbfO.rearrange("kt p t -> p kt t")

    onesb = p.sb([128, 1], F32, "onesb")
    onesr = p.sb([1, 128], F32, "onesr")
    p.op("gpsimd", lambda E: E.memset(onesb[:], 1.0))
    p.op("gpsimd", lambda E: E.memset(onesr[:], 1.0))
    p.bar()
    wstage = p.sb([128, D], F32, "wstage")
    lnst = p.sb([1, 4 * CH], F32, "lnst")
    wkbf = p.sb([128, DT * D], BF16, "wkbf")
    wvbf = p.sb([128, DT * D], BF16, "wvbf")
    wrbf = p.sb([128, DT * D], BF16, "wrbf")
    wgf = p.sb([128, DT * 8], F32, "wgf")
    load_weight_bf16(p, wkD, DT, D, wstage, wkbf, pieces=2)
    load_weight_bf16(p, wvD, DT, D, wstage, wvbf, pieces=2)
    load_weight_bf16(p, wrD, DT, D, wstage, wrbf, pieces=2)
    for k_ in range(DT):
        p.dma(wgf[:, k_ * 8:(k_ + 1) * 8], wgD[k_ * 128:(k_ + 1) * 128, :])
    p.bar()
    tmbuf = p.sb([128, DT * 6], F32, "tmbuf")
    for i, dsrc in enumerate((tmkD, tmvD, tmrD)):
        p.dma(tmbuf[:, i * DT:(i + 1) * DT], dsrc[:])
    bgrow = p.sb([1, 8], F32, "bgrow")
    p.dma(bgrow[:], bgD[:])
    p.bar()
    for i in range(3):
        p.op("vector", lambda E, i_=i: E.tensor_scalar(
            tmbuf[:, (3 + i_) * DT:(4 + i_) * DT], tmbuf[:, i_ * DT:(i_ + 1) * DT],
            -1.0, 1.0, ALU.mult, ALU.add))
    p.bar()

    s1 = p.ps([1, CH], F32, "s1"); s2 = p.ps([1, CH], F32, "s2")
    rb = p.ps([128, CH], F32, "rb"); mb = p.ps([128, CH], F32, "mb")
    pmm = [p.ps([128, CH], F32, f"pmm{i}") for i in range(4)]

    bgs = p.sb([128, 8], F32, "bgs")
    p.op("tensor", lambda E: E.matmul(pmm[0][:, 0:8], onesr[:], bgrow[:], start=True, stop=True))
    p.bar()
    p.op("scalar", lambda E: E.activation(bgs[:], pmm[0][:, 0:8], AF.Copy))
    p.bar()

    xh = p.sb([128, DT * CH], F32, "xh")
    sqbuf = p.sb([128, DT * CH], F32, "sqbuf")
    xn = p.sb([128, DT * CH], F32, "xn")
    xprev = p.sb([128, DT], F32, "xprev")
    xprev2 = p.sb([128, DT], F32, "xprev2")
    mixbf = p.sb([128, DT * CH], BF16, "mixbf")
    xnbf = p.sb([128, DT * CH], BF16, "xnbf_t")
    mtmp = p.sb([128, DT * CH], F32, "mtmp")
    kcb = p.sb([128, DT * CH], F32, "kcb")
    vb = p.sb([128, DT * CH], F32, "vb")
    gbuf = p.sb([128, 8 * 3], F32, "gbuf")
    gred = p.sb([128, 4], F32, "gred")

    p.dma(xh[:, 0:DT], x0Tv[:, :, 0], allow_slow_non_contiguous=True)
    p.bar()
    hsrc = [xh[:, j:j + 1] for j in range(DT)]
    hsq = [sqbuf[:, j:j + 1] for j in range(DT)]
    ln_stats(p, hsrc, 1, onesb, onesr, s1[:, 0:1], s2[:, 0:1], rb[:, 0:1], mb[:, 0:1], hsq, lnst)
    ln_apply(p, hsrc, rb[:, 0:1], mb[:, 0:1], [xprev[:, j:j + 1] for j in range(DT)])

    for ch in range(NCH):
        c0 = ch * CH
        p.dma(xh[:].rearrange("p (kt t) -> p kt t", kt=DT),
              x0Tv[:, :, 1 + c0:1 + c0 + CH])
        p.bar()
        srcs = [xh[:, j * CH:(j + 1) * CH] for j in range(DT)]
        sqd = [sqbuf[:, j * CH:(j + 1) * CH] for j in range(DT)]
        xns = [xn[:, j * CH:(j + 1) * CH] for j in range(DT)]
        ln_stats(p, srcs, CH, onesb, onesr, s1[:], s2[:], rb[:], mb[:], sqd, lnst)
        ln_apply(p, srcs, rb[:], mb[:], xns)
        for j in range(DT):
            p.op("vector", lambda E, j_=j: E.tensor_copy(
                xprev2[:, j_:j_ + 1], xn[:, j_ * CH + CH - 1: j_ * CH + CH]))
        for j in range(DT):
            p.op("scalar", lambda E, j_=j: E.activation(
                xnbf[:, j_ * CH:(j_ + 1) * CH], xn[:, j_ * CH:(j_ + 1) * CH], AF.Copy))
        p.bar()
        p.dma(xnbfV[:, :, c0:c0 + CH], xnbf[:].rearrange("p (kt t) -> p kt t", kt=DT))
        p.bar()

        for proj in range(3):
            wbf = (wkbf, wvbf, wrbf)[proj]
            for j in range(DT):
                p.op("vector", lambda E, j_=j, t_=proj: E.tensor_scalar(
                    mtmp[:, j_ * CH: j_ * CH + 1], xprev[:, j_:j_ + 1],
                    tmbuf[:, (3 + t_) * DT + j_:(3 + t_) * DT + j_ + 1], None, ALU.mult))
                p.op("vector", lambda E, j_=j, t_=proj: E.tensor_scalar(
                    mtmp[:, j_ * CH + 1:(j_ + 1) * CH],
                    xn[:, j_ * CH: j_ * CH + CH - 1],
                    tmbuf[:, (3 + t_) * DT + j_:(3 + t_) * DT + j_ + 1], None, ALU.mult))
            p.bar()
            for j in range(DT):
                p.op("vector", lambda E, j_=j, t_=proj: E.scalar_tensor_tensor(
                    mixbf[:, j_ * CH:(j_ + 1) * CH], xn[:, j_ * CH:(j_ + 1) * CH],
                    tmbuf[:, t_ * DT + j_: t_ * DT + j_ + 1],
                    mtmp[:, j_ * CH:(j_ + 1) * CH], ALU.mult, ALU.add))
            p.bar()

            if proj == 0:
                def ev(m, ps_):
                    p.op("vector", lambda E, m_=m, q=ps_: E.tensor_scalar(
                        kcb[:, m_ * CH:(m_ + 1) * CH], q[:], 60.0, None, ALU.min))
            elif proj == 1:
                def ev(m, ps_):
                    p.op("scalar", lambda E, m_=m, q=ps_: E.activation(
                        vb[:, m_ * CH:(m_ + 1) * CH], q[:], AF.Copy))
            else:
                def ev(m, ps_):
                    p.op("scalar", lambda E, m_=m, q=ps_: E.activation(
                        mtmp[:, m_ * CH:(m_ + 1) * CH], q[:], AF.Sigmoid))
            emit_mm(p, wbf, DT, DT, lambda k: mixbf[:, k * CH:(k + 1) * CH], CH, pmm, ev)

            if proj == 0:
                for j in range(DT):
                    p.op("scalar", lambda E, j_=j: E.activation(
                        kcb[:, j_ * CH:(j_ + 1) * CH], kcb[:, j_ * CH:(j_ + 1) * CH], AF.Exp))
                p.bar()
                p.dma(kcV[:, :, c0:c0 + CH], kcb[:].rearrange("p (kt t) -> p kt t", kt=DT))
                p.bar()
            elif proj == 1:
                for j in range(DT):
                    p.op("vector", lambda E, j_=j: E.tensor_tensor(
                        vb[:, j_ * CH:(j_ + 1) * CH], vb[:, j_ * CH:(j_ + 1) * CH],
                        kcb[:, j_ * CH:(j_ + 1) * CH], ALU.mult))
                p.bar()
                p.dma(kvV[:, :, c0:c0 + CH], vb[:].rearrange("p (kt t) -> p kt t", kt=DT))
                p.bar()
            else:
                p.dma(srV[:, :, c0:c0 + CH], mtmp[:].rearrange("p (kt t) -> p kt t", kt=DT))
                p.bar()

        for j in range(DT):
            p.op("vector", lambda E, j_=j: E.tensor_copy(xprev[:, j_:j_ + 1], xprev2[:, j_:j_ + 1]))
        p.bar()

        for st_ in range(CH // 128):
            t0 = st_ * 128
            for kt in range(DT):
                p.op("tensor", lambda E, kt_=kt, t0_=t0: E.matmul(
                    mb[:, 0:8], xn[:, kt_ * CH + t0_: kt_ * CH + t0_ + 128],
                    wgf[:, kt_ * 8:(kt_ + 1) * 8],
                    start=(kt_ == 0), stop=(kt_ == DT - 1)))
            p.bar()
            G = gbuf[:, 0:8]; Gm = gbuf[:, 8:16]; W_ = gbuf[:, 16:24]
            p.op("scalar", lambda E: E.activation(G, mb[:, 0:8], AF.Copy))
            p.bar()
            p.op("vector", lambda E: E.tensor_tensor(G, G, bgs[:], ALU.add))
            p.bar()
            mx = gred[:, 0:1]; sm = gred[:, 1:2]; m2c = gred[:, 2:3]
            p.op("vector", lambda E: E.tensor_reduce(mx, G, AX.X, ALU.max))
            p.bar()
            p.op("vector", lambda E: E.tensor_scalar(mx, mx, -1.0, None, ALU.mult))
            p.bar()
            p.op("scalar", lambda E: E.activation(Gm, G, AF.Exp, bias=mx))
            p.bar()
            p.op("vector", lambda E: E.tensor_reduce(sm, Gm, AX.X, ALU.add))
            p.bar()
            p.op("vector", lambda E: E.reciprocal(sm, sm))
            p.bar()
            p.op("vector", lambda E: E.tensor_scalar(G, Gm, sm, None, ALU.mult))
            p.bar()
            p.op("vector", lambda E: E.tensor_reduce(mx, G, AX.X, ALU.max))
            p.bar()
            p.op("vector", lambda E: E.tensor_scalar(Gm, G, mx, None, ALU.is_equal))
            p.bar()
            p.op("vector", lambda E: E.scalar_tensor_tensor(
                Gm, Gm, -1e30, G, ALU.mult, ALU.add))
            p.bar()
            p.op("vector", lambda E: E.tensor_reduce(m2c, Gm, AX.X, ALU.max))
            p.bar()
            p.op("vector", lambda E: E.tensor_scalar(W_, G, m2c, None, ALU.is_ge))
            p.bar()
            p.op("vector", lambda E: E.tensor_tensor(W_, W_, G, ALU.mult))
            p.op("vector", lambda E: E.tensor_tensor(sm, mx, m2c, ALU.add))
            p.bar()
            p.op("vector", lambda E: E.reciprocal(sm, sm))
            p.bar()
            p.op("vector", lambda E: E.tensor_scalar(W_, W_, sm, None, ALU.mult))
            p.bar()
            p.dma(w0O[c0 + t0:c0 + t0 + 128, :], W_)
            p.bar()
    return p


def run_kvr(I, Hst):
    x0 = I["L0_attn_xq"]
    x0T = np.ascontiguousarray(x0.reshape(NTOK, D).T)
    p = build_kvr()
    nc = p.finish()
    in_maps = []
    for c in range(NCORES):
        c0 = c * TPC
        xs = np.empty((D, TPC + 1), np.float32)
        xs[:, 1:] = x0T[:, c0:c0 + TPC]
        if c0 % T == 0:
            xs[:, 0] = 0.0
        else:
            xs[:, 0] = x0T[:, c0 - 1]
        in_maps.append({
            "x0T": xs.reshape(DT, 128, TPC + 1),
            "tmk": np.ascontiguousarray(I["L0_attn_time_mix_k"].reshape(DT, 128).T),
            "tmv": np.ascontiguousarray(I["L0_attn_time_mix_v"].reshape(DT, 128).T),
            "tmr": np.ascontiguousarray(I["L0_attn_time_mix_r"].reshape(DT, 128).T),
            "Wk": np.ascontiguousarray(I["L0_attn_Wk"].T),
            "Wv": np.ascontiguousarray(I["L0_attn_Wv"].T),
            "Wr": np.ascontiguousarray(I["L0_attn_Wr"].T),
            "Wg": np.ascontiguousarray(I["L0_ffn_Wg"]),
            "bg": I["L0_ffn_bg"].reshape(1, 8),
        })
    res = run_bass_kernel_spmd(nc, in_maps, core_ids=list(range(NCORES)))
    p.stack.close()
    Hst["kcT"] = np.concatenate([r["kc"].reshape(D, TPC) for r in res.results], axis=1)
    Hst["kvT"] = np.concatenate([r["kv"].reshape(D, TPC) for r in res.results], axis=1)
    Hst["srT"] = np.concatenate([r["sr"].reshape(D, TPC) for r in res.results], axis=1)
    Hst["w0"] = np.concatenate([r["w0"] for r in res.results], axis=0)
    Hst["xn0bf"] = np.concatenate([r["xnbf"].reshape(D, TPC) for r in res.results], axis=1)



def build_scan0():
    p = P("scan0")
    kcD = p.din("kcS", (128, B, T)); kvD = p.din("kvS", (128, B, T))
    srD = p.din("srS", (128, B, T))
    tdD = p.din("td", (128, 1)); tfD = p.din("tf", (128, 1))
    outD = p.dout("rwkv", (128, B, T))

    sc = p.sb([128, 4], F32, "sc")
    dfull = p.sb([128, T], F32, "dfull")
    kvb = p.sb([128, T], F32, "kvb")
    kcbuf = p.sb([128, T], F32, "kcbuf")
    srb = p.sb([128, T], F32, "srb")
    Akv = p.sb([128, T + 1], F32, "Akv")
    Ak = p.sb([128, T + 1], F32, "Ak")
    wkvb = p.sb([128, T], F32, "wkvb")
    wkb = p.sb([128, T], F32, "wkb")

    p.dma(sc[:, 0:1], tdD[:])
    p.dma(sc[:, 1:2], tfD[:])
    p.op("gpsimd", lambda E: E.memset(Akv[:, 0:1], 0.0))
    p.op("gpsimd", lambda E: E.memset(Ak[:, 0:1], 0.0))
    p.op("gpsimd", lambda E: E.memset(dfull[:], 1.0))
    p.bar()
    p.op("scalar", lambda E: E.activation(sc[:, 2:3], sc[:, 0:1], AF.Exp))
    p.op("scalar", lambda E: E.activation(sc[:, 1:2], sc[:, 1:2], AF.Exp))
    p.bar()
    p.op("scalar", lambda E: E.activation(sc[:, 0:1], sc[:, 2:3], AF.Exp, scale=-1.0))
    p.bar()
    p.op("vector", lambda E: E.tensor_scalar(dfull[:], dfull[:], sc[:, 0:1], None, ALU.mult))
    p.bar()
    for b in range(B):
        p.dma(kvb[:], kvD[:, b, :])
        p.dma(kcbuf[:], kcD[:, b, :])
        p.dma(srb[:], srD[:, b, :])
        p.bar()
        p.op("vector", lambda E: E.tensor_tensor_scan(
            Akv[:, 1:T + 1], dfull[:], kvb[:], 0.0, ALU.mult, ALU.add))
        p.bar()
        p.op("vector", lambda E: E.tensor_tensor_scan(
            Ak[:, 1:T + 1], dfull[:], kcbuf[:], 0.0, ALU.mult, ALU.add))
        p.bar()
        p.op("vector", lambda E: E.scalar_tensor_tensor(
            wkvb[:], kvb[:], sc[:, 1:2], Akv[:, 0:T], ALU.mult, ALU.add))
        p.bar()
        p.op("vector", lambda E: E.scalar_tensor_tensor(
            wkb[:], kcbuf[:], sc[:, 1:2], Ak[:, 0:T], ALU.mult, ALU.add))
        p.bar()
        p.op("vector", lambda E: E.tensor_scalar(wkb[:], wkb[:], 1e-8, None, ALU.add))
        p.bar()
        p.op("vector", lambda E: E.reciprocal(wkb[:], wkb[:]))
        p.bar()
        p.op("vector", lambda E: E.tensor_tensor(wkvb[:], wkvb[:], wkb[:], ALU.mult))
        p.bar()
        p.op("vector", lambda E: E.tensor_tensor(wkvb[:], wkvb[:], srb[:], ALU.mult))
        p.bar()
        p.dma(outD[:, b, :], wkvb[:])
        p.bar()
    return p


def run_scan0(I, Hst):
    KC = Hst["kcT"].reshape(D, B, T)
    KV = Hst["kvT"].reshape(D, B, T)
    SR = Hst["srT"].reshape(D, B, T)
    p = build_scan0()
    nc = p.finish()
    in_maps = []
    for c in range(NCORES):
        cs = slice(c * 128, (c + 1) * 128)
        in_maps.append({
            "kcS": np.ascontiguousarray(KC[cs]),
            "kvS": np.ascontiguousarray(KV[cs]),
            "srS": np.ascontiguousarray(SR[cs]),
            "td": np.ascontiguousarray(I["L0_attn_time_decay"][cs]),
            "tf": np.ascontiguousarray(I["L0_attn_time_first"][cs]),
        })
    res = run_bass_kernel_spmd(nc, in_maps, core_ids=list(range(NCORES)))
    p.stack.close()
    Hst["rwkvT"] = np.concatenate([r["rwkv"] for r in res.results], axis=0)


MOE0_TAIL = True



def build_moe0():
    p = P("moe0")
    rwD = p.din("rwkvT", (DT, 128, TPC))
    x0D = p.din("x0T", (DT, 128, TPC))
    xn0D = p.din("xn0bf", (DT, 128, TPC), BF16)
    w0D = p.din("w0T", (8, TPC))
    woD = p.din("WoT", (D, D))
    weD = p.din("We", (8, D, D))
    beD = p.din("beH", (128, 8 * DT))
    wqD = p.din("WqT", (D, D)); wkD = p.din("WkT", (D, D)); wvD = p.din("WvT", (D, D))
    wg1D = p.din("wg1", (D, 8))
    x1O = p.dout("x1T", (DT, 128, TPC))
    xn1O = p.dout("xn1bf", (DT, 128, TPC), BF16)
    qO = p.dout("qbf", (DT, 128, TPC), BF16)
    kO = p.dout("kbf", (DT, 128, TPC), BF16)
    vO = p.dout("vbf", (DT, 128, TPC), BF16)
    mkO = p.dout("mask", (TPC, 8))
    gsO = p.dout("gs", (TPC, 1))

    rwV = rwD.rearrange("kt p t -> p kt t")
    x0V = x0D.rearrange("kt p t -> p kt t")
    xn0V = xn0D.rearrange("kt p t -> p kt t")
    x1V = x1O.rearrange("kt p t -> p kt t")
    xn1V = xn1O.rearrange("kt p t -> p kt t")
    qV = qO.rearrange("kt p t -> p kt t")
    kV = kO.rearrange("kt p t -> p kt t")
    vV = vO.rearrange("kt p t -> p kt t")

    onesb = p.sb([128, 1], F32, "onesb")
    onesr = p.sb([1, 128], F32, "onesr")
    p.op("gpsimd", lambda E: E.memset(onesb[:], 1.0))
    p.op("gpsimd", lambda E: E.memset(onesr[:], 1.0))
    p.bar()
    s1 = p.ps([1, CH], F32, "s1"); s2 = p.ps([1, CH], F32, "s2")
    rb = p.ps([128, CH], F32, "rb"); mb = p.ps([128, CH], F32, "mb")
    pmm = [p.ps([128, CH], F32, f"pmm{i}") for i in range(4)]
    lnst = p.sb([1, 4 * CH], F32, "lnst")
    A1 = p.sb([128, DT * TPC], F32, "A1")
    A2 = p.sb([128, DT * TPC], BF16, "A2")
    A3 = p.sb([128, DT * D], BF16, "A3")
    A4 = p.sb([128, DT * CH], F32, "A4")
    A5 = p.sb([128, DT * CH], F32, "A5")
    A6 = p.sb([128, DT * CH], BF16, "A6")
    wst = p.sb([128, D], F32, "wst")
    beH = p.sb([128, 8 * DT], F32, "beHs")
    wg1 = p.sb([128, DT * 8], F32, "wg1sb")
    wrow = p.sb([1, CH], F32, "wrow")
    gbuf = p.sb([128, 8 * 3], F32, "gbuf")
    gred = p.sb([128, 4], F32, "gred")

    p.dma(beH[:], beD[:])
    for k_ in range(DT):
        p.dma(wg1[:, k_ * 8:(k_ + 1) * 8], wg1D[k_ * 128:(k_ + 1) * 128, :])
    p.dma(A2[:].rearrange("p (kt t) -> p kt t", kt=DT), xn0V[:, :, :])
    p.bar()

    for e in range(8):
        for k_ in range(DT):
            p.dma(wst[:, 0:D], weD[e, k_ * 128:(k_ + 1) * 128, :])
            p.bar()
            p.op("scalar", lambda E, k2=k_: E.activation(
                A3[:, k2 * D:(k2 + 1) * D], wst[:, 0:D], AF.Copy))
            p.bar()
        for ch in range(NCH):
            c0 = ch * CH
            p.dma(wrow[:], w0D[e:e + 1, c0:c0 + CH])
            p.bar()
            p.op("tensor", lambda E: E.matmul(rb[:], onesr[:], wrow[:], start=True, stop=True))
            p.bar()

            def ev(m, ps_, e_=e, c0_=c0):
                p.op("vector", lambda E, m_=m, q=ps_: E.tensor_scalar(
                    A4[:, m_ * CH:(m_ + 1) * CH], q, beH[:, e_ * DT + m_:e_ * DT + m_ + 1],
                    None, ALU.add))
            emit_mm(p, A3, DT, DT,
                    lambda k, c0_=c0: A2[:, k * TPC + c0_: k * TPC + c0_ + CH],
                    CH, pmm, ev, wcols=D)
            if e == 0:
                for m in range(DT):
                    p.op("vector", lambda E, m_=m, c0_=c0: E.tensor_tensor(
                        A1[:, m_ * TPC + c0_: m_ * TPC + c0_ + CH],
                        A4[:, m_ * CH:(m_ + 1) * CH], rb[:], ALU.mult))
                p.bar()
            else:
                for m in range(DT):
                    p.op("vector", lambda E, m_=m: E.tensor_tensor(
                        A5[:, m_ * CH:(m_ + 1) * CH],
                        A4[:, m_ * CH:(m_ + 1) * CH], rb[:], ALU.mult))
                p.bar()
                for m in range(DT):
                    p.op("vector", lambda E, m_=m, c0_=c0: E.tensor_tensor(
                        A1[:, m_ * TPC + c0_: m_ * TPC + c0_ + CH],
                        A1[:, m_ * TPC + c0_: m_ * TPC + c0_ + CH],
                        A5[:, m_ * CH:(m_ + 1) * CH], ALU.add))
                p.bar()
    for m in range(DT):
        for ch in range(NCH):
            p.op("scalar", lambda E, m_=m, c_=ch: E.activation(
                A1[:, m_ * TPC + c_ * CH: m_ * TPC + (c_ + 1) * CH],
                A1[:, m_ * TPC + c_ * CH: m_ * TPC + (c_ + 1) * CH], AF.Silu))
    p.bar()

    woBF = A3
    load_weight_bf16(p, woD, DT, D, wst, woBF)
    for ch in range(NCH):
        c0 = ch * CH
        p.dma(A4[:].rearrange("p (kt t) -> p kt t", kt=DT), rwV[:, :, c0:c0 + CH])
        p.bar()
        for j in range(DT):
            p.op("scalar", lambda E, j_=j: E.activation(
                A6[:, j_ * CH:(j_ + 1) * CH], A4[:, j_ * CH:(j_ + 1) * CH], AF.Copy))
        p.bar()
        p.dma(A5[:].rearrange("p (kt t) -> p kt t", kt=DT), x0V[:, :, c0:c0 + CH])
        p.bar()

        def evs0(m, ps_, c0_=c0):
            p.op("vector", lambda E, m_=m, q=ps_: E.tensor_tensor(
                A4[:, m_ * CH:(m_ + 1) * CH], q,
                A1[:, m_ * TPC + c0_: m_ * TPC + c0_ + CH], ALU.mult))
        emit_mm(p, woBF, DT, DT, lambda k: A6[:, k * CH:(k + 1) * CH], CH, pmm, evs0, wcols=D)
        for m in range(DT):
            p.op("vector", lambda E, m_=m, c0_=c0: E.tensor_tensor(
                A1[:, m_ * TPC + c0_: m_ * TPC + c0_ + CH],
                A5[:, m_ * CH:(m_ + 1) * CH], A4[:, m_ * CH:(m_ + 1) * CH], ALU.add))
        p.bar()
        p.dma(x1V[:, :, c0:c0 + CH],
              A1[:].rearrange("p (kt t) -> p kt t", kt=DT)[:, :, c0:c0 + CH])
        p.bar()
        srcs = [A1[:, j * TPC + c0: j * TPC + c0 + CH] for j in range(DT)]
        sqd = [A5[:, j * CH:(j + 1) * CH] for j in range(DT)]
        xns = [A4[:, j * CH:(j + 1) * CH] for j in range(DT)]
        ln_stats(p, srcs, CH, onesb, onesr, s1[:], s2[:], rb[:], mb[:], sqd, lnst)
        ln_apply(p, srcs, rb[:], mb[:], xns)
        for j in range(DT):
            p.op("scalar", lambda E, j_=j, c0_=c0: E.activation(
                A2[:, j_ * TPC + c0_: j_ * TPC + c0_ + CH],
                A4[:, j_ * CH:(j_ + 1) * CH], AF.Copy))
        p.bar()
        p.dma(xn1V[:, :, c0:c0 + CH],
              A2[:].rearrange("p (kt t) -> p kt t", kt=DT)[:, :, c0:c0 + CH])
        p.bar()
        for st_ in range(CH // 128):
            t0 = st_ * 128
            for kt in range(DT):
                p.op("tensor", lambda E, kt_=kt, t0_=t0: E.matmul(
                    mb[:, 0:8], A4[:, kt_ * CH + t0_: kt_ * CH + t0_ + 128],
                    wg1[:, kt_ * 8:(kt_ + 1) * 8],
                    start=(kt_ == 0), stop=(kt_ == DT - 1)))
            p.bar()
            G = gbuf[:, 0:8]; Gm = gbuf[:, 8:16]; W_ = gbuf[:, 16:24]
            mx = gred[:, 0:1]; sm = gred[:, 1:2]
            p.op("scalar", lambda E: E.activation(G, mb[:, 0:8], AF.Copy))
            p.bar()
            p.op("vector", lambda E: E.tensor_reduce(mx, G, AX.X, ALU.max))
            p.bar()
            p.op("vector", lambda E: E.tensor_scalar(Gm, G, mx, None, ALU.is_ge))
            p.bar()
            p.op("vector", lambda E: E.tensor_scalar(mx, mx, -1.0, None, ALU.mult))
            p.bar()
            p.op("scalar", lambda E: E.activation(G, G, AF.Exp, bias=mx))
            p.bar()
            p.op("vector", lambda E: E.tensor_reduce(sm, G, AX.X, ALU.add))
            p.bar()
            p.op("vector", lambda E: E.reciprocal(sm, sm))
            p.bar()
            p.op("vector", lambda E: E.tensor_scalar(G, G, sm, None, ALU.mult))
            p.bar()
            p.op("vector", lambda E: E.tensor_tensor(W_, G, Gm, ALU.mult))
            p.bar()
            p.op("vector", lambda E: E.tensor_reduce(sm, W_, AX.X, ALU.add))
            p.bar()
            p.dma(mkO[c0 + t0:c0 + t0 + 128, :], Gm)
            p.dma(gsO[c0 + t0:c0 + t0 + 128, :], sm)
            p.bar()

    for proj, (wD_, outV) in enumerate(((wqD, qV), (wkD, kV), (wvD, vV))):
        load_weight_bf16(p, wD_, DT, D, wst, A3)
        for ch in range(NCH):
            c0 = ch * CH

            def evp(m, ps_):
                p.op("scalar", lambda E, m_=m, q=ps_: E.activation(
                    A6[:, m_ * CH:(m_ + 1) * CH], q, AF.Copy))
            emit_mm(p, A3, DT, DT,
                    lambda k, c0_=c0: A2[:, k * TPC + c0_: k * TPC + c0_ + CH],
                    CH, pmm, evp, wcols=D)
            p.dma(outV[:, :, c0:c0 + CH], A6[:].rearrange("p (kt t) -> p kt t", kt=DT))
            p.bar()
    return p


def run_moe0(I, Hst):
    p = build_moe0()
    nc = p.finish()
    x0T = np.ascontiguousarray(I["L0_attn_xq"].reshape(NTOK, D).T)
    rwT = Hst["rwkvT"].reshape(D, NTOK)
    w0T = np.ascontiguousarray(Hst["w0"].T)
    be = I["L0_ffn_be"]
    beH = np.ascontiguousarray(be.reshape(8, DT, 128).transpose(2, 0, 1).reshape(128, 8 * DT))
    in_maps = []
    for c in range(NCORES):
        sl = slice(c * TPC, (c + 1) * TPC)
        in_maps.append({
            "rwkvT": np.ascontiguousarray(rwT[:, sl]).reshape(DT, 128, TPC),
            "x0T": np.ascontiguousarray(x0T[:, sl]).reshape(DT, 128, TPC),
            "xn0bf": np.ascontiguousarray(Hst["xn0bf"][:, sl]).reshape(DT, 128, TPC),
            "w0T": np.ascontiguousarray(w0T[:, sl]),
            "WoT": np.ascontiguousarray(I["L0_attn_Wo"].T),
            "We": I["L0_ffn_We"],
            "beH": beH,
            "WqT": np.ascontiguousarray(I["L1_attn_Wq"].T),
            "WkT": np.ascontiguousarray(I["L1_attn_Wk"].T),
            "WvT": np.ascontiguousarray(I["L1_attn_Wv"].T),
            "wg1": I["L1_ffn_wg"],
        })
    res = run_bass_kernel_spmd(nc, in_maps, core_ids=list(range(NCORES)))
    p.stack.close()
    Hst["x1T"] = np.concatenate([r["x1T"].reshape(D, TPC) for r in res.results], axis=1)
    Hst["xn1bf"] = np.concatenate([r["xn1bf"].reshape(D, TPC) for r in res.results], axis=1)
    Hst["qbf"] = np.concatenate([r["qbf"].reshape(D, TPC) for r in res.results], axis=1)
    Hst["kbf"] = np.concatenate([r["kbf"].reshape(D, TPC) for r in res.results], axis=1)
    Hst["vbf"] = np.concatenate([r["vbf"].reshape(D, TPC) for r in res.results], axis=1)
    Hst["mask1"] = np.concatenate([r["mask"] for r in res.results], axis=0)
    Hst["gs"] = np.concatenate([r["gs"] for r in res.results], axis=0)[:, 0]



def build_attn():
    p = P("attn")
    NP = 8
    qD = p.din("qTs", (NP, 64, T), BF16)
    kD = p.din("kTs", (NP, 64, T), BF16)
    vD = p.din("vs", (NP, T, 64), BF16)
    mD = p.din("mrow", (1, NTOK))
    gD = p.din("gsrow", (1, NTOK))
    aoO = p.dout("ao", (NP, 64, T))
    wsO = p.dout("wswr", (1, NTOK))

    onesb = p.sb([128, 1], F32, "onesb")
    onesr = p.sb([1, 128], F32, "onesr")
    ones1 = p.sb([1, T], F32, "ones1")
    onesbBF = p.sb([128, 1], BF16, "onesbBF")
    p.op("gpsimd", lambda E: E.memset(onesb[:], 1.0))
    p.op("gpsimd", lambda E: E.memset(onesr[:], 1.0))
    p.op("gpsimd", lambda E: E.memset(ones1[:], 1.0))
    p.op("gpsimd", lambda E: E.memset(onesbBF[:], 1.0))
    p.bar()

    NPieces = 4
    PW = NTOK // NPieces
    cums = [p.sb([1, PW], F32, f"cum{i}") for i in range(NPieces)]
    mrb = p.sb([1, PW], F32, "mrb")
    wout = p.sb([1, PW], F32, "wout")
    for pc in range(NPieces):
        p.dma(mrb[:], mD[:, pc * PW:(pc + 1) * PW])
        p.bar()
        init = 0.0 if pc == 0 else cums[pc - 1][:, PW - 1:PW]
        p.op("vector", lambda E, i=init, pc_=pc: E.tensor_tensor_scan(
            cums[pc_][:], ones1[:, 0:PW], mrb[:], i, ALU.mult, ALU.add))
        p.bar()
        p.op("vector", lambda E, pc_=pc: E.tensor_scalar(
            wout[:], cums[pc_][:], 2048.5, None, ALU.is_lt))
        p.bar()
        p.op("vector", lambda E: E.tensor_tensor(wout[:], wout[:], mrb[:], ALU.mult))
        p.bar()
        p.dma(mrb[:], gD[:, pc * PW:(pc + 1) * PW])
        p.bar()
        p.op("vector", lambda E: E.tensor_tensor(wout[:], wout[:], mrb[:], ALU.mult))
        p.bar()
        p.dma(wsO[:, pc * PW:(pc + 1) * PW], wout[:])
        p.bar()

    qT = p.sb([64, T], BF16, "qT")
    kT = p.sb([64, T], BF16, "kT")
    vt = p.sb([128, 32 * 64], BF16, "vt")
    EsT = p.sb([128, 32 * 512], BF16, "EsT")
    numsb = p.sb([64, 512], F32, "numsb")
    aob = p.sb([64, T], F32, "aob")
    dsb = p.sb([1, 512], F32, "dsb")
    psc = [p.ps([128, 512], F32, f"psc{i}") for i in range(4)]
    pden = p.ps([1, 512], F32, "pden")
    pnum = p.ps([64, 512], F32, "pnum")
    pdb = p.ps([64, 512], F32, "pdb")
    NST = T // 128

    for i in range(8):
        p.dma(qT[:], qD[i])
        p.dma(kT[:], kD[i])
        p.dma(vt[:].rearrange("p (tt d) -> p tt d", tt=NST), vD[i].rearrange("(tt p) d -> p tt d", p=128))
        p.bar()
        for qg in range(T // 512):
            q0 = qg * 512
            for sg in range(0, NST, 4):
                for si in range(4):
                    st = sg + si
                    p.op("tensor", lambda E, st_=st, si_=si, q0_=q0: E.matmul(
                        psc[si_][:], kT[:, st_ * 128:(st_ + 1) * 128],
                        qT[:, q0_:q0_ + 512], start=True, stop=True))
                p.bar()
                for si in range(4):
                    st = sg + si
                    p.op("scalar", lambda E, st_=st, si_=si: E.activation(
                        EsT[:, st_ * 512:(st_ + 1) * 512], psc[si_][:],
                        AF.Exp, scale=np.float32(0.125)))
                p.bar()
            for st in range(NST):
                p.op("tensor", lambda E, st_=st: E.matmul(
                    pden[:], onesbBF[:, 0:1], EsT[:, st_ * 512:(st_ + 1) * 512],
                    start=(st_ == 0), stop=(st_ == NST - 1)))
            for st in range(NST):
                p.op("tensor", lambda E, st_=st: E.matmul(
                    pnum[:], vt[:, st_ * 64:(st_ + 1) * 64],
                    EsT[:, st_ * 512:(st_ + 1) * 512],
                    start=(st_ == 0), stop=(st_ == NST - 1)))
            p.bar()
            p.op("vector", lambda E: E.tensor_scalar(dsb[:], pden[:], 1e-8, None, ALU.add))
            p.op("scalar", lambda E: E.activation(numsb[:], pnum[:], AF.Copy))
            p.bar()
            p.op("vector", lambda E: E.reciprocal(dsb[:], dsb[:]))
            p.bar()
            p.op("tensor", lambda E: E.matmul(pdb[:], onesr[:, 0:64], dsb[:], start=True, stop=True))
            p.bar()
            p.op("vector", lambda E, q0_=q0: E.tensor_tensor(
                aob[:, q0_:q0_ + 512], numsb[:], pdb[:], ALU.mult))
            p.bar()
        p.dma(aoO[i], aob[:])
        p.bar()
    return p


def run_attn(I, Hst):
    p = build_attn()
    nc = p.finish()
    Qr = Hst["qbf"].reshape(16, 64, B, T)
    Kr = Hst["kbf"].reshape(16, 64, B, T)
    Vr = Hst["vbf"].reshape(16, 64, B, T)
    m1T = np.ascontiguousarray(Hst["mask1"].T)
    gsr = np.ascontiguousarray(Hst["gs"].reshape(1, NTOK))
    in_maps = []
    for c in range(NCORES):
        qs = []; ks = []; vs = []
        for j in range(8):
            pr = c * 8 + j
            b = pr // 16; h = pr % 16
            qs.append(np.ascontiguousarray(Qr[h, :, b, :]))
            ks.append(np.ascontiguousarray(Kr[h, :, b, :]))
            vs.append(np.ascontiguousarray(Vr[h, :, b, :].T))
        in_maps.append({
            "qTs": np.stack(qs), "kTs": np.stack(ks), "vs": np.stack(vs),
            "mrow": np.ascontiguousarray(m1T[c:c + 1, :]),
            "gsrow": gsr,
        })
    res = run_bass_kernel_spmd(nc, in_maps, core_ids=list(range(NCORES)))
    p.stack.close()
    AO = np.empty((B, 16, 64, T), np.float32)
    for c in range(NCORES):
        for j in range(8):
            pr = c * 8 + j
            AO[pr // 16, pr % 16] = res.results[c]["ao"][j]
    Hst["aoT"] = np.ascontiguousarray(AO.transpose(1, 2, 0, 3).reshape(D, NTOK))
    Hst["wsw"] = np.concatenate([r["wswr"] for r in res.results], axis=0)



PREP_SPECS = None


def build_prep(specs):
    p = P("prep")
    CW = 4096
    stg = [p.sb([128, CW], F32, f"pst{i}") for i in range(2)]
    cvt = [p.sb([128, CW], BF16, f"pcv{i}") for i in range(2)]
    for name, cols in specs:
        din = p.din(name, (128, cols))
        dout = p.dout(name + "_bf", (128, cols), BF16)
        for c0 in range(0, cols, CW):
            w = min(CW, cols - c0)
            i = (c0 // CW) % 2
            p.dma(stg[i][:, 0:w], din[:, c0:c0 + w])
            p.bar()
            p.op("scalar", lambda E, i_=i, w_=w: E.activation(
                cvt[i_][:, 0:w_], stg[i_][:, 0:w_], AF.Copy))
            p.bar()
            p.dma(dout[:, c0:c0 + w], cvt[i][:, 0:w])
            p.bar()
    return p


def run_prep(I, Hst):
    raw = {
        "Wk0": I["L0_attn_Wk"].T, "Wv0": I["L0_attn_Wv"].T, "Wr0": I["L0_attn_Wr"].T,
        "Wo0": I["L0_attn_Wo"].T, "We0": I["L0_ffn_We"],
        "Wq1": I["L1_attn_Wq"].T, "Wk1": I["L1_attn_Wk"].T, "Wv1": I["L1_attn_Wv"].T,
        "Wo1": I["L1_attn_Wo"].T,
        "w1s": I["L1_ffn_w1"], "w2s": I["L1_ffn_w2"],
        "Win1": I["L2_attn_m1_W_in"].T, "Win2": I["L2_attn_m2_W_in"].T,
        "Wxp1": I["L2_attn_m1_W_xp"].T, "Wxp2": I["L2_attn_m2_W_xp"].T,
        "Wdt1": I["L2_attn_m1_W_dt"].T, "Wdt2": I["L2_attn_m2_W_dt"].T,
        "Wout1": I["L2_attn_m1_W_out"].T, "Wout2": I["L2_attn_m2_W_out"].T,
        "tw1": I["L2_ffn_w1_w"].T, "tw2": I["L2_ffn_w2_w"].T,
        "th1": I["L2_ffn_h1_w"].T, "th2": I["L2_ffn_h2_w"].T,
        "tw3": I["L2_ffn_w3_w"].T,
        "uw": I["L3_attn_u_w"].T, "vw": I["L3_attn_v_w"].T, "ow": I["L3_attn_o_w"].T,
        "fw1": I["L3_ffn_w1"], "fw2": I["L3_ffn_w2"],
    }
    shapes = {}
    specs = []
    for k, v in raw.items():
        v = np.ascontiguousarray(v, np.float32)
        shapes[k] = v.shape
        n = v.size
        assert n % (8 * 128) == 0, (k, v.shape)
        specs.append((k, n // (8 * 128)))
    p = build_prep(specs)
    nc = p.finish()
    in_maps = []
    for c in range(NCORES):
        m = {}
        for k, v in raw.items():
            vv = np.ascontiguousarray(v, np.float32).reshape(8, -1)
            m[k] = vv[c].reshape(128, -1)
        in_maps.append(m)
    res = run_bass_kernel_spmd(nc, in_maps, core_ids=list(range(NCORES)))
    p.stack.close()
    WB = {}
    for k in raw:
        parts = [res.results[c][k + "_bf"].reshape(-1) for c in range(NCORES)]
        WB[k] = np.concatenate(parts).reshape(shapes[k])
    Hst["WB"] = WB



def build_ffn1a():
    p = P("ffn1a")
    xnD = p.din("xn1bf", (DT, 128, TPC), BF16)
    w1D = p.din("w1bf", (8, D, 4096), BF16)
    w2D = p.din("w2bf", (8, 4096, D), BF16)
    wsD = p.din("wswT", (8, TPC))
    g1O = p.dout("g1", (DT, 128, TPC))
    xnV = xnD.rearrange("kt p t -> p kt t")
    g1V = g1O.rearrange("kt p t -> p kt t")

    onesr = p.sb([1, 128], F32, "onesr")
    p.op("gpsimd", lambda E: E.memset(onesr[:], 1.0))
    p.bar()
    acc = p.sb([128, DT * TPC], F32, "acc")
    wA = p.sb([128, DT * 2048], BF16, "wA")
    wB = p.sb([128, 16 * D], BF16, "wB")
    hbuf = p.sb([128, 32 * CH], BF16, "hbuf")
    xnc = p.sb([128, DT * CH], BF16, "xnc")
    ytmp = p.sb([128, DT * CH], F32, "ytmp")
    wtmp = p.sb([128, DT * CH], F32, "wtmp")
    wrow = p.sb([1, CH], F32, "wrow")
    pm = [p.ps([128, CH], F32, f"pm{i}") for i in range(8)]

    for ch in range(NCH):
        c0 = ch * CH
        p.dma(xnc[:].rearrange("p (kt t) -> p kt t", kt=DT), xnV[:, :, c0:c0 + CH])
        p.bar()
        for e in range(8):
            for mh in range(2):
                p.dma(wA[:].rearrange("p (kt m) -> p kt m", m=2048),
                      w1D[e].rearrange("(kt p) m -> p kt m", p=128)[:, :, mh * 2048:(mh + 1) * 2048])
                p.bar()
                for mg in range(2):
                    for mi in range(8):
                        m = mg * 8 + mi
                        for k in range(DT):
                            p.op("tensor", lambda E, m_=m, mi_=mi, k_=k: E.matmul(
                                pm[mi_][:], wA[:, k_ * 2048 + m_ * 128: k_ * 2048 + (m_ + 1) * 128],
                                xnc[:, k_ * CH:(k_ + 1) * CH],
                                start=(k_ == 0), stop=(k_ == DT - 1)))
                    p.bar()
                    for mi in range(8):
                        m = mh * 16 + mg * 8 + mi
                        p.op("scalar", lambda E, m_=m, mi_=mi: E.activation(
                            hbuf[:, m_ * CH:(m_ + 1) * CH], pm[mi_][:], AF.Relu))
                    p.bar()
            for kh in range(2):
                p.dma(wB[:].rearrange("p (kt m) -> p kt m", m=D),
                      w2D[e].rearrange("(khh kt p) m -> khh p kt m", khh=2, p=128)[kh])
                p.bar()
                for m in range(DT):
                    for k in range(16):
                        kg = kh * 16 + k
                        p.op("tensor", lambda E, m_=m, k_=k, kg_=kg, kh_=kh: E.matmul(
                            pm[m_][:], wB[:, k_ * D + m_ * 128: k_ * D + (m_ + 1) * 128],
                            hbuf[:, kg_ * CH:(kg_ + 1) * CH],
                            start=(kh_ == 0 and k_ == 0), stop=(kh_ == 1 and k_ == 15)))
                p.bar()
            p.dma(wrow[:], wsD[e:e + 1, c0:c0 + CH])
            p.bar()
            pb = p.ps([128, CH], F32, "pbr") if False else None
            for m in range(DT):
                p.op("scalar", lambda E, m_=m: E.activation(
                    ytmp[:, m_ * CH:(m_ + 1) * CH], pm[m_][:], AF.Copy))
            p.bar()
            p.op("tensor", lambda E: E.matmul(pm[0][:], onesr[:], wrow[:], start=True, stop=True))
            p.bar()
            if e == 0:
                for m in range(DT):
                    p.op("vector", lambda E, m_=m, c0_=c0: E.tensor_tensor(
                        acc[:, m_ * TPC + c0_: m_ * TPC + c0_ + CH],
                        ytmp[:, m_ * CH:(m_ + 1) * CH], pm[0][:], ALU.mult))
                p.bar()
            else:
                for m in range(DT):
                    p.op("vector", lambda E, m_=m: E.tensor_tensor(
                        wtmp[:, m_ * CH:(m_ + 1) * CH],
                        ytmp[:, m_ * CH:(m_ + 1) * CH], pm[0][:], ALU.mult))
                p.bar()
                for m in range(DT):
                    p.op("vector", lambda E, m_=m, c0_=c0: E.tensor_tensor(
                        acc[:, m_ * TPC + c0_: m_ * TPC + c0_ + CH],
                        acc[:, m_ * TPC + c0_: m_ * TPC + c0_ + CH],
                        wtmp[:, m_ * CH:(m_ + 1) * CH], ALU.add))
                p.bar()
    for m in range(DT):
        for ch in range(NCH):
            p.op("scalar", lambda E, m_=m, c_=ch: E.activation(
                acc[:, m_ * TPC + c_ * CH: m_ * TPC + (c_ + 1) * CH],
                acc[:, m_ * TPC + c_ * CH: m_ * TPC + (c_ + 1) * CH], AF.Silu))
    p.bar()
    for ch in range(NCH):
        c0 = ch * CH
        p.dma(g1V[:, :, c0:c0 + CH],
              acc[:].rearrange("p (kt t) -> p kt t", kt=DT)[:, :, c0:c0 + CH])
    p.bar()
    return p


def run_ffn1a(I, Hst):
    p = build_ffn1a()
    nc = p.finish()
    wswT = Hst["wsw"]
    in_maps = []
    for c in range(NCORES):
        sl = slice(c * TPC, (c + 1) * TPC)
        in_maps.append({
            "xn1bf": np.ascontiguousarray(Hst["xn1bf"][:, sl]).reshape(DT, 128, TPC),
            "w1bf": Hst["WB"]["w1s"],
            "w2bf": Hst["WB"]["w2s"],
            "wswT": np.ascontiguousarray(wswT[:, sl]),
        })
    res = run_bass_kernel_spmd(nc, in_maps, core_ids=list(range(NCORES)))
    p.stack.close()
    Hst["g1T"] = np.concatenate([r["g1"].reshape(D, TPC) for r in res.results], axis=1)



def build_ffn1b():
    p = P("ffn1b")
    aoD = p.din("aoT", (DT, 128, TPC))
    woD = p.din("Wo1bf", (D, D), BF16)
    boD = p.din("boH", (128, DT))
    x1D = p.din("x1T", (DT, 128, TPC))
    g1D = p.din("g1T", (DT, 128, TPC))
    lnwD = p.din("ln1wH", (128, DT)); lnbD = p.din("ln1bH", (128, DT))
    win1D = p.din("Win1bf", (D, 4096), BF16)
    win2D = p.din("Win2bf", (D, 4096), BF16)
    x2O = p.dout("x2T", (DT, 128, TPC))
    xn2O = p.dout("xn2bf", (DT, 128, TPC), BF16)
    xlnO = p.dout("xlnT", (DT, 128, TPC))
    xpO = [p.dout(nm, (16, 128, TPC)) for nm in ("xp1", "z1", "xp2", "z2")]
    aoV = aoD.rearrange("kt p t -> p kt t")
    x1V = x1D.rearrange("kt p t -> p kt t")
    g1V = g1D.rearrange("kt p t -> p kt t")
    x2V = x2O.rearrange("kt p t -> p kt t")
    xn2V = xn2O.rearrange("kt p t -> p kt t")
    xlnV = xlnO.rearrange("kt p t -> p kt t")
    xpV = [o.rearrange("kt p t -> p kt t") for o in xpO]

    onesb = p.sb([128, 1], F32, "onesb")
    onesr = p.sb([1, 128], F32, "onesr")
    p.op("gpsimd", lambda E: E.memset(onesb[:], 1.0))
    p.op("gpsimd", lambda E: E.memset(onesr[:], 1.0))
    p.bar()
    s1p = p.ps([1, CH], F32, "s1"); s2p = p.ps([1, CH], F32, "s2")
    rb = p.ps([128, CH], F32, "rb"); mb = p.ps([128, CH], F32, "mb")
    pmm = [p.ps([128, CH], F32, f"pmm{i}") for i in range(4)]
    lnst = p.sb([1, 4 * CH], F32, "lnst")
    wo = p.sb([128, DT * D], BF16, "wo")
    boH = p.sb([128, DT], F32, "boHs")
    lnw = p.sb([128, DT], F32, "lnw"); lnb = p.sb([128, DT], F32, "lnbs")
    aoc = p.sb([128, DT * CH], F32, "aoc")
    aocb = p.sb([128, DT * CH], BF16, "aocb")
    x1c = p.sb([128, DT * CH], F32, "x1c")
    sqc = p.sb([128, DT * CH], F32, "sqc")
    xnc = p.sb([128, DT * CH], F32, "xnc")
    xlc = p.sb([128, DT * CH], F32, "xlc")
    xlbf = p.sb([128, DT * TPC], BF16, "xlbf")
    xzc = p.sb([128, 8 * CH], F32, "xzc")
    p.dma(boH[:], boD[:]); p.dma(lnw[:], lnwD[:]); p.dma(lnb[:], lnbD[:])
    for k_ in range(DT):
        p.dma(wo[:, k_ * D:(k_ + 1) * D], woD[k_ * 128:(k_ + 1) * 128, :])
    p.bar()

    for ch in range(NCH):
        c0 = ch * CH
        p.dma(aoc[:].rearrange("p (kt t) -> p kt t", kt=DT), aoV[:, :, c0:c0 + CH])
        p.dma(x1c[:].rearrange("p (kt t) -> p kt t", kt=DT), x1V[:, :, c0:c0 + CH])
        p.bar()
        for j in range(DT):
            p.op("scalar", lambda E, j_=j: E.activation(
                aocb[:, j_ * CH:(j_ + 1) * CH], aoc[:, j_ * CH:(j_ + 1) * CH], AF.Copy))
        p.bar()

        def evs1(m, ps_):
            p.op("vector", lambda E, m_=m, q=ps_: E.tensor_scalar(
                sqc[:, m_ * CH:(m_ + 1) * CH], q, boH[:, m_:m_ + 1], None, ALU.add))
        emit_mm(p, wo, DT, DT, lambda k: aocb[:, k * CH:(k + 1) * CH], CH, pmm, evs1, wcols=D)
        p.dma(xnc[:].rearrange("p (kt t) -> p kt t", kt=DT), g1V[:, :, c0:c0 + CH])
        p.bar()
        for j in range(DT):
            p.op("vector", lambda E, j_=j: E.tensor_tensor(
                sqc[:, j_ * CH:(j_ + 1) * CH], sqc[:, j_ * CH:(j_ + 1) * CH],
                xnc[:, j_ * CH:(j_ + 1) * CH], ALU.mult))
        p.bar()
        for j in range(DT):
            p.op("vector", lambda E, j_=j: E.tensor_tensor(
                x1c[:, j_ * CH:(j_ + 1) * CH], x1c[:, j_ * CH:(j_ + 1) * CH],
                sqc[:, j_ * CH:(j_ + 1) * CH], ALU.add))
        p.bar()
        p.dma(x2V[:, :, c0:c0 + CH], x1c[:].rearrange("p (kt t) -> p kt t", kt=DT))
        p.bar()
        srcs = [x1c[:, j * CH:(j + 1) * CH] for j in range(DT)]
        sqd = [sqc[:, j * CH:(j + 1) * CH] for j in range(DT)]
        xns = [xnc[:, j * CH:(j + 1) * CH] for j in range(DT)]
        ln_stats(p, srcs, CH, onesb, onesr, s1p[:], s2p[:], rb[:], mb[:], sqd, lnst)
        ln_apply(p, srcs, rb[:], mb[:], xns)
        for j in range(DT):
            p.op("scalar", lambda E, j_=j: E.activation(
                aocb[:, j_ * CH:(j_ + 1) * CH], xnc[:, j_ * CH:(j_ + 1) * CH], AF.Copy))
        p.bar()
        p.dma(xn2V[:, :, c0:c0 + CH], aocb[:].rearrange("p (kt t) -> p kt t", kt=DT))
        p.bar()
        srcs2 = [xnc[:, j * CH:(j + 1) * CH] for j in range(DT)]
        xls = [xlc[:, j * CH:(j + 1) * CH] for j in range(DT)]
        ln_stats(p, srcs2, CH, onesb, onesr, s1p[:], s2p[:], rb[:], mb[:], sqd, lnst)
        ln_apply(p, srcs2, rb[:], mb[:], xls)
        for j in range(DT):
            p.op("vector", lambda E, j_=j: E.tensor_scalar(
                xlc[:, j_ * CH:(j_ + 1) * CH], xlc[:, j_ * CH:(j_ + 1) * CH],
                lnw[:, j_:j_ + 1], lnb[:, j_:j_ + 1], ALU.mult, ALU.add))
        p.bar()
        p.dma(xlnV[:, :, c0:c0 + CH], xlc[:].rearrange("p (kt t) -> p kt t", kt=DT))
        for j in range(DT):
            p.op("scalar", lambda E, j_=j, c0_=c0: E.activation(
                xlbf[:, j_ * TPC + c0_: j_ * TPC + c0_ + CH],
                xlc[:, j_ * CH:(j_ + 1) * CH], AF.Copy))
        p.bar()

    wih = p.sb([128, DT * 2048], BF16, "wih")
    for mi_, winD in enumerate((win1D, win2D)):
        for mh in range(2):
            p.dma(wih[:].rearrange("p (kt m) -> p kt m", m=2048),
                  winD.rearrange("(kt p) m -> p kt m", p=128)[:, :, mh * 2048:(mh + 1) * 2048])
            p.bar()
            outV = xpV[mi_ * 2] if mh == 0 else xpV[mi_ * 2 + 1]
            for ch in range(NCH):
                c0 = ch * CH
                for mg8 in range(2):

                    def evp(m, ps_):
                        p.op("scalar", lambda E, m_=m, q=ps_: E.activation(
                            xzc[:, m_ * CH:(m_ + 1) * CH], q, AF.Copy))
                    emit_mm(p, wih[:, mg8 * 1024:], DT, 8,
                            lambda k, c0_=c0: xlbf[:, k * TPC + c0_: k * TPC + c0_ + CH],
                            CH, pmm, evp, wcols=2048)
                    p.dma(outV[:, mg8 * 8:(mg8 + 1) * 8, c0:c0 + CH],
                          xzc[:].rearrange("p (kt t) -> p kt t", kt=8))
                    p.bar()
    return p


def run_ffn1b(I, Hst):
    p = build_ffn1b()
    nc = p.finish()
    WB = Hst["WB"]
    boH = np.ascontiguousarray(I["L1_attn_bo"].reshape(DT, 128).T)
    lnw = np.ascontiguousarray(I["L2_attn_ln1_w"].reshape(DT, 128).T)
    lnb = np.ascontiguousarray(I["L2_attn_ln1_b"].reshape(DT, 128).T)
    in_maps = []
    for c in range(NCORES):
        sl = slice(c * TPC, (c + 1) * TPC)
        in_maps.append({
            "aoT": np.ascontiguousarray(Hst["aoT"][:, sl]).reshape(DT, 128, TPC),
            "Wo1bf": WB["Wo1"],
            "boH": boH,
            "x1T": np.ascontiguousarray(Hst["x1T"][:, sl]).reshape(DT, 128, TPC),
            "g1T": np.ascontiguousarray(Hst["g1T"][:, sl]).reshape(DT, 128, TPC),
            "ln1wH": lnw, "ln1bH": lnb,
            "Win1bf": WB["Win1"], "Win2bf": WB["Win2"],
        })
    res = run_bass_kernel_spmd(nc, in_maps, core_ids=list(range(NCORES)))
    p.stack.close()
    for nm, key in (("x2T", "x2T"), ("xlnT", "xlnT")):
        Hst[key] = np.concatenate([r[nm].reshape(D, TPC) for r in res.results], axis=1)
    Hst["xn2bf"] = np.concatenate([r["xn2bf"].reshape(D, TPC) for r in res.results], axis=1)
    for nm in ("xp1", "z1", "xp2", "z2"):
        Hst[nm] = np.concatenate([r[nm].reshape(2048, TPC) for r in res.results], axis=1)



def build_conv():
    p = P("conv")
    xps = [p.din("xp1S", (8, 128, T)), p.din("xp2S", (8, 128, T))]
    cws = [p.din("cw1", (128, 8 * 4)), p.din("cw2", (128, 8 * 4))]
    cbs = [p.din("cb1", (128, 8)), p.din("cb2", (128, 8))]
    outs = [p.dout("xc1S", (8, 128, T)), p.dout("xc2S", (8, 128, T))]
    xpb = p.sb([128, T + 3], F32, "xpb")
    accA = p.sb([128, T], F32, "accA")
    accB = p.sb([128, T], F32, "accB")
    xcb = p.sb([128, T], F32, "xcb")
    cw = p.sb([128, 8 * 4 * 2], F32, "cw")
    cb = p.sb([128, 8 * 2], F32, "cb")
    p.dma(cw[:, 0:32], cws[0][:]); p.dma(cw[:, 32:64], cws[1][:])
    p.dma(cb[:, 0:8], cbs[0][:]); p.dma(cb[:, 8:16], cbs[1][:])
    p.op("gpsimd", lambda E: E.memset(xpb[:, 0:3], 0.0))
    p.bar()
    for mi in range(2):
        for j in range(8):
            p.dma(xpb[:, 3:T + 3], xps[mi][j])
            p.bar()
            cwo = mi * 32 + j * 4
            p.op("vector", lambda E, c=cwo: E.tensor_scalar(
                accA[:], xpb[:, 0:T], cw[:, c:c + 1], None, ALU.mult))
            p.bar()
            p.op("vector", lambda E, c=cwo: E.scalar_tensor_tensor(
                accB[:], xpb[:, 1:T + 1], cw[:, c + 1:c + 2], accA[:], ALU.mult, ALU.add))
            p.bar()
            p.op("vector", lambda E, c=cwo: E.scalar_tensor_tensor(
                accA[:], xpb[:, 2:T + 2], cw[:, c + 2:c + 3], accB[:], ALU.mult, ALU.add))
            p.bar()
            p.op("vector", lambda E, c=cwo: E.scalar_tensor_tensor(
                accB[:], xpb[:, 3:T + 3], cw[:, c + 3:c + 4], accA[:], ALU.mult, ALU.add))
            p.bar()
            p.op("scalar", lambda E, mi_=mi, j_=j: E.activation(
                xcb[:], accB[:], AF.Silu, bias=cb[:, mi_ * 8 + j_: mi_ * 8 + j_ + 1]))
            p.bar()
            p.dma(outs[mi][j], xcb[:])
            p.bar()
    return p


def run_conv(I, Hst):
    p = build_conv()
    nc = p.finish()
    XP1 = Hst["xp1"].reshape(2048, B, T)
    XP2f = np.ascontiguousarray(np.flip(Hst["xp2"].reshape(2048, B, T), axis=2))
    def cwl(pre):
        c = I[f"L2_attn_{pre}_conv_w"]
        return c.reshape(2, 8, 128, 4)
    def cbl(pre):
        c = I[f"L2_attn_{pre}_conv_b"]
        return c.reshape(2, 8, 128)
    cw1 = cwl("m1"); cw2 = cwl("m2"); cb1 = cbl("m1"); cb2 = cbl("m2")
    in_maps = []
    for c in range(NCORES):
        b = c // 2; eh = c % 2
        sl = slice(eh * 1024, (eh + 1) * 1024)
        in_maps.append({
            "xp1S": np.ascontiguousarray(XP1[sl, b]).reshape(8, 128, T),
            "xp2S": np.ascontiguousarray(XP2f[sl, b]).reshape(8, 128, T),
            "cw1": np.ascontiguousarray(cw1[eh].transpose(1, 0, 2).reshape(128, 32)),
            "cw2": np.ascontiguousarray(cw2[eh].transpose(1, 0, 2).reshape(128, 32)),
            "cb1": np.ascontiguousarray(cb1[eh].transpose(1, 0)),
            "cb2": np.ascontiguousarray(cb2[eh].transpose(1, 0)),
        })
    res = run_bass_kernel_spmd(nc, in_maps, core_ids=list(range(NCORES)))
    p.stack.close()
    XC1 = np.empty((2048, B, T), np.float32)
    XC2f = np.empty((2048, B, T), np.float32)
    for c in range(NCORES):
        b = c // 2; eh = c % 2
        sl = slice(eh * 1024, (eh + 1) * 1024)
        XC1[sl, b] = res.results[c]["xc1S"].reshape(1024, T)
        XC2f[sl, b] = res.results[c]["xc2S"].reshape(1024, T)
    Hst["XC1"] = XC1
    Hst["XC2f"] = XC2f



def build_xdbl():
    p = P("xdbl")
    TH = T // 2
    xcD = [p.din("xc1S", (16, 128, TH)), p.din("xc2S", (16, 128, TH))]
    wxD = [p.din("Wxp1bf", (2048, 192), BF16), p.din("Wxp2bf", (2048, 192), BF16)]
    wdD = [p.din("Wdt1bf", (64, 2048), BF16), p.din("Wdt2bf", (64, 2048), BF16)]
    dtbD = [p.din("dtb1", (128, 16)), p.din("dtb2", (128, 16))]
    dtO = [p.dout("dt1S", (16, 128, TH)), p.dout("dt2S", (16, 128, TH))]
    dtxO = [p.dout("dtx1S", (16, 128, TH), BF16), p.dout("dtx2S", (16, 128, TH), BF16)]
    bmO = [p.dout("Bm1S", (64, TH), BF16), p.dout("Bm2S", (64, TH), BF16)]
    cmO = [p.dout("Cm1S", (64, TH)), p.dout("Cm2S", (64, TH))]

    wxp = p.sb([128, 16 * 192], BF16, "wxp")
    wdt = p.sb([64, 2048], BF16, "wdt")
    dtb = p.sb([128, 16], F32, "dtb")
    xcc = p.sb([128, 16 * CH], F32, "xcc")
    xcbf = p.sb([128, 16 * CH], BF16, "xcbf")
    dtc = p.sb([128, 16 * CH], F32, "dtc")
    dtxc = p.sb([128, 16 * CH], BF16, "dtxc")
    dlo = p.sb([64, CH], BF16, "dlo")
    bmc = p.sb([64, CH], BF16, "bmc")
    cmc = p.sb([64, CH], F32, "cmc")
    t1 = p.sb([128, CH], F32, "t1")
    t2 = p.sb([128, CH], F32, "t2")
    psA = p.ps([128, CH], F32, "psA")
    psB = p.ps([128, CH], F32, "psB")
    pm = [p.ps([128, CH], F32, f"pm{i}") for i in range(4)]

    for mi in range(2):
        p.dma(wxp[:].rearrange("p (kt m) -> p kt m", m=192),
              wxD[mi].rearrange("(kt p) m -> p kt m", p=128))
        p.dma(wdt[:], wdD[mi][:])
        p.dma(dtb[:], dtbD[mi][:])
        p.bar()
        for ch in range(4):
            c0 = ch * CH
            p.dma(xcc[:].rearrange("p (kt t) -> p kt t", kt=16), xcD[mi].rearrange("kt p t -> p kt t")[:, :, c0:c0 + CH])
            p.bar()
            for k in range(16):
                p.op("scalar", lambda E, k_=k: E.activation(
                    xcbf[:, k_ * CH:(k_ + 1) * CH], xcc[:, k_ * CH:(k_ + 1) * CH], AF.Copy))
            p.bar()
            for k in range(16):
                p.op("tensor", lambda E, k_=k: E.matmul(
                    psA[:], wxp[:, k_ * 192: k_ * 192 + 128],
                    xcbf[:, k_ * CH:(k_ + 1) * CH],
                    start=(k_ == 0), stop=(k_ == 15)))
            for k in range(16):
                p.op("tensor", lambda E, k_=k: E.matmul(
                    psB[0:64, :], wxp[:, k_ * 192 + 128: k_ * 192 + 192],
                    xcbf[:, k_ * CH:(k_ + 1) * CH],
                    start=(k_ == 0), stop=(k_ == 15)))
            p.bar()
            p.op("scalar", lambda E: E.activation(dlo[:], psA[0:64, :], AF.Copy))
            p.op("scalar", lambda E: E.activation(bmc[:], psA[64:128, :], AF.Copy))
            p.op("scalar", lambda E: E.activation(cmc[:], psB[0:64, :], AF.Copy))
            p.bar()
            p.dma(bmO[mi][:, c0:c0 + CH], bmc[:])
            p.dma(cmO[mi][:, c0:c0 + CH], cmc[:])
            for mg in range(4):
                for mi2 in range(4):
                    m = mg * 4 + mi2
                    p.op("tensor", lambda E, m_=m, mi2_=mi2: E.matmul(
                        pm[mi2_][:], wdt[:, m_ * 128:(m_ + 1) * 128], dlo[:],
                        start=True, stop=True))
                p.bar()
                for mi2 in range(4):
                    m = mg * 4 + mi2
                    p.op("vector", lambda E, m_=m, mi2_=mi2: E.tensor_scalar(
                        t1[:], pm[mi2_][:], dtb[:, m_:m_ + 1], None, ALU.add))
                    p.bar()
                    p.op("scalar", lambda E: E.activation(t2[:], t1[:], AF.Exp))
                    p.bar()
                    p.op("scalar", lambda E, m_=m: E.activation(
                        dtc[:, m_ * CH:(m_ + 1) * CH], t2[:], AF.Ln, bias=np.float32(1.0)))
                    p.bar()
            for k in range(16):
                p.op("vector", lambda E, k_=k: E.tensor_tensor(
                    dtxc[:, k_ * CH:(k_ + 1) * CH], dtc[:, k_ * CH:(k_ + 1) * CH],
                    xcc[:, k_ * CH:(k_ + 1) * CH], ALU.mult))
            p.bar()
            p.dma(dtO[mi].rearrange("kt p t -> p kt t")[:, :, c0:c0 + CH], dtc[:].rearrange("p (kt t) -> p kt t", kt=16))
            p.dma(dtxO[mi].rearrange("kt p t -> p kt t")[:, :, c0:c0 + CH], dtxc[:].rearrange("p (kt t) -> p kt t", kt=16))
            p.bar()
    return p


def run_xdbl(I, Hst):
    p = build_xdbl()
    nc = p.finish()
    TH = T // 2
    WB = Hst["WB"]
    def dtb_l(pre):
        return I[f"L2_attn_{pre}_dt_b"].reshape(16, 128).T
    in_maps = []
    for c in range(NCORES):
        b = c // 2; th = c % 2
        ts = slice(th * TH, (th + 1) * TH)
        in_maps.append({
            "xc1S": np.ascontiguousarray(Hst["XC1"][:, b, ts]).reshape(16, 128, TH),
            "xc2S": np.ascontiguousarray(Hst["XC2f"][:, b, ts]).reshape(16, 128, TH),
            "Wxp1bf": WB["Wxp1"], "Wxp2bf": WB["Wxp2"],
            "Wdt1bf": WB["Wdt1"], "Wdt2bf": WB["Wdt2"],
            "dtb1": np.ascontiguousarray(dtb_l("m1")),
            "dtb2": np.ascontiguousarray(dtb_l("m2")),
        })
    res = run_bass_kernel_spmd(nc, in_maps, core_ids=list(range(NCORES)))
    p.stack.close()
    DTs = {}
    for nm, dt_ in (("dt1S", np.float32), ("dt2S", np.float32),
                    ("dtx1S", BF), ("dtx2S", BF),
                    ("Bm1S", BF), ("Bm2S", BF), ("Cm1S", np.float32), ("Cm2S", np.float32)):
        if nm.startswith("dt"):
            A = np.empty((2048, B, T), dt_)
        else:
            A = np.empty((64, B, T), dt_)
        for c in range(NCORES):
            b = c // 2; th = c % 2
            ts = slice(th * TH, (th + 1) * TH)
            r = res.results[c][nm]
            if nm.startswith("dt"):
                A[:, b, ts] = r.reshape(2048, TH)
            else:
                A[:, b, ts] = r
        DTs[nm] = A
    Hst.update(DTs)



def build_scan2():
    p = P("scan2")
    TH = T // 2
    dtD = [p.din("dt1S", (8, 128, T)), p.din("dt2S", (8, 128, T))]
    dtxD = [p.din("dtx1S", (8, 128, T), BF16), p.din("dtx2S", (8, 128, T), BF16)]
    bmD = [p.din("Bm1S", (64, T), BF16), p.din("Bm2S", (64, T), BF16)]
    cmD = [p.din("Cm1S", (64, T)), p.din("Cm2S", (64, T))]
    xcD = [p.din("xc1S", (8, 128, T)), p.din("xc2S", (8, 128, T))]
    zD = [p.din("z1S", (8, 128, T)), p.din("z2S", (8, 128, T))]
    alD = [p.din("Al1", (128, 512)), p.din("Al2", (128, 512))]
    dD = [p.din("D1", (128, 8)), p.din("D2", (128, 8))]
    yO = [p.dout("y1S", (8, 128, T)), p.dout("y2S", (8, 128, T))]

    AH = p.sb([128, 1024], F32, "AH")
    dvec = p.sb([128, 16], F32, "dvec")
    NJ = 4
    dtb_ = p.sb([128, NJ * TH], BF16, "dtb_")
    dtxb = p.sb([128, NJ * TH], BF16, "dtxb")
    yac = p.sb([128, NJ * TH], F32, "yac")
    Bb2 = [p.sb([128, TH], BF16, f"Bb{i}") for i in range(2)]
    Cb2 = [p.sb([128, TH], F32, f"Cb{i}") for i in range(2)]
    al2 = [p.sb([128, TH], F32, f"al{i}") for i in range(2)]
    uu2 = [p.sb([128, TH], F32, f"uu{i}") for i in range(2)]
    hh2 = [p.sb([128, TH], F32, f"hh{i}") for i in range(2)]
    tm2 = [p.sb([128, TH], F32, f"tm{i}") for i in range(2)]
    hl = p.sb([128, 512], F32, "hl")
    dts = p.sb([128, NJ * TH], F32, "dts")

    for mi in range(2):
        p.dma(AH[:, mi * 512:(mi + 1) * 512], alD[mi][:])
        p.dma(dvec[:, mi * 8:(mi + 1) * 8], dD[mi][:])
    p.bar()
    for mi in range(2):
        p.op("scalar", lambda E, mi_=mi: E.activation(
            AH[:, mi_ * 512:(mi_ + 1) * 512], AH[:, mi_ * 512:(mi_ + 1) * 512], AF.Exp))
    p.bar()
    p.op("vector", lambda E: E.tensor_scalar(AH[:], AH[:], -1.0, None, ALU.mult))
    p.bar()

    for mi in range(2):
        for jg in range(2):
            j0 = jg * NJ
            for half in range(2):
                h0 = half * TH
                for j in range(NJ):
                    p.dma(dts[:, j * TH:(j + 1) * TH], dtD[mi][j0 + j][:, h0:h0 + TH])
                    p.dma(dtxb[:, j * TH:(j + 1) * TH], dtxD[mi][j0 + j][:, h0:h0 + TH])
                p.bar()
                for j in range(NJ):
                    p.op("scalar", lambda E, j_=j: E.activation(
                        dtb_[:, j_ * TH:(j_ + 1) * TH], dts[:, j_ * TH:(j_ + 1) * TH], AF.Copy))
                p.bar()
                p.dma(Bb2[0][:], bass.AP(bmD[mi].tensor, 0 * T + h0, [[0, 128], [1, TH]]))
                p.dma(Cb2[0][:], bass.AP(cmD[mi].tensor, 0 * T + h0, [[0, 128], [1, TH]]))
                p.bar()
                NIT = 64 * NJ
                seq = [(j, n) for n in range(64) for j in range(NJ)]
                for it in range(NIT + 3):
                    if it < NIT:
                        j, n = seq[it]
                        if j == min(2, NJ - 1) and n + 1 < 64:
                            p.dma(Bb2[(n + 1) % 2][:], bass.AP(
                                bmD[mi].tensor, (n + 1) * T + h0, [[0, 128], [1, TH]]))
                            p.dma(Cb2[(n + 1) % 2][:], bass.AP(
                                cmD[mi].tensor, (n + 1) * T + h0, [[0, 128], [1, TH]]))
                        p.op("scalar", lambda E, j_=j, n_=n, mi_=mi, i_=it, j0_=j0: E.activation(
                            al2[i_ % 2][:], dtb_[:, j_ * TH:(j_ + 1) * TH], AF.Exp,
                            scale=AH[:, mi_ * 512 + (j_ + j0_) * 64 + n_:
                                     mi_ * 512 + (j_ + j0_) * 64 + n_ + 1]))
                        p.op("gpsimd", lambda E, j_=j, n_=n, i_=it: E.tensor_tensor(
                            uu2[i_ % 2][:], dtxb[:, j_ * TH:(j_ + 1) * TH],
                            Bb2[n_ % 2][:], ALU.mult))
                    itb = it - 1
                    if 0 <= itb < NIT:
                        j, n = seq[itb]
                        hidx = (j + j0) * 64 + n
                        init = 0.0 if half == 0 else hl[:, hidx:hidx + 1]
                        p.op("vector", lambda E, i_=itb, ii=init: E.tensor_tensor_scan(
                            hh2[i_ % 2][:], al2[i_ % 2][:], uu2[i_ % 2][:], ii,
                            ALU.mult, ALU.add))
                    itc = it - 2
                    if 0 <= itc < NIT:
                        j, n = seq[itc]
                        hidx = (j + j0) * 64 + n
                        if half == 0:
                            p.op("vector", lambda E, i_=itc, h_=hidx: E.tensor_copy(
                                hl[:, h_:h_ + 1], hh2[i_ % 2][:, TH - 1:TH]))
                        p.op("vector", lambda E, i_=itc, n_=n: E.tensor_tensor(
                            tm2[i_ % 2][:], hh2[i_ % 2][:], Cb2[n_ % 2][:], ALU.mult))
                    itd = it - 3
                    if 0 <= itd < NIT:
                        j, n = seq[itd]
                        if n == 0:
                            p.op("vector", lambda E, i_=itd, j_=j: E.tensor_copy(
                                yac[:, j_ * TH:(j_ + 1) * TH], tm2[i_ % 2][:]))
                        else:
                            p.op("vector", lambda E, i_=itd, j_=j: E.tensor_tensor(
                                yac[:, j_ * TH:(j_ + 1) * TH], yac[:, j_ * TH:(j_ + 1) * TH],
                                tm2[i_ % 2][:], ALU.add))
                    p.bar()
                for j in range(NJ):
                    p.dma(al2[0][:], xcD[mi][j0 + j][:, h0:h0 + TH])
                    p.dma(al2[1][:], zD[mi][j0 + j][:, h0:h0 + TH])
                    p.bar()
                    p.op("vector", lambda E, j_=j, mi_=mi, j0_=j0: E.scalar_tensor_tensor(
                        al2[0][:], al2[0][:], dvec[:, mi_ * 8 + j0_ + j_: mi_ * 8 + j0_ + j_ + 1],
                        yac[:, j_ * TH:(j_ + 1) * TH], ALU.mult, ALU.add))
                    p.bar()
                    p.op("scalar", lambda E: E.activation(al2[1][:], al2[1][:], AF.Silu))
                    p.bar()
                    p.op("vector", lambda E: E.tensor_tensor(al2[0][:], al2[0][:], al2[1][:], ALU.mult))
                    p.bar()
                    p.dma(yO[mi][j0 + j][:, h0:h0 + TH], al2[0][:])
                    p.bar()
    return p


def run_scan2(I, Hst):
    p = build_scan2()
    nc = p.finish()
    Z1 = Hst["z1"].reshape(2048, B, T)
    Z2f = np.ascontiguousarray(np.flip(Hst["z2"].reshape(2048, B, T), axis=2))
    def al_l(pre, eh):
        a = I[f"L2_attn_{pre}_A_log"][eh * 1024:(eh + 1) * 1024]
        return np.ascontiguousarray(a.reshape(8, 128, 64).transpose(1, 0, 2).reshape(128, 512))
    def d_l(pre, eh):
        d = I[f"L2_attn_{pre}_D"][eh * 1024:(eh + 1) * 1024]
        return np.ascontiguousarray(d.reshape(8, 128).T)
    in_maps = []
    for c in range(NCORES):
        b = c // 2; eh = c % 2
        sl = slice(eh * 1024, (eh + 1) * 1024)
        in_maps.append({
            "dt1S": np.ascontiguousarray(Hst["dt1S"][sl, b]).reshape(8, 128, T),
            "dt2S": np.ascontiguousarray(Hst["dt2S"][sl, b]).reshape(8, 128, T),
            "dtx1S": np.ascontiguousarray(Hst["dtx1S"][sl, b]).reshape(8, 128, T),
            "dtx2S": np.ascontiguousarray(Hst["dtx2S"][sl, b]).reshape(8, 128, T),
            "Bm1S": np.ascontiguousarray(Hst["Bm1S"][:, b]),
            "Bm2S": np.ascontiguousarray(Hst["Bm2S"][:, b]),
            "Cm1S": np.ascontiguousarray(Hst["Cm1S"][:, b]),
            "Cm2S": np.ascontiguousarray(Hst["Cm2S"][:, b]),
            "xc1S": np.ascontiguousarray(Hst["XC1"][sl, b]).reshape(8, 128, T),
            "xc2S": np.ascontiguousarray(Hst["XC2f"][sl, b]).reshape(8, 128, T),
            "z1S": np.ascontiguousarray(Z1[sl, b]).reshape(8, 128, T),
            "z2S": np.ascontiguousarray(Z2f[sl, b]).reshape(8, 128, T),
            "Al1": al_l("m1", eh), "Al2": al_l("m2", eh),
            "D1": d_l("m1", eh), "D2": d_l("m2", eh),
        })
    res = run_bass_kernel_spmd(nc, in_maps, core_ids=list(range(NCORES)))
    p.stack.close()
    Y1 = np.empty((2048, B, T), np.float32)
    Y2f = np.empty((2048, B, T), np.float32)
    for c in range(NCORES):
        b = c // 2; eh = c % 2
        sl = slice(eh * 1024, (eh + 1) * 1024)
        Y1[sl, b] = res.results[c]["y1S"].reshape(1024, T)
        Y2f[sl, b] = res.results[c]["y2S"].reshape(1024, T)
    Hst["Y1"] = Y1
    Hst["Y2"] = np.ascontiguousarray(np.flip(Y2f, axis=2))



def build_out2():
    p = P("out2")
    y1D = p.din("y1S", (16, 128, TPC)); y2D = p.din("y2S", (16, 128, TPC))
    wo1D = p.din("Wout1bf", (2048, D), BF16); wo2D = p.din("Wout2bf", (2048, D), BF16)
    ln2wD = p.din("ln2wH", (128, DT)); ln2bD = p.din("ln2bH", (128, DT))
    xlnD = p.din("xlnT", (DT, 128, TPC))
    x2D = p.din("x2T", (DT, 128, TPC))
    xn2D = p.din("xn2bf", (DT, 128, TPC), BF16)
    tw1D = p.din("tw1bf", (D, 4096), BF16); tw2D = p.din("tw2bf", (D, 4096), BF16)
    th1D = p.din("th1bf", (4096, 4096), BF16); th2D = p.din("th2bf", (4096, 4096), BF16)
    tw3D = p.din("tw3bf", (4096, D), BF16)
    s1D = p.din("ts1H", (128, 32)); s2D = p.din("ts2H", (128, 32))
    h1D = p.din("th1H", (128, 32)); h2D = p.din("th2H", (128, 32))
    s3D = p.din("ts3H", (128, DT))
    x3O = p.dout("x3T", (DT, 128, TPC))
    xn3O = p.dout("xn3T", (DT, 128, TPC))
    xn3bO = p.dout("xn3bf", (DT, 128, TPC), BF16)
    y1V = y1D.rearrange("kt p t -> p kt t"); y2V = y2D.rearrange("kt p t -> p kt t")
    xlnV = xlnD.rearrange("kt p t -> p kt t"); x2V = x2D.rearrange("kt p t -> p kt t")
    xn2V = xn2D.rearrange("kt p t -> p kt t")
    x3V = x3O.rearrange("kt p t -> p kt t"); xn3V = xn3O.rearrange("kt p t -> p kt t")
    xn3bV = xn3bO.rearrange("kt p t -> p kt t")

    onesb = p.sb([128, 1], F32, "onesb"); onesr = p.sb([1, 128], F32, "onesr")
    p.op("gpsimd", lambda E: E.memset(onesb[:], 1.0))
    p.op("gpsimd", lambda E: E.memset(onesr[:], 1.0))
    p.bar()
    s1p = p.ps([1, CH], F32, "s1"); s2p = p.ps([1, CH], F32, "s2")
    rb = p.ps([128, CH], F32, "rb"); mb = p.ps([128, CH], F32, "mb")
    pmm = [p.ps([128, CH], F32, f"pmm{i}") for i in range(4)]
    lnst = p.sb([1, 4 * CH], F32, "lnst")
    F1 = p.sb([128, 16 * CH], F32, "F1")
    ybf = p.sb([128, 16 * CH], BF16, "ybf")
    macc = p.sb([128, DT * CH], F32, "macc")
    s2c = p.sb([128, DT * CH], F32, "s2c")
    x2c = p.sb([128, DT * CH], F32, "x2c")
    xn2c = p.sb([128, DT * CH], BF16, "xn2c")
    HA = p.sb([128, 32 * CH], BF16, "HA")
    HB = p.sb([128, 32 * CH], BF16, "HB")
    WS = p.sb([128, 8192], BF16, "WS")
    scl = p.sb([128, 32 * 4 + DT], F32, "scl")
    ln2w = p.sb([128, DT], F32, "ln2w"); ln2b = p.sb([128, DT], F32, "ln2b")
    p.dma(scl[:, 0:32], s1D[:]); p.dma(scl[:, 32:64], s2D[:])
    p.dma(scl[:, 64:96], h1D[:]); p.dma(scl[:, 96:128], h2D[:])
    p.dma(scl[:, 128:128 + DT], s3D[:])
    p.dma(ln2w[:], ln2wD[:]); p.dma(ln2b[:], ln2bD[:])
    p.bar()

    for ch in range(NCH):
        c0 = ch * CH
        for (yV, woD, first) in ((y1V, wo1D, True), (y2V, wo2D, False)):
            p.dma(F1[:].rearrange("p (kt t) -> p kt t", kt=16), yV[:, :, c0:c0 + CH])
            p.bar()
            for k in range(16):
                p.op("scalar", lambda E, k_=k: E.activation(
                    ybf[:, k_ * CH:(k_ + 1) * CH], F1[:, k_ * CH:(k_ + 1) * CH], AF.Copy))
            p.bar()
            if first:
                def evm(m, ps_):
                    p.op("scalar", lambda E, m_=m, q=ps_: E.activation(
                        macc[:, m_ * CH:(m_ + 1) * CH], q, AF.Copy))
            else:
                def evm(m, ps_):
                    p.op("vector", lambda E, m_=m, q=ps_: E.tensor_tensor(
                        macc[:, m_ * CH:(m_ + 1) * CH], macc[:, m_ * CH:(m_ + 1) * CH],
                        q, ALU.add))
            stream_mm(p, WS, woD, 16, DT, 4,
                      lambda k: ybf[:, k * CH:(k + 1) * CH], evm, pmm, CH)
        srcs = [macc[:, j * CH:(j + 1) * CH] for j in range(DT)]
        sqd = [F1[:, j * CH:(j + 1) * CH] for j in range(DT)]
        dsts = [s2c[:, j * CH:(j + 1) * CH] for j in range(DT)]
        ln_stats(p, srcs, CH, onesb, onesr, s1p[:], s2p[:], rb[:], mb[:], sqd, lnst)
        ln_apply(p, srcs, rb[:], mb[:], dsts)
        for j in range(DT):
            p.op("vector", lambda E, j_=j: E.tensor_scalar(
                s2c[:, j_ * CH:(j_ + 1) * CH], s2c[:, j_ * CH:(j_ + 1) * CH],
                ln2w[:, j_:j_ + 1], ln2b[:, j_:j_ + 1], ALU.mult, ALU.add))
        p.bar()
        p.dma(x2c[:].rearrange("p (kt t) -> p kt t", kt=DT), xlnV[:, :, c0:c0 + CH])
        p.bar()
        for j in range(DT):
            p.op("vector", lambda E, j_=j: E.tensor_tensor(
                s2c[:, j_ * CH:(j_ + 1) * CH], s2c[:, j_ * CH:(j_ + 1) * CH],
                x2c[:, j_ * CH:(j_ + 1) * CH], ALU.add))
        p.bar()
        p.dma(xn2c[:].rearrange("p (kt t) -> p kt t", kt=DT), xn2V[:, :, c0:c0 + CH])
        p.bar()
        for li, (twD, sof, act) in enumerate(((tw1D, 0, AF.Silu), (tw2D, 32, AF.Copy))):
            dst = HA if li == 0 else HB

            def evt(mm_, ps_, sof_=sof, act_=act, dst_=dst):
                p.op("scalar", lambda E, q=ps_, mm2=mm_: E.activation(
                    dst_[:, mm2 * CH:(mm2 + 1) * CH], q, act_,
                    scale=scl[:, sof_ + mm2:sof_ + mm2 + 1]))
            stream_mm(p, WS, twD, DT, 32, 4,
                      lambda k: xn2c[:, k * CH:(k + 1) * CH], evt, pmm, CH)
        for m in range(32):
            p.op("vector", lambda E, m_=m: E.tensor_tensor(
                HA[:, m_ * CH:(m_ + 1) * CH], HA[:, m_ * CH:(m_ + 1) * CH],
                HB[:, m_ * CH:(m_ + 1) * CH], ALU.mult))
        p.bar()
        for li, (thD, sof) in enumerate(((th1D, 64), (th2D, 96))):
            srcbuf = HA if li == 0 else HB
            dstbuf = HB if li == 0 else HA

            def evg(mm_, ps_, sof_=sof, dst_=dstbuf):
                p.op("scalar", lambda E, q=ps_, mm2=mm_: E.activation(
                    dst_[:, mm2 * CH:(mm2 + 1) * CH], q, AF.Gelu,
                    scale=scl[:, sof_ + mm2:sof_ + mm2 + 1]))
            stream_mm(p, WS, thD, 32, 32, 2,
                      lambda k, s_=srcbuf: s_[:, k * CH:(k + 1) * CH], evg, pmm, CH)

        def evw3(mm_, ps_):
            p.op("scalar", lambda E, q=ps_, m_=mm_: E.activation(
                macc[:, m_ * CH:(m_ + 1) * CH], q, AF.Silu,
                scale=scl[:, 128 + m_:128 + m_ + 1]))
        stream_mm(p, WS, tw3D, 32, DT, 2,
                  lambda k: HA[:, k * CH:(k + 1) * CH], evw3, pmm, CH)
        p.dma(x2c[:].rearrange("p (kt t) -> p kt t", kt=DT), x2V[:, :, c0:c0 + CH])
        p.bar()
        for j in range(DT):
            p.op("vector", lambda E, j_=j: E.tensor_tensor(
                macc[:, j_ * CH:(j_ + 1) * CH], macc[:, j_ * CH:(j_ + 1) * CH],
                s2c[:, j_ * CH:(j_ + 1) * CH], ALU.mult))
        p.bar()
        for j in range(DT):
            p.op("vector", lambda E, j_=j: E.tensor_tensor(
                x2c[:, j_ * CH:(j_ + 1) * CH], x2c[:, j_ * CH:(j_ + 1) * CH],
                macc[:, j_ * CH:(j_ + 1) * CH], ALU.add))
        p.bar()
        p.dma(x3V[:, :, c0:c0 + CH], x2c[:].rearrange("p (kt t) -> p kt t", kt=DT))
        p.bar()
        srcs = [x2c[:, j * CH:(j + 1) * CH] for j in range(DT)]
        sqd = [F1[:, j * CH:(j + 1) * CH] for j in range(DT)]
        dsts = [s2c[:, j * CH:(j + 1) * CH] for j in range(DT)]
        ln_stats(p, srcs, CH, onesb, onesr, s1p[:], s2p[:], rb[:], mb[:], sqd, lnst)
        ln_apply(p, srcs, rb[:], mb[:], dsts)
        for j in range(DT):
            p.op("scalar", lambda E, j_=j: E.activation(
                xn2c[:, j_ * CH:(j_ + 1) * CH], s2c[:, j_ * CH:(j_ + 1) * CH], AF.Copy))
        p.bar()
        p.dma(xn3V[:, :, c0:c0 + CH], s2c[:].rearrange("p (kt t) -> p kt t", kt=DT))
        p.dma(xn3bV[:, :, c0:c0 + CH], xn2c[:].rearrange("p (kt t) -> p kt t", kt=DT))
        p.bar()
    return p


def run_out2(I, Hst):
    p = build_out2()
    nc = p.finish()
    WB = Hst["WB"]
    Y1 = Hst["Y1"].reshape(2048, NTOK); Y2 = Hst["Y2"].reshape(2048, NTOK)
    def scaleH(name, nt):
        return np.ascontiguousarray(I[name].reshape(nt, 128).T)
    in_maps = []
    for c in range(NCORES):
        sl = slice(c * TPC, (c + 1) * TPC)
        in_maps.append({
            "y1S": np.ascontiguousarray(Y1[:, sl]).reshape(16, 128, TPC),
            "y2S": np.ascontiguousarray(Y2[:, sl]).reshape(16, 128, TPC),
            "Wout1bf": WB["Wout1"], "Wout2bf": WB["Wout2"],
            "ln2wH": scaleH("L2_attn_ln2_w", DT), "ln2bH": scaleH("L2_attn_ln2_b", DT),
            "xlnT": np.ascontiguousarray(Hst["xlnT"][:, sl]).reshape(DT, 128, TPC),
            "x2T": np.ascontiguousarray(Hst["x2T"][:, sl]).reshape(DT, 128, TPC),
            "xn2bf": np.ascontiguousarray(Hst["xn2bf"][:, sl]).reshape(DT, 128, TPC),
            "tw1bf": WB["tw1"], "tw2bf": WB["tw2"],
            "th1bf": WB["th1"], "th2bf": WB["th2"], "tw3bf": WB["tw3"],
            "ts1H": scaleH("L2_ffn_w1_s", 32), "ts2H": scaleH("L2_ffn_w2_s", 32),
            "th1H": scaleH("L2_ffn_h1_s", 32), "th2H": scaleH("L2_ffn_h2_s", 32),
            "ts3H": scaleH("L2_ffn_w3_s", DT),
        })
    res = run_bass_kernel_spmd(nc, in_maps, core_ids=list(range(NCORES)))
    p.stack.close()
    Hst["x3T"] = np.concatenate([r["x3T"].reshape(D, TPC) for r in res.results], axis=1)
    Hst["xn3T"] = np.concatenate([r["xn3T"].reshape(D, TPC) for r in res.results], axis=1)
    Hst["xn3bf"] = np.concatenate([r["xn3bf"].reshape(D, TPC) for r in res.results], axis=1)



def build_l3a():
    p = P("l3a")
    xnD = p.din("xn3T", (DT, 128, TPC))
    uwD = p.din("uwbf", (D, 2048), BF16); vwD = p.din("vwbf", (D, 2048), BF16)
    ubD = p.din("ubH", (128, 16)); vbD = p.din("vbH", (128, 16))
    ceD = p.din("cent", (8, D)); wrD = p.din("wrt", (D, 8))
    uO = p.dout("u3S", (16, 128, TPC)); vO = p.dout("v3S", (16, 128, TPC))
    gO = p.dout("g3g", (TPC, 8))
    xnV = xnD.rearrange("kt p t -> p kt t")
    uV = uO.rearrange("kt p t -> p kt t"); vV = vO.rearrange("kt p t -> p kt t")

    onesb = p.sb([128, 1], F32, "onesb"); onesr = p.sb([1, 128], F32, "onesr")
    iden = p.sb([128, 128], F32, "iden")
    p.op("gpsimd", lambda E: E.memset(onesb[:], 1.0))
    p.op("gpsimd", lambda E: E.memset(onesr[:], 1.0))
    p.op("gpsimd", lambda E: E.memset(iden[:], 0.0))
    p.bar()
    p.op("gpsimd", lambda E: E.affine_select(
        out=iden[:], in_=iden[:], compare_op=ALU.not_equal, fill=1.0,
        base=0, pattern=[[-1, 128]], channel_multiplier=1))
    p.bar()
    s1p = p.ps([1, CH], F32, "s1"); s2p = p.ps([1, CH], F32, "s2")
    rb = p.ps([128, CH], F32, "rb"); mb = p.ps([128, CH], F32, "mb")
    pmm = [p.ps([128, CH], F32, f"pmm{i}") for i in range(4)]
    xnc = p.sb([128, DT * CH], F32, "xnc")
    sqc = p.sb([128, DT * CH], F32, "sqc")
    xhbf = p.sb([128, DT * CH], BF16, "xhbf")
    xnormc = p.sb([128, DT * CH], F32, "xnormc")
    uvb = p.sb([128, 16 * CH], F32, "uvb")
    WS = p.sb([128, DT * 2048], BF16, "WS")
    cent = p.sb([8, D], F32, "cent_s")
    cnT = p.sb([128, DT * 8], F32, "cnT")
    wrt = p.sb([128, DT * 8], F32, "wrt_s")
    bias = p.sb([128, 32], F32, "bias")
    gbuf = p.sb([128, 8 * 3], F32, "gbuf")
    gred = p.sb([128, 4], F32, "gred")
    stt_ = p.sb([1, 4 * CH], F32, "stt")
    crow = p.sb([8, 4], F32, "crow")

    p.dma(cent[:], ceD[:])
    for k_ in range(DT):
        p.dma(wrt[:, k_ * 8:(k_ + 1) * 8], wrD[k_ * 128:(k_ + 1) * 128, :])
    p.dma(bias[:, 0:16], ubD[:]); p.dma(bias[:, 16:32], vbD[:])
    p.bar()
    sqe = p.sb([8, D], F32, "sqe")
    p.op("scalar", lambda E: E.activation(sqe[:], cent[:], AF.Square))
    p.bar()
    p.op("vector", lambda E: E.tensor_reduce(crow[:, 0:1], sqe[:], AX.X, ALU.add))
    p.bar()
    p.op("scalar", lambda E: E.activation(crow[:, 1:2], crow[:, 0:1], AF.Sqrt))
    p.bar()
    p.op("vector", lambda E: E.tensor_scalar(crow[:, 1:2], crow[:, 1:2], 1e-12, None, ALU.max))
    p.bar()
    p.op("vector", lambda E: E.reciprocal(crow[:, 2:3], crow[:, 1:2]))
    p.bar()
    p.op("vector", lambda E: E.tensor_scalar(sqe[:], cent[:], crow[:, 2:3], None, ALU.mult))
    p.bar()
    for k_ in range(DT):
        p.op("tensor", lambda E, k2=k_: E.transpose(
            pmm[0][:, 0:8], sqe[:, k2 * 128:(k2 + 1) * 128], iden[0:8, 0:8]))
        p.bar()
        p.op("scalar", lambda E, k2=k_: E.activation(
            cnT[:, k2 * 8:(k2 + 1) * 8], pmm[0][:, 0:8], AF.Copy))
        p.bar()

    for ch in range(NCH):
        c0 = ch * CH
        p.dma(xnc[:].rearrange("p (kt t) -> p kt t", kt=DT), xnV[:, :, c0:c0 + CH])
        p.bar()
        for j in range(DT):
            p.op("scalar", lambda E, j_=j: E.activation(
                sqc[:, j_ * CH:(j_ + 1) * CH], xnc[:, j_ * CH:(j_ + 1) * CH], AF.Square))
        p.bar()
        for j in range(DT):
            p.op("tensor", lambda E, j_=j: E.matmul(
                s1p[:], onesb[:, 0:1], sqc[:, j_ * CH:(j_ + 1) * CH],
                start=(j_ == 0), stop=(j_ == DT - 1)))
        p.bar()
        nr = stt_[:, 0:CH]; ar = stt_[:, CH:2 * CH]; br = stt_[:, 2 * CH:3 * CH]
        p.op("scalar", lambda E: E.activation(nr, s1p[:], AF.Sqrt))
        p.bar()
        p.op("vector", lambda E: E.tensor_scalar(ar, nr, 1.0 / 32.0, 1e-8, ALU.mult, ALU.add))
        p.op("vector", lambda E: E.tensor_scalar(br, nr, 1e-12, None, ALU.max))
        p.bar()
        p.op("vector", lambda E: E.reciprocal(ar, ar))
        p.op("vector", lambda E: E.reciprocal(br, br))
        p.bar()
        p.op("tensor", lambda E: E.matmul(rb[:], onesr[:], ar, start=True, stop=True))
        p.op("tensor", lambda E: E.matmul(mb[:], onesr[:], br, start=True, stop=True))
        p.bar()
        for j in range(DT):
            p.op("vector", lambda E, j_=j: E.tensor_tensor(
                xhbf[:, j_ * CH:(j_ + 1) * CH], xnc[:, j_ * CH:(j_ + 1) * CH], rb[:], ALU.mult))
            p.op("vector", lambda E, j_=j: E.tensor_tensor(
                xnormc[:, j_ * CH:(j_ + 1) * CH], xnc[:, j_ * CH:(j_ + 1) * CH], mb[:], ALU.mult))
        p.bar()
        for st_ in range(CH // 128):
            t0 = st_ * 128
            for kt in range(DT):
                p.op("tensor", lambda E, kt_=kt, t0_=t0: E.matmul(
                    mb[:, 0:8], xnormc[:, kt_ * CH + t0_: kt_ * CH + t0_ + 128],
                    cnT[:, kt_ * 8:(kt_ + 1) * 8],
                    start=(kt_ == 0), stop=False))
            for kt in range(DT):
                p.op("tensor", lambda E, kt_=kt, t0_=t0: E.matmul(
                    mb[:, 0:8], xnc[:, kt_ * CH + t0_: kt_ * CH + t0_ + 128],
                    wrt[:, kt_ * 8:(kt_ + 1) * 8],
                    start=False, stop=(kt_ == DT - 1)))
            p.bar()
            G = gbuf[:, 0:8]; Gm = gbuf[:, 8:16]; W_ = gbuf[:, 16:24]
            mx = gred[:, 0:1]; sm = gred[:, 1:2]; m2c = gred[:, 2:3]
            p.op("scalar", lambda E: E.activation(G, mb[:, 0:8], AF.Copy))
            p.bar()
            p.op("vector", lambda E: E.tensor_reduce(mx, G, AX.X, ALU.max))
            p.bar()
            p.op("vector", lambda E: E.tensor_scalar(Gm, G, mx, None, ALU.is_equal))
            p.bar()
            p.op("vector", lambda E: E.scalar_tensor_tensor(Gm, Gm, -1e30, G, ALU.mult, ALU.add))
            p.bar()
            p.op("vector", lambda E: E.tensor_reduce(m2c, Gm, AX.X, ALU.max))
            p.bar()
            p.op("vector", lambda E: E.tensor_scalar(Gm, G, m2c, None, ALU.is_ge))
            p.bar()
            p.op("vector", lambda E: E.tensor_scalar(mx, mx, -1.0, None, ALU.mult))
            p.bar()
            p.op("scalar", lambda E: E.activation(G, G, AF.Exp, bias=mx))
            p.bar()
            p.op("vector", lambda E: E.tensor_tensor(G, G, Gm, ALU.mult))
            p.bar()
            p.op("vector", lambda E: E.tensor_reduce(sm, G, AX.X, ALU.add))
            p.bar()
            p.op("vector", lambda E: E.reciprocal(sm, sm))
            p.bar()
            p.op("vector", lambda E: E.tensor_scalar(G, G, sm, None, ALU.mult))
            p.bar()
            p.dma(gO[c0 + t0:c0 + t0 + 128, :], G)
            p.bar()
        for pi, (wD_, outV, bof) in enumerate(((uwD, uV, 0), (vwD, vV, 16))):
            p.dma(WS[:].rearrange("p (kt m) -> p kt m", m=2048),
                  wD_.rearrange("(kt p) m -> p kt m", p=128))
            p.bar()

            def evuv(m, ps_, bof_=bof):
                p.op("scalar", lambda E, m_=m, q=ps_: E.activation(
                    uvb[:, m_ * CH:(m_ + 1) * CH], q, AF.Silu,
                    bias=bias[:, bof_ + m_:bof_ + m_ + 1]))
            emit_mm(p, WS, DT, 16, lambda k: xhbf[:, k * CH:(k + 1) * CH], CH, pmm,
                    evuv, wcols=2048)
            p.dma(outV[:, :, c0:c0 + CH], uvb[:].rearrange("p (kt t) -> p kt t", kt=16))
            p.bar()
    return p


def run_l3a(I, Hst):
    p = build_l3a()
    nc = p.finish()
    WB = Hst["WB"]
    ubH = np.ascontiguousarray(I["L3_attn_u_b"].reshape(16, 128).T)
    vbH = np.ascontiguousarray(I["L3_attn_v_b"].reshape(16, 128).T)
    in_maps = []
    for c in range(NCORES):
        sl = slice(c * TPC, (c + 1) * TPC)
        in_maps.append({
            "xn3T": np.ascontiguousarray(Hst["xn3T"][:, sl]).reshape(DT, 128, TPC),
            "uwbf": WB["uw"], "vwbf": WB["vw"],
            "ubH": ubH, "vbH": vbH,
            "cent": I["L3_ffn_centroids"],
            "wrt": np.ascontiguousarray(I["L3_ffn_w_route"].T),
        })
    res = run_bass_kernel_spmd(nc, in_maps, core_ids=list(range(NCORES)))
    p.stack.close()
    Hst["u3S"] = np.concatenate([r["u3S"].reshape(2048, TPC) for r in res.results], axis=1)
    Hst["v3S"] = np.concatenate([r["v3S"].reshape(2048, TPC) for r in res.results], axis=1)
    Hst["g3g"] = np.concatenate([r["g3g"] for r in res.results], axis=0)



def build_l3b():
    p = P("l3b")
    xnD = p.din("xn3bf", (DT, 128, TPC), BF16)
    w1D = p.din("fw1bf", (8, D, 512), BF16)
    w2D = p.din("fw2bf", (8, 512, D), BF16)
    b1D = p.din("b1H", (128, 8 * 4)); b2D = p.din("b2H", (128, 8 * DT))
    gtD = p.din("gtT", (8, TPC))
    g3O = p.dout("g3S", (DT, 128, TPC))
    xnV = xnD.rearrange("kt p t -> p kt t")
    g3V = g3O.rearrange("kt p t -> p kt t")
    onesr = p.sb([1, 128], F32, "onesr")
    p.op("gpsimd", lambda E: E.memset(onesr[:], 1.0))
    p.bar()
    pmm = [p.ps([128, CH], F32, f"pmm{i}") for i in range(4)]
    pw = p.ps([128, CH], F32, "pw")
    xnc = p.sb([128, DT * CH], BF16, "xnc")
    w1 = p.sb([128, DT * 512], BF16, "w1")
    w2 = p.sb([128, 4 * D], BF16, "w2")
    hbf = p.sb([128, 4 * CH], BF16, "hbf")
    b1 = p.sb([128, 8 * 4], F32, "b1s"); b2 = p.sb([128, 8 * DT], F32, "b2s")
    acc = p.sb([128, DT * TPC], F32, "acc")
    ytmp = p.sb([128, DT * CH], F32, "ytmp")
    wtmp = p.sb([128, DT * CH], F32, "wtmp")
    wrow = p.sb([1, CH], F32, "wrow")
    p.dma(b1[:], b1D[:]); p.dma(b2[:], b2D[:])
    p.bar()
    for ch in range(NCH):
        c0 = ch * CH
        p.dma(xnc[:].rearrange("p (kt t) -> p kt t", kt=DT), xnV[:, :, c0:c0 + CH])
        p.bar()
        for e in range(8):
            p.dma(w1[:].rearrange("p (kt m) -> p kt m", m=512),
                  w1D[e].rearrange("(kt p) m -> p kt m", p=128))
            p.dma(w2[:].rearrange("p (kt m) -> p kt m", m=D),
                  w2D[e].rearrange("(kt p) m -> p kt m", p=128))
            p.bar()

            def evh(m, ps_, e_=e):
                p.op("scalar", lambda E, m_=m, q=ps_: E.activation(
                    hbf[:, m_ * CH:(m_ + 1) * CH], q, AF.Gelu,
                    bias=b1[:, e_ * 4 + m_: e_ * 4 + m_ + 1]))
            emit_mm(p, w1, DT, 4, lambda k: xnc[:, k * CH:(k + 1) * CH], CH, pmm, evh,
                    wcols=512)

            def evy(m, ps_, e_=e):
                p.op("vector", lambda E, m_=m, q=ps_, e2=e_: E.tensor_scalar(
                    ytmp[:, m_ * CH:(m_ + 1) * CH], q, b2[:, e2 * DT + m_: e2 * DT + m_ + 1],
                    None, ALU.add))
            emit_mm(p, w2, 4, DT, lambda k: hbf[:, k * CH:(k + 1) * CH], CH, pmm, evy,
                    wcols=D)
            p.dma(wrow[:], gtD[e:e + 1, c0:c0 + CH])
            p.bar()
            p.op("tensor", lambda E: E.matmul(pw[:], onesr[:], wrow[:], start=True, stop=True))
            p.bar()
            if e == 0:
                for m in range(DT):
                    p.op("vector", lambda E, m_=m, c0_=c0: E.tensor_tensor(
                        acc[:, m_ * TPC + c0_: m_ * TPC + c0_ + CH],
                        ytmp[:, m_ * CH:(m_ + 1) * CH], pw[:], ALU.mult))
                p.bar()
            else:
                for m in range(DT):
                    p.op("vector", lambda E, m_=m: E.tensor_tensor(
                        wtmp[:, m_ * CH:(m_ + 1) * CH],
                        ytmp[:, m_ * CH:(m_ + 1) * CH], pw[:], ALU.mult))
                p.bar()
                for m in range(DT):
                    p.op("vector", lambda E, m_=m, c0_=c0: E.tensor_tensor(
                        acc[:, m_ * TPC + c0_: m_ * TPC + c0_ + CH],
                        acc[:, m_ * TPC + c0_: m_ * TPC + c0_ + CH],
                        wtmp[:, m_ * CH:(m_ + 1) * CH], ALU.add))
                p.bar()
    for m in range(DT):
        for ch in range(NCH):
            p.op("scalar", lambda E, m_=m, c_=ch: E.activation(
                acc[:, m_ * TPC + c_ * CH: m_ * TPC + (c_ + 1) * CH],
                acc[:, m_ * TPC + c_ * CH: m_ * TPC + (c_ + 1) * CH], AF.Silu))
    p.bar()
    for ch in range(NCH):
        c0 = ch * CH
        p.dma(g3V[:, :, c0:c0 + CH],
              acc[:].rearrange("p (kt t) -> p kt t", kt=DT)[:, :, c0:c0 + CH])
    p.bar()
    return p


def run_l3b(I, Hst):
    p = build_l3b()
    nc = p.finish()
    WB = Hst["WB"]
    b1H = np.ascontiguousarray(I["L3_ffn_b1"].reshape(8, 4, 128).transpose(2, 0, 1).reshape(128, 32))
    b2H = np.ascontiguousarray(I["L3_ffn_b2"].reshape(8, DT, 128).transpose(2, 0, 1).reshape(128, 8 * DT))
    gtT = np.ascontiguousarray(Hst["g3g"].T)
    in_maps = []
    for c in range(NCORES):
        sl = slice(c * TPC, (c + 1) * TPC)
        in_maps.append({
            "xn3bf": np.ascontiguousarray(Hst["xn3bf"][:, sl]).reshape(DT, 128, TPC),
            "fw1bf": WB["fw1"], "fw2bf": WB["fw2"],
            "b1H": b1H, "b2H": b2H,
            "gtT": np.ascontiguousarray(gtT[:, sl]),
        })
    res = run_bass_kernel_spmd(nc, in_maps, core_ids=list(range(NCORES)))
    p.stack.close()
    Hst["g3S"] = np.concatenate([r["g3S"].reshape(D, TPC) for r in res.results], axis=1)



def build_mix():
    p = P("mix")
    TS = T // NCORES
    uD = p.din("uS", (16, 128, B, TS)); vD = p.din("vS", (16, 128, B, TS))
    x3D = p.din("x3S", (DT, 128, B, TS)); xn3D = p.din("xn3S", (DT, 128, B, TS))
    g3D = p.din("g3Ss", (DT, 128, B, TS))
    mhD = p.din("MH", (128, 16 * 16))
    owD = p.din("owbf", (2048, D), BF16)
    obD = p.din("obH", (128, DT))
    x4O = p.dout("x4S", (DT, 128, B, TS))
    uV = uD.rearrange("kt p b t -> p kt b t"); vV = vD.rearrange("kt p b t -> p kt b t")
    x3V = x3D.rearrange("kt p b t -> p kt b t")
    xn3V = xn3D.rearrange("kt p b t -> p kt b t")
    g3V = g3D.rearrange("kt p b t -> p kt b t")
    x4V = x4O.rearrange("kt p b t -> p kt b t")

    pmm = [p.ps([128, TS], F32, f"pmm{i}") for i in range(4)]
    MH = p.sb([128, 256], F32, "MHs")
    ow = p.sb([128, 16 * D], BF16, "ows")
    obH = p.sb([128, DT], F32, "obHs")
    vt = p.sb([128, 4 * TS], F32, "vt")
    ut = p.sb([128, 4 * TS], F32, "ut")
    yb = p.sb([128, 4 * TS], F32, "yb")
    obuf = p.sb([128, 16 * 4 * TS], BF16, "obuf")
    xc1 = p.sb([128, 4 * TS], F32, "xc1")
    xc2 = p.sb([128, 4 * TS], F32, "xc2")
    xc3 = p.sb([128, 4 * TS], F32, "xc3")
    p.dma(MH[:], mhD[:]); p.dma(obH[:], obD[:])
    p.dma(ow[:].rearrange("p (kt m) -> p kt m", m=D),
          owD.rearrange("(kt p) m -> p kt m", p=128))
    p.bar()
    for jj in range(16):
        p.dma(vt[:].rearrange("p (b t) -> p b t", b=B), vV[:, jj])
        p.dma(ut[:].rearrange("p (b t) -> p b t", b=B), uV[:, jj])
        p.bar()
        for i in range(B):
            p.op("vector", lambda E, jj_=jj, i_=i: E.tensor_scalar(
                yb[:, i_ * TS:(i_ + 1) * TS], vt[:, 0:TS],
                MH[:, jj_ * 16 + i_ * 4: jj_ * 16 + i_ * 4 + 1], None, ALU.mult))
        p.bar()
        for k in range(1, B):
            for i in range(B):
                p.op("vector", lambda E, jj_=jj, i_=i, k_=k: E.scalar_tensor_tensor(
                    yb[:, i_ * TS:(i_ + 1) * TS], vt[:, k_ * TS:(k_ + 1) * TS],
                    MH[:, jj_ * 16 + i_ * 4 + k_: jj_ * 16 + i_ * 4 + k_ + 1],
                    yb[:, i_ * TS:(i_ + 1) * TS], ALU.mult, ALU.add))
            p.bar()
        for i in range(B):
            p.op("vector", lambda E, jj_=jj, i_=i: E.tensor_tensor(
                obuf[:, jj_ * 4 * TS + i_ * TS: jj_ * 4 * TS + (i_ + 1) * TS],
                ut[:, i_ * TS:(i_ + 1) * TS], yb[:, i_ * TS:(i_ + 1) * TS], ALU.mult))
        p.bar()
    for m in range(DT):
        for nb in range(4):
            for k in range(16):
                p.op("tensor", lambda E, m_=m, nb_=nb, k_=k: E.matmul(
                    pmm[nb_][:], ow[:, k_ * D + m_ * 128: k_ * D + (m_ + 1) * 128],
                    obuf[:, k_ * 4 * TS + nb_ * TS: k_ * 4 * TS + (nb_ + 1) * TS],
                    start=(k_ == 0), stop=(k_ == 15)))
        p.bar()
        p.dma(xc1[:].rearrange("p (b t) -> p b t", b=B), xn3V[:, m])
        p.dma(xc2[:].rearrange("p (b t) -> p b t", b=B), x3V[:, m])
        p.dma(xc3[:].rearrange("p (b t) -> p b t", b=B), g3V[:, m])
        p.bar()
        for nb in range(4):
            p.op("vector", lambda E, m_=m, nb_=nb: E.tensor_scalar(
                yb[:, nb_ * TS:(nb_ + 1) * TS], pmm[nb_][:], obH[:, m_:m_ + 1],
                None, ALU.add))
        p.bar()
        p.op("vector", lambda E: E.tensor_tensor(yb[:], yb[:], xc1[:], ALU.add))
        p.bar()
        p.op("vector", lambda E: E.tensor_tensor(yb[:], yb[:], xc3[:], ALU.mult))
        p.bar()
        p.op("vector", lambda E: E.tensor_tensor(yb[:], yb[:], xc2[:], ALU.add))
        p.bar()
        p.dma(x4V[:, m], yb[:].rearrange("p (b t) -> p b t", b=B))
        p.bar()
    return p


def run_mix(I, Hst):
    p = build_mix()
    nc = p.finish()
    TS = T // NCORES
    WB = Hst["WB"]
    U = Hst["u3S"].reshape(2048, B, T); V = Hst["v3S"].reshape(2048, B, T)
    X3 = Hst["x3T"].reshape(D, B, T); XN3 = Hst["xn3T"].reshape(D, B, T)
    G3 = Hst["g3S"].reshape(D, B, T)
    pos, zero, neg = I["L3_attn_pos"], I["L3_attn_zero"], I["L3_attn_neg"]
    M = np.zeros((8, 4, 4), np.float32)
    for i in range(4):
        for k_ in range(4):
            dlt = i - k_
            if dlt == 0:
                M[:, i, k_] = zero[:, 0]
            elif dlt > 0:
                M[:, i, k_] = pos[:, dlt - 1]
            else:
                M[:, i, k_] = neg[:, -dlt - 1]
    MH = np.repeat(M, 256, axis=0).reshape(16, 128, 16)
    MH = np.ascontiguousarray(MH.transpose(1, 0, 2).reshape(128, 256))
    obH = np.ascontiguousarray(I["L3_attn_o_b"].reshape(DT, 128).T)
    in_maps = []
    for c in range(NCORES):
        ts = slice(c * TS, (c + 1) * TS)
        in_maps.append({
            "uS": np.ascontiguousarray(U[:, :, ts]).reshape(16, 128, B, TS),
            "vS": np.ascontiguousarray(V[:, :, ts]).reshape(16, 128, B, TS),
            "x3S": np.ascontiguousarray(X3[:, :, ts]).reshape(DT, 128, B, TS),
            "xn3S": np.ascontiguousarray(XN3[:, :, ts]).reshape(DT, 128, B, TS),
            "g3Ss": np.ascontiguousarray(G3[:, :, ts]).reshape(DT, 128, B, TS),
            "MH": MH, "owbf": WB["ow"], "obH": obH,
        })
    res = run_bass_kernel_spmd(nc, in_maps, core_ids=list(range(NCORES)))
    p.stack.close()
    X4 = np.empty((D, B, T), np.float32)
    for c in range(NCORES):
        ts = slice(c * TS, (c + 1) * TS)
        X4[:, :, ts] = res.results[c]["x4S"].reshape(D, B, TS)
    return np.ascontiguousarray(X4.transpose(1, 2, 0))


def kernel(**I):
    I = {k: np.asarray(v, np.float32) for k, v in I.items()}
    Hst = {}
    run_prep(I, Hst)
    run_kvr(I, Hst)
    run_scan0(I, Hst)
    run_moe0(I, Hst)
    run_attn(I, Hst)
    run_ffn1a(I, Hst)
    run_ffn1b(I, Hst)
    run_conv(I, Hst)
    run_xdbl(I, Hst)
    run_scan2(I, Hst)
    run_out2(I, Hst)
    run_l3a(I, Hst)
    run_l3b(I, Hst)
    return run_mix(I, Hst)
